# version 02
# baseline (speedup 1.0000x reference)
import numpy as np
import ml_dtypes
from contextlib import ExitStack
import concourse.bass as bass
import concourse.mybir as mybir
from concourse.bass_utils import run_bass_kernel_spmd

F32 = mybir.dt.float32
BF16 = mybir.dt.bfloat16
U8 = mybir.dt.uint8
I32 = mybir.dt.int32
AF = mybir.ActivationFunctionType
ALU = mybir.AluOpType

T = 2048
TC = 4
DT = 32
IT = 86
DIM = 512
ADT = 4
FT = 16
DEPTH = 4
CH = 128
NCH = 16
EPS = 1e-5
MAGIC = 1234567

_P = {}


def _build_pmap():
    idx = [0]

    def add(name):
        _P[name] = idx[0]
        idx[0] += 1

    def addvec(name, ntile):
        for i in range(ntile):
            add(f"{name}{i}")

    _P.clear()
    addvec("bin", 4)
    for l in range(4):
        addvec(f"ln1w_{l}_", 4)
        addvec(f"ln1b_{l}_", 4)
        addvec(f"ln2w_{l}_", 4)
        addvec(f"ln2b_{l}_", 4)
        addvec(f"phr_{l}_", 4)
        addvec(f"phi_{l}_", 4)
        addvec(f"lcr_{l}_", 4)
        addvec(f"lci_{l}_", 4)
        addvec(f"scl_{l}_", 4)
        addvec(f"b1_{l}_", 16)
        addvec(f"b2_{l}_", 4)
    addvec("lnlw", 4)
    addvec("lnlb", 4)
    addvec("bout", 32)
    add("c_neghalf")
    add("c_half")


_build_pmap()



def _stage_inputs(inp):
    out = {}
    x = np.ascontiguousarray(np.asarray(inp["L0_attn_x"], np.float32))

    def lowbyte(a):
        a = np.ascontiguousarray(np.asarray(a, np.int32))
        return np.ascontiguousarray(a.view(np.uint8)[:, ::4])

    gq = lowbyte(inp["L0_ffn_gate_q"])
    uq = lowbyte(inp["L0_ffn_up_q"])
    dq = lowbyte(inp["L0_ffn_down_q"])
    out["wg8"] = np.ascontiguousarray(
        gq.reshape(32, 128, 86, 128).transpose(2, 1, 0, 3)).reshape(86, 128, 4096)
    out["wu8"] = np.ascontiguousarray(
        uq.reshape(32, 128, 86, 128).transpose(2, 1, 0, 3)).reshape(86, 128, 4096)
    out["wd8"] = np.ascontiguousarray(dq.reshape(86, 128, 4096))
    gs = np.asarray(inp["L0_ffn_gate_s"], np.float32)
    us = np.asarray(inp["L0_ffn_up_s"], np.float32)
    out["gsr"] = np.ascontiguousarray(
        gs.reshape(32, 86, 128).transpose(1, 0, 2)).reshape(86, 4096)
    out["usr"] = np.ascontiguousarray(
        us.reshape(32, 86, 128).transpose(1, 0, 2)).reshape(86, 4096)
    out["dsr"] = np.ascontiguousarray(np.asarray(inp["L0_ffn_down_s"], np.float32))
    out["win_t"] = np.ascontiguousarray(
        np.asarray(inp["L0_attn_Win"], np.float32).T).reshape(32, 128, 512)
    out["w1t"] = np.ascontiguousarray(
        np.transpose(np.asarray(inp["L0_attn_W1"], np.float32), (0, 2, 1))
    ).reshape(4, 4, 128, 2048)
    out["w2t"] = np.ascontiguousarray(
        np.transpose(np.asarray(inp["L0_attn_W2"], np.float32), (0, 2, 1))
    ).reshape(4, 16, 128, DIM)
    out["wout_t"] = np.ascontiguousarray(
        np.asarray(inp["L0_attn_Wout"], np.float32).T).reshape(4, 128, 4096)
    out["params"] = _pack_params(inp)
    shards = []
    for c in range(8):
        b, h = c // 2, c % 2
        xt = np.ascontiguousarray(x[b, h * T:(h + 1) * T, :].T)
        shards.append(np.ascontiguousarray(xt.reshape(32, 128, T)))
    out["_resid_shards"] = shards
    return out


def _pack_params(inp):
    arr = np.zeros((128, 288), np.float32)

    def put(name, vec128):
        arr[:, _P[name]] = np.asarray(vec128, np.float32).reshape(128)

    def putvec(name, v, ntile):
        v = np.asarray(v, np.float32).reshape(ntile, 128)
        for i in range(ntile):
            put(f"{name}{i}", v[i])

    putvec("bin", inp["L0_attn_bin"], 4)
    for l in range(DEPTH):
        putvec(f"ln1w_{l}_", inp["L0_attn_ln1_w"][l], 4)
        putvec(f"ln1b_{l}_", inp["L0_attn_ln1_b"][l], 4)
        putvec(f"ln2w_{l}_", inp["L0_attn_ln2_w"][l], 4)
        putvec(f"ln2b_{l}_", inp["L0_attn_ln2_b"][l], 4)
        putvec(f"phr_{l}_", inp["L0_attn_phz_re"][l], 4)
        putvec(f"phi_{l}_", inp["L0_attn_phz_im"][l], 4)
        putvec(f"lcr_{l}_", inp["L0_attn_lc_re"][l], 4)
        putvec(f"lci_{l}_", inp["L0_attn_lc_im"][l], 4)
        putvec(f"scl_{l}_", inp["L0_attn_sc_lin"][l], 4)
        putvec(f"b1_{l}_", inp["L0_attn_b1"][l], 16)
        putvec(f"b2_{l}_", inp["L0_attn_b2"][l], 4)
    putvec("lnlw", inp["L0_attn_lnl_w"], 4)
    putvec("lnlb", inp["L0_attn_lnl_b"], 4)
    putvec("bout", inp["L0_attn_bout"], 32)
    put("c_neghalf", np.full(128, -0.5))
    put("c_half", np.full(128, 0.5))
    return arr



class _Any:
    def __call__(self, *a, **k):
        return _Any()

    def __getattr__(self, nm):
        return _Any()

    def __enter__(self):
        return _Any()

    def __exit__(self, *a):
        return False


class BB:
    SEMS = ("sWa", "sWb", "sBg", "sBu", "sX", "sXa", "sXb", "sYa", "sYb",
            "sAa", "sAb", "sLB",
            "sLA", "sLN0", "sLN1", "sW12", "sDB",
            "sR0", "sR1", "sR2", "sR3", "sST0", "sST1", "sST2", "sST3",
            "sPE", "sV", "sS", "sEx", "sP0", "sP1", "sP2", "sP3")

    def __init__(self):
        self.n = {s: 0 for s in self.SEMS}
        self.h = {}
        self.L = {}
        self.seq = []
        self.mode = "plan"

    def on(self, eng, f):
        self.seq.append((eng, f))

    def inc(self, inst, nm, amt, key=None):
        if self.mode == "plan":
            self.n[nm] += amt
            if key is not None:
                self.L[key] = self.n[nm]
        else:
            inst.then_inc(self.h[nm], amt)

    def mark(self, nm, key):
        if self.mode == "plan":
            self.L[key] = self.n[nm]

    def wait(self, e, nm, key):
        if self.mode == "emit":
            v = key if isinstance(key, int) else self.L[key]
            if v > 0:
                e.wait_ge(self.h[nm], v)

    def snap(self, key):
        if self.mode == "plan":
            self.L[key] = dict(self.n)

    def bar(self, e, key):
        if self.mode == "emit":
            for nm, v in self.L[key].items():
                if v > 0:
                    e.wait_ge(self.h[nm], v)

    def run_plan(self):
        self.mode = "plan"
        fake = _Any()
        for eng, f in self.seq:
            f(fake)

    def run_emit(self, eng, e):
        self.mode = "emit"
        for en, f in self.seq:
            if en == eng:
                f(e)


def build_program(stop_after="full", debug_dump=None, det=False):
    nc = bass.Bass("TRN2", target_bir_lowering=False,
                   detect_race_conditions=det)
    NP = 288

    resid_in = nc.dram_tensor("resid_in", [DT, 128, T], F32, kind="ExternalInput").ap()
    wg8 = nc.dram_tensor("wg8", [IT, 128, 4096], U8, kind="ExternalInput").ap()
    wu8 = nc.dram_tensor("wu8", [IT, 128, 4096], U8, kind="ExternalInput").ap()
    wd8 = nc.dram_tensor("wd8", [IT, 128, 4096], U8, kind="ExternalInput").ap()
    gsr = nc.dram_tensor("gsr", [IT, 4096], F32, kind="ExternalInput").ap()
    usr = nc.dram_tensor("usr", [IT, 4096], F32, kind="ExternalInput").ap()
    dsr = nc.dram_tensor("dsr", [IT, 4096], F32, kind="ExternalInput").ap()
    win_t = nc.dram_tensor("win_t", [DT, 128, DIM], F32, kind="ExternalInput").ap()
    w1t = nc.dram_tensor("w1t", [DEPTH, 4, 128, 2048], F32, kind="ExternalInput").ap()
    w2t = nc.dram_tensor("w2t", [DEPTH, 16, 128, DIM], F32, kind="ExternalInput").ap()
    wout_t = nc.dram_tensor("wout_t", [DEPTH, 128, 4096], F32,
                            kind="ExternalInput").ap()
    params = nc.dram_tensor("params", [128, NP], F32, kind="ExternalInput").ap()
    evenflag = nc.dram_tensor("evenflag", [1, 1], I32, kind="ExternalInput").ap()
    emaskin = nc.dram_tensor("emaskin", [128, 2], F32, kind="ExternalInput").ap()

    resid_out = nc.dram_tensor("resid_out", [DT, 128, T], F32,
                               kind="ExternalOutput").ap()

    wgbf = nc.dram_tensor("wgbf", [IT, 128, 4096], BF16, kind="Internal").ap()
    wubf = nc.dram_tensor("wubf", [IT, 128, 4096], BF16, kind="Internal").ap()
    wdbf = nc.dram_tensor("wdbf", [IT, 128, 4096], BF16, kind="Internal").ap()
    at_s = nc.dram_tensor("at_s", [TC, 128, IT, 512], BF16, kind="Internal").ap()
    resid_s = nc.dram_tensor("resid_s", [DT, 128, T], F32, kind="Internal").ap()
    xch = nc.dram_tensor("xch", [DEPTH, 128, 8], F32, kind="Internal",
                         addr_space="Shared").ap()
    xfl = nc.dram_tensor("xfl", [DEPTH, 1], I32, kind="Internal",
                         addr_space="Shared").ap()
    dbg = None
    if debug_dump:
        shp = {"ybf": ([DT, 128, T], BF16), "at": ([TC, 128, IT, 512], BF16),
               "resid": ([DT, 128, T], F32), "h": ([ADT, 128, T], F32),
               "yct": ([1, 128, T], F32)}[debug_dump]
        dbg = nc.dram_tensor("dbg", shp[0], shp[1], kind="ExternalOutput").ap()

    bb = BB()
    L = bb.L
    on = bb.on
    es = ExitStack()
    with es:
        def sbt(name, shape, dt):
            return es.enter_context(nc.sbuf_tensor(name, shape, dt))

        par = sbt("par", [128, NP], F32)
        ones_k = sbt("ones_k", [128, 1], BF16)
        ones_b = sbt("ones_b", [1, 128], F32)
        rowsA = sbt("rowsA", [128, T], F32)
        rowsB = sbt("rowsB", [128, T], F32)
        r_rstd = rowsA[0:1, :]
        r_sq = rowsA[32:33, :]
        r_mean = rowsB[0:1, :]
        r_tmp = rowsB[32:33, :]
        magict = sbt("magict", [1, 1], I32)
        zerot = sbt("zerot", [1, 1], I32)
        rflag = sbt("rflag", [1, 1], I32)
        pollt = sbt("pollt", [1, 1], I32)
        emask = sbt("emask", [128, 2], F32)
        sx = sbt("sx", [128, 8], F32)
        rdat = sbt("rdat", [128, 8], F32)
        pwt = sbt("pwt", [128, 128], F32)
        prgs_r = sbt("prgs_r", [128, 512], F32)
        prgs_i = sbt("prgs_i", [128, 512], F32)
        nrd = sbt("nrd", [128, 8], F32)
        e0r = sbt("e0r", [128, NCH + 1], F32)
        e0i = sbt("e0i", [128, NCH + 1], F32)
        e1r = sbt("e1r", [128, NCH + 1], F32)
        e1i = sbt("e1i", [128, NCH + 1], F32)
        esn = sbt("esn", [128, NCH + 1], F32)
        s64a = sbt("s64a", [128, 64], F32)
        s64b = sbt("s64b", [128, 64], F32)

        ybf = sbt("ybf", [128, DT * T], BF16)
        bufA = sbt("bufA", [128, 4096], BF16)
        bufB = sbt("bufB", [128, 4096], BF16)
        bufC = sbt("bufC", [128, 4096], BF16)
        bufD = sbt("bufD", [128, 4096], BF16)
        sb_g = sbt("sb_g", [128, 2048], BF16)
        sb_u = sbt("sb_u", [128, 2048], BF16)
        sq0 = sbt("sq0", [128, T], BF16)
        sq1 = sbt("sq1", [128, T], BF16)
        rs_ = [sbt(f"rs{i}", [128, 512], F32) for i in range(4)]
        at0, at1 = sq0, sq1
        sil0 = rs_[3][:, :].bitcast(BF16)[:, 0:512]
        sil1 = rs_[3][:, :].bitcast(BF16)[:, 512:1024]

        psum = [es.enter_context(nc.psum_tensor(f"ps{i}", [128, 512], F32))
                for i in range(8)]

        def yb(c0, n):
            return ybf[:, c0:c0 + n]

        h_t = [b_[:, :].bitcast(F32) for b_ in (bufA, bufB, bufC, bufD)]
        y2bf = [yb(i * T, T) for i in range(ADT)]
        LAr = yb(8192, 4096).bitcast(F32)
        LBr = yb(12288, 4096).bitcast(F32)
        LAi = yb(16384, 4096).bitcast(F32)
        LBi = yb(20480, 4096).bitcast(F32)
        y_ct = yb(24576, 4096).bitcast(F32)
        w1bf = yb(28672, 8192)
        w2bf = yb(36864, 8192)
        woutbf = yb(45056, 16384)
        hbf = yb(61440, 2048)
        hbf2 = yb(63488, 2048)
        at_res = ybf[:, 0:IT * 512]


        blk = es.enter_context(nc.Block())

        def pcol(name):
            return par[:, _P[name]:_P[name] + 1]

        def pcol1(name):
            return par[0:1, _P[name]:_P[name] + 1]

        def mm(t, out, lhsT, rhs, start, stop):
            return t.matmul(out, lhsT, rhs, start=start, stop=stop,
                            skip_group_check=True)

        bank_free = [("sV", "p0_const")] * 8

        def cap_banks(idxs=range(8)):
            return [tuple(bank_free[b_]) for b_ in idxs]

        def wait_banks(t, cap):
            if bb.mode == "emit":
                seen = set()
                for nm, key in cap:
                    v = key if isinstance(key, int) else L.get(key, 0)
                    if v > 0 and (nm, v) not in seen:
                        seen.add((nm, v))
                        t.wait_ge(bb.h[nm], v)

        def set_bank_free(idxs, nm, key):
            for b_ in idxs:
                bank_free[b_] = (nm, key)

        def p0_sync(s):
            bb.inc(s.dma_start(par[:, :], params[:, :]), "sX", 16)
            bb.inc(s.dma_start(emask[:, :], emaskin[:, :]), "sX", 16)
            bb.inc(s.dma_start(rflag[:, :], evenflag[:, :]), "sX", 16, "p0_ld")

        def p0_vec(v):
            bb.wait(v, "sX", "p0_ld")
            v.memset(ones_k[:, :], 1.0)
            v.memset(ones_b[:, :], 1.0)
            v.memset(magict[:, :], MAGIC)
            v.memset(rdat[:, :], 0.0)
            bb.inc(v.memset(zerot[:, :], 0), "sV", 1, "p0_const")

        on("sync", p0_sync)
        on("vector", p0_vec)

        def emit_ln(tag, src, pre_bar=None):
            def f_gp(g):
                bb.wait(g, "sV", "p0_const")
                if pre_bar:
                    bb.bar(g, pre_bar)
                for dt in range(DT):
                    if dt >= 2:
                        bb.wait(g, "sPE", (tag, "st", dt - 2))
                    bb.inc(g.dma_start(ybf[:, dt * T:(dt + 1) * T], src[dt, :, :]),
                           ("sLN0", "sLN1")[dt % 2], 16, (tag, "x", dt))
            on("gpsimd", f_gp)

            def f_sq(sc):
                for dt in range(DT):
                    sq = (sq0, sq1)[dt % 2]
                    bb.wait(sc, ("sLN0", "sLN1")[dt % 2], (tag, "x", dt))
                    if dt >= 2:
                        bb.wait(sc, "sPE", (tag, "st", dt - 2))
                    bb.inc(sc.square(sq[:, :], ybf[:, dt * T:(dt + 1) * T]),
                           "sS", 1, (tag, "sq", dt))
            on("scalar", f_sq)

            bcap = cap_banks()

            def f_pe(t):
                bb.wait(t, "sV", "p0_const")
                for dt in range(DT):
                    sq = (sq0, sq1)[dt % 2]
                    bb.wait(t, "sS", (tag, "sq", dt))
                    if dt == 0:
                        wait_banks(t, bcap)
                    for tc in range(TC):
                        mm(t, psum[tc][0:1, :], ones_k[:, :],
                           ybf[:, dt * T + tc * 512:dt * T + (tc + 1) * 512],
                           dt == 0, dt == DT - 1)
                        li = mm(t, psum[4 + tc][0:1, :], ones_k[:, :],
                                sq[:, tc * 512:(tc + 1) * 512],
                                dt == 0, dt == DT - 1)
                    bb.inc(li, "sPE", 1, (tag, "st", dt))
            on("tensor", f_pe)

            def f_v1(v):
                bb.wait(v, "sPE", (tag, "st", DT - 1))
                for tc in range(TC):
                    v.tensor_copy(r_mean[0:1, tc * 512:(tc + 1) * 512],
                                  psum[tc][0:1, :])
                    v.tensor_copy(r_sq[0:1, tc * 512:(tc + 1) * 512],
                                  psum[4 + tc][0:1, :])
                v.tensor_scalar_mul(r_mean[0:1, :], r_mean[0:1, :], 1.0 / 4096.0)
                bb.inc(v.tensor_scalar_mul(r_sq[0:1, :], r_sq[0:1, :],
                                           1.0 / 4096.0),
                       "sV", 1, (tag, "ex2"))
            on("vector", f_v1)

            def f_msq(sc):
                bb.wait(sc, "sV", (tag, "ex2"))
                for tc in range(TC):
                    bb.inc(sc.square(psum[4 + tc][64:65, :],
                                     r_mean[0:1, tc * 512:(tc + 1) * 512]),
                           "sS", 1, (tag, "msq", tc))
            on("scalar", f_msq)

            def f_var(v):
                bb.wait(v, "sS", (tag, "msq", TC - 1))
                for tc in range(TC):
                    v.tensor_sub(r_sq[0:1, tc * 512:(tc + 1) * 512],
                                 r_sq[0:1, tc * 512:(tc + 1) * 512],
                                 psum[4 + tc][64:65, :])
                bb.inc(v.tensor_scalar_add(r_sq[0:1, :], r_sq[0:1, :], EPS),
                       "sV", 1, (tag, "var"))
            on("vector", f_var)

            def f_sc2(sc):
                bb.wait(sc, "sV", (tag, "var"))
                sc.activation(r_tmp[0:1, :], r_sq[0:1, :], AF.Ln)
                bb.inc(sc.activation(r_rstd[0:1, :], r_tmp[0:1, :], AF.Exp,
                                     scale=pcol1("c_neghalf")),
                       "sS", 1, (tag, "rstd"))
            on("scalar", f_sc2)

            def f_pe2(t):
                bb.wait(t, "sS", (tag, "rstd"))
                for tc in range(TC):
                    mm(t, psum[tc][:, :], ones_b[0:1, :],
                       r_rstd[0:1, tc * 512:(tc + 1) * 512], True, True)
                    li = mm(t, psum[4 + tc][:, :], ones_b[0:1, :],
                            r_mean[0:1, tc * 512:(tc + 1) * 512], True, True)
                bb.inc(li, "sPE", 1, (tag, "bcast"))
            on("tensor", f_pe2)

            def f_v3(v):
                bb.wait(v, "sPE", (tag, "bcast"))
                for dt in range(DT):
                    for tc in range(TC):
                        s5 = slice(dt * T + tc * 512, dt * T + (tc + 1) * 512)
                        v.tensor_sub(sq1[:, tc * 512:(tc + 1) * 512],
                                     ybf[:, s5], psum[4 + tc][:, :])
                        li = v.tensor_mul(ybf[:, s5],
                                          sq1[:, tc * 512:(tc + 1) * 512],
                                          psum[tc][:, :])
                bb.inc(li, "sV", 1, (tag, "done"))
            on("vector", f_v3)
            set_bank_free(range(8), "sV", (tag, "done"))

        def emit_gateup(tag, call, ln_tag, pre_bar=None):
            parbufs = [(bufA, bufC), (bufB, bufD)]

            if call == 1:
                def f_gp(g):
                    if pre_bar:
                        bb.bar(g, pre_bar)
                    for k in range(IT):
                        gq, uq = parbufs[k % 2]
                        if k >= 2:
                            bb.wait(g, "sPE", (tag, "udone", k - 2))
                            bb.wait(g, ("sYa", "sYb")[k % 2], (tag, "wb", k - 2))
                        sw = ("sWa", "sWb")[k % 2]
                        bb.inc(g.dma_start(gq[:, :], wg8[k, :, :]), sw, 16)
                        bb.inc(g.dma_start(uq[:, :], wu8[k, :, :]), sw, 16,
                               (tag, "cast", k))
                        for hh in range(2):
                            if k >= 1 or hh == 1:
                                pk, ph = (k, 0) if hh == 1 else (k - 1, 1)
                                bb.wait(g, "sV", (tag, "dqg", pk, ph))
                                bb.wait(g, "sV", (tag, "dqu", pk, ph))
                            src_g = bass.AP(gsr.tensor, k * 4096 + hh * 2048,
                                            [[0, 128], [1, 2048]])
                            src_u = bass.AP(usr.tensor, k * 4096 + hh * 2048,
                                            [[0, 128], [1, 2048]])
                            bb.inc(g.dma_start(sb_g[:, :], src_g), "sBg", 16,
                                   (tag, "bcg", k, hh))
                            bb.inc(g.dma_start(sb_u[:, :], src_u), "sBu", 16,
                                   (tag, "bcu", k, hh))
                on("gpsimd", f_gp)

                def f_vdq_atm(v):
                    def atm(k):
                        att = (at0, at1)[k % 2]
                        bb.wait(v, "sPE", (tag, "udone", k))
                        if k >= 2:
                            bb.wait(v, ("sAa", "sAb")[k % 2], (tag, "ats", k - 2))
                        for tc in range(TC):
                            sil = (sil0, sil1)[tc % 2]
                            bb.wait(v, "sS", (tag, "silt", k, tc))
                            bb.inc(v.tensor_mul(att[:, tc * 512:(tc + 1) * 512],
                                                sil[:, :], psum[4 + tc][:, :]),
                                   "sV", 1, (tag, "atmt", k, tc))
                    for k in range(IT):
                        gq, uq = parbufs[k % 2]
                        bb.wait(v, ("sWa", "sWb")[k % 2], (tag, "cast", k))
                        for hh in range(2):
                            sl = slice(hh * 2048, (hh + 1) * 2048)
                            bb.wait(v, "sBg", (tag, "bcg", k, hh))
                            bb.inc(v.scalar_tensor_tensor(
                                gq[:, sl], gq[:, sl], 8.0, sb_g[:, :],
                                op0=ALU.subtract, op1=ALU.mult),
                                "sV", 1, (tag, "dqg", k, hh))
                            bb.wait(v, "sBu", (tag, "bcu", k, hh))
                            bb.inc(v.scalar_tensor_tensor(
                                uq[:, sl], uq[:, sl], 8.0, sb_u[:, :],
                                op0=ALU.subtract, op1=ALU.mult),
                                "sV", 1, (tag, "dqu", k, hh))
                        bb.mark("sV", (tag, "dq", k))
                        if k >= 1:
                            atm(k - 1)
                    atm(IT - 1)
                on("vector", f_vdq_atm)

                def f_wb_sil(sc):
                    for k in range(IT):
                        gq, uq = parbufs[k % 2]
                        bb.wait(sc, "sV", (tag, "dq", k))
                        sy = ("sYa", "sYb")[k % 2]
                        bb.inc(sc.dma_start(wgbf[k, :, :], gq[:, :]), sy, 16)
                        bb.inc(sc.dma_start(wubf[k, :, :], uq[:, :]), sy, 16,
                               (tag, "wb", k))
                        bb.wait(sc, "sPE", (tag, "gdone", k))
                        for tc in range(TC):
                            sil = (sil0, sil1)[tc % 2]
                            if k > 0 or tc >= 2:
                                kk2, tc2 = ((k, tc - 2) if tc >= 2
                                            else (k - 1, tc + 2))
                                bb.wait(sc, "sV", (tag, "atmt", kk2, tc2))
                            bb.inc(sc.activation(sil[:, :], psum[tc][:, :],
                                                 AF.Silu),
                                   "sS", 1, (tag, "silt", k, tc))
                on("scalar", f_wb_sil)
            else:
                def f_ld(s):
                    if pre_bar:
                        bb.bar(s, pre_bar)
                    for k in range(IT):
                        gq, uq = parbufs[k % 2]
                        if k >= 2:
                            bb.wait(s, "sPE", (tag, "udone", k - 2))
                        sw = ("sXa", "sXb")[k % 2]
                        bb.inc(s.dma_start(gq[:, :], wgbf[k, :, :]), sw, 16)
                        bb.inc(s.dma_start(uq[:, :], wubf[k, :, :]), sw, 16,
                               (tag, "ld", k))
                on("sync", f_ld)

                def f_atm2(v):
                    for k in range(IT):
                        att = (at0, at1)[k % 2]
                        bb.wait(v, "sPE", (tag, "udone", k))
                        if k >= 2:
                            bb.wait(v, ("sAa", "sAb")[k % 2], (tag, "ats", k - 2))
                        for tc in range(TC):
                            sil = (sil0, sil1)[tc % 2]
                            bb.wait(v, "sS", (tag, "silt", k, tc))
                            bb.inc(v.tensor_mul(att[:, tc * 512:(tc + 1) * 512],
                                                sil[:, :], psum[4 + tc][:, :]),
                                   "sV", 1, (tag, "atmt", k, tc))
                on("vector", f_atm2)

                def f_sil_ats(sc):
                    for k in range(IT):
                        att = (at0, at1)[k % 2]
                        bb.wait(sc, "sPE", (tag, "gdone", k))
                        for tc in range(TC):
                            sil = (sil0, sil1)[tc % 2]
                            if k > 0 or tc >= 2:
                                kk2, tc2 = ((k, tc - 2) if tc >= 2
                                            else (k - 1, tc + 2))
                                bb.wait(sc, "sV", (tag, "atmt", kk2, tc2))
                            bb.inc(sc.activation(sil[:, :], psum[tc][:, :],
                                                 AF.Silu),
                                   "sS", 1, (tag, "silt", k, tc))
                        if k >= 1:
                            for tc in range(TC):
                                bb.wait(sc, "sV", (tag, "atmt", k - 1, tc))
                                bb.inc(sc.dma_start(
                                    at_s[tc, :, k - 1, :],
                                    (at0, at1)[(k - 1) % 2]
                                    [:, tc * 512:(tc + 1) * 512]),
                                    ("sAa", "sAb")[(k - 1) % 2], 16,
                                    (tag, "ats", k - 1) if tc == TC - 1
                                    else None)
                    for tc in range(TC):
                        bb.wait(sc, "sV", (tag, "atmt", IT - 1, tc))
                        bb.inc(sc.dma_start(
                            at_s[tc, :, IT - 1, :],
                            (at0, at1)[(IT - 1) % 2][:, tc * 512:(tc + 1) * 512]),
                            ("sAa", "sAb")[(IT - 1) % 2], 16,
                            (tag, "ats", IT - 1) if tc == TC - 1 else None)
                on("scalar", f_sil_ats)

            bcap = cap_banks()

            def f_pe(t):
                bb.wait(t, "sV", (ln_tag, "done"))
                for k in range(IT):
                    gq, uq = parbufs[k % 2]
                    if call == 1:
                        bb.wait(t, "sV", (tag, "dq", k))
                    else:
                        bb.wait(t, ("sXa", "sXb")[k % 2], (tag, "ld", k))
                    if k == 0:
                        wait_banks(t, bcap)
                    if k >= 1:
                        bb.wait(t, "sS", (tag, "silt", k - 1, TC - 1))
                    for kk in range(DT):
                        wsl = gq[:, kk * 128:(kk + 1) * 128]
                        for tc in range(TC):
                            li = mm(t, psum[tc][:, :], wsl,
                                    ybf[:, kk * T + tc * 512:
                                        kk * T + (tc + 1) * 512],
                                    kk == 0, kk == DT - 1)
                    bb.inc(li, "sPE", 1, (tag, "gdone", k))
                    if k >= 1:
                        bb.wait(t, "sV", (tag, "atmt", k - 1, TC - 1))
                    for kk in range(DT):
                        wsl = uq[:, kk * 128:(kk + 1) * 128]
                        for tc in range(TC):
                            li = mm(t, psum[4 + tc][:, :], wsl,
                                    ybf[:, kk * T + tc * 512:
                                        kk * T + (tc + 1) * 512],
                                    kk == 0, kk == DT - 1)
                    bb.inc(li, "sPE", 1, (tag, "udone", k))
            on("tensor", f_pe)

            if call == 1:
                def f_ats(e):
                    for k in range(IT):
                        att = (at0, at1)[k % 2]
                        for tc in range(TC):
                            bb.wait(e, "sV", (tag, "atmt", k, tc))
                            bb.inc(e.dma_start(at_s[tc, :, k, :],
                                               att[:, tc * 512:(tc + 1) * 512]),
                                   ("sAa", "sAb")[k % 2], 16,
                                   (tag, "ats", k) if tc == TC - 1 else None)
                on("sync", f_ats)

            set_bank_free(range(4), "sS", (tag, "silt", IT - 1, TC - 1))
            set_bank_free(range(4, 8), "sV", (tag, "atmt", IT - 1, TC - 1))

        def emit_down(tag, call, rsrc, rdst, pre_bar, out_sem):
            parw = [bufA, bufB, bufC, bufD]
            sbb = [sb_g, sb_u]
            kblocks = []
            kk = 0
            while kk < IT:
                kblocks.append((kk, min(2, IT - kk)))
                kk += 2
            NKB = len(kblocks)
            dqmode = (call == 1)

            def wd_re(ap3):
                return ap3.rearrange("k p j -> p k j")

            KH = 43
            slots = [ybf[:, 0:KH * 512], ybf[:, KH * 512:2 * KH * 512], None]

            def slot_tile(s, t):
                if s < 2:
                    return slots[s][:, t * 512:(t + 1) * 512]
                if t < 42:
                    return ybf[:, (86 + t) * 512:(86 + t + 1) * 512]
                return sq0[:, 0:512]

            def slotA(tc):
                return (2 * tc) % 3

            def slotB(tc):
                return (2 * tc + 1) % 3

            def at_rd(tc, k):
                if k < KH:
                    return slot_tile(slotA(tc), k)
                return slot_tile(slotB(tc), k - KH)

            def lda_emit(e, sem, tc, half, key):
                s = slotA(tc) if half == 0 else slotB(tc)
                k0 = 0 if half == 0 else KH
                if s < 2:
                    bb.inc(e.dma_start(
                        slots[s][:, :],
                        at_s[tc, :, k0:k0 + KH, :].rearrange("p k j -> p (k j)")),
                        sem, 16, key)
                else:
                    bb.inc(e.dma_start(
                        ybf[:, 86 * 512:128 * 512],
                        at_s[tc, :, k0:k0 + 42, :].rearrange("p k j -> p (k j)")),
                        sem, 16)
                    bb.inc(e.dma_start(sq0[:, 0:512], at_s[tc, :, k0 + 42, :]),
                           sem, 16, key)

            def f_sync(s):
                bb.bar(s, pre_bar)
                lda_emit(s, "sLA", 0, 0, (tag, "ldA", 0))
                lda_emit(s, "sLA", 0, 1, (tag, "ldB", 0))
                lda_emit(s, "sLA", 1, 0, (tag, "ldA", 1))
                prevuse = {}
                for tc in range(TC):
                    for o in range(4):
                        for m in range(8):
                            dt = o * 8 + m
                            rs = rs_[m % 4]
                            pk = prevuse.get(m % 4)
                            if pk is not None:
                                bb.wait(s, "sV", pk[0])
                                bb.wait(s, f"sST{m % 4}", pk[1])
                            bb.inc(s.dma_start(
                                rs[:, :], rsrc[dt, :, tc * 512:(tc + 1) * 512]),
                                f"sR{m % 4}", 16, (tag, "rld", tc, o, m))
                            prevuse[m % 4] = ((tag, "evac", tc, o, m),
                                              (tag, "ost", tc, o, m))
            on("sync", f_sync)

            def f_ldw(g):
                if pre_bar:
                    bb.bar(g, pre_bar)
                bidx = 0
                for tc in range(TC):
                    for o in range(4):
                        for bi, (k0, nk) in enumerate(kblocks):
                            wbuf = parw[bidx % 4]
                            if bidx >= 4:
                                bb.wait(g, "sPE", (tag, "wused", bidx - 4))
                            sw = ("sWa", "sWb")[bidx % 2]
                            if dqmode and tc == 0:
                                bb.inc(g.dma_start(
                                    wbuf[:, 0:nk * 1024],
                                    wd_re(wd8)[:, k0:k0 + nk,
                                               o * 1024:(o + 1) * 1024]),
                                    sw, 16, (tag, "wld", bidx))
                                if bidx >= 2:
                                    bb.wait(g, "sV", (tag, "wdqp", bidx - 2))
                                src_s = bass.AP(
                                    dsr.tensor, k0 * 4096 + o * 1024,
                                    [[0, 128], [4096, nk], [1, 1024]])
                                bb.inc(g.dma_start(
                                    sbb[bidx % 2][:, 0:nk * 1024], src_s),
                                    ("sBg", "sBu")[bidx % 2], 16,
                                    (tag, "wbc", bidx))
                            else:
                                if dqmode:
                                    bb.wait(g, ("sYa", "sYb")[(o * NKB + bi) % 2],
                                            (tag, "wwb", o * NKB + bi))
                                bb.inc(g.dma_start(
                                    wbuf[:, 0:nk * 1024],
                                    wd_re(wdbf)[:, k0:k0 + nk,
                                                o * 1024:(o + 1) * 1024]),
                                    sw, 16, (tag, "wld", bidx))
                            bidx += 1
            on("gpsimd", f_ldw)

            def f_wdq_ev(v):
                for tc in range(TC):
                    for o in range(4):
                        if dqmode and tc == 0:
                            for bi, (k0, nk) in enumerate(kblocks):
                                bidx = o * NKB + bi
                                wbuf = parw[bidx % 4]
                                bb.wait(v, ("sWa", "sWb")[bidx % 2],
                                        (tag, "wld", bidx))
                                bb.wait(v, ("sBg", "sBu")[bidx % 2],
                                        (tag, "wbc", bidx))
                                bb.inc(v.scalar_tensor_tensor(
                                    wbuf[:, 0:nk * 1024],
                                    wbuf[:, 0:nk * 1024], 8.0,
                                    sbb[bidx % 2][:, 0:nk * 1024],
                                    op0=ALU.subtract, op1=ALU.mult),
                                    "sV", 1, (tag, "wdqp", bidx))
                        bb.wait(v, "sPE", (tag, "odone", tc, o))
                        for m in range(8):
                            rs = rs_[m % 4]
                            bb.wait(v, f"sR{m % 4}", (tag, "rld", tc, o, m))
                            bb.inc(v.scalar_tensor_tensor(
                                rs[:, :], psum[m][:, :], 0.5, rs[:, :],
                                op0=ALU.mult, op1=ALU.add),
                                "sV", 1, (tag, "evac", tc, o, m))
            on("vector", f_wdq_ev)

            bcap = cap_banks()

            def f_pe(t):
                bidx = 0
                for tc in range(TC):
                    laA_sem = "sLA" if tc <= 1 else "sLB"
                    laB_sem = "sLA" if tc == 0 else "sLB"
                    bb.wait(t, laA_sem, (tag, "ldA", tc))
                    for o in range(4):
                        if (tc, o) == (0, 0):
                            wait_banks(t, bcap)
                        else:
                            po = (tc, o - 1) if o > 0 else (tc - 1, 3)
                            bb.wait(t, "sV", (tag, "evac", po[0], po[1], 7))
                        for bi, (k0, nk) in enumerate(kblocks):
                            wbuf = parw[bidx % 4]
                            if k0 >= KH or (k0 + nk > KH):
                                bb.wait(t, laB_sem, (tag, "ldB", tc))
                            if dqmode and tc == 0:
                                bb.wait(t, "sV", (tag, "wdqp", bidx))
                            else:
                                bb.wait(t, ("sWa", "sWb")[bidx % 2],
                                        (tag, "wld", bidx))
                            for ki in range(nk):
                                k = k0 + ki
                                for m_ in range(8):
                                    li = mm(t, psum[m_][:, :],
                                            wbuf[:, ki * 1024 + m_ * 128:
                                                 ki * 1024 + (m_ + 1) * 128],
                                            at_rd(tc, k),
                                            k == 0, k == IT - 1)
                            bb.inc(li, "sPE", 1, (tag, "wused", bidx))
                            bidx += 1
                        bb.mark("sPE", (tag, "odone", tc, o))
            on("tensor", f_pe)

            def f_wwb_ost(sc):
                for tc in range(TC):
                    for o in range(4):
                        if dqmode and tc == 0:
                            for bi, (k0, nk) in enumerate(kblocks):
                                bidx = o * NKB + bi
                                wbuf = parw[bidx % 4]
                                bb.wait(sc, "sV", (tag, "wdqp", bidx))
                                bb.inc(sc.dma_start(
                                    wd_re(wdbf)[:, k0:k0 + nk,
                                                o * 1024:(o + 1) * 1024],
                                    wbuf[:, 0:nk * 1024]),
                                    ("sYa", "sYb")[bidx % 2], 16,
                                    (tag, "wwb", bidx))
                        for m in range(8):
                            dt = o * 8 + m
                            rs = rs_[m % 4]
                            bb.wait(sc, "sV", (tag, "evac", tc, o, m))
                            bb.inc(sc.dma_start(
                                rdst[dt, :, tc * 512:(tc + 1) * 512], rs[:, :]),
                                f"sST{m % 4}", 16, (tag, "ost", tc, o, m))
                    if tc + 1 < TC and tc + 1 >= 2:
                        lda_emit(sc, "sLB", tc + 1, 0, (tag, "ldA", tc + 1))
                    if tc + 1 < TC:
                        lda_emit(sc, "sLB", tc + 1, 1, (tag, "ldB", tc + 1))
            on("scalar", f_wwb_ost)
            set_bank_free(range(8), "sV", (tag, "evac", TC - 1, 3, 7))

        def emit_attn_ln(tag, hready_key):
            def f_cast(sc):
                for dt in range(ADT):
                    hb = (hbf, hbf2)[dt % 2]
                    sq = (sq0, sq1)[dt % 2]
                    bb.wait(sc, "sV", hready_key(dt))
                    if dt >= 2:
                        bb.wait(sc, "sPE", (tag, "st", dt - 2))
                    sc.copy(hb[:, :], h_t[dt])
                    bb.inc(sc.square(sq[:, 0:T], h_t[dt]), "sS", 1,
                           (tag, "sq", dt))
            on("scalar", f_cast)

            bcap = cap_banks()

            def f_pe(t):
                for dt in range(ADT):
                    hb = (hbf, hbf2)[dt % 2]
                    sq = (sq0, sq1)[dt % 2]
                    bb.wait(t, "sS", (tag, "sq", dt))
                    if dt == 0:
                        wait_banks(t, bcap)
                    for tc in range(TC):
                        mm(t, psum[tc][0:1, :], ones_k[:, :],
                           hb[:, tc * 512:(tc + 1) * 512], dt == 0, dt == ADT - 1)
                        li = mm(t, psum[4 + tc][0:1, :], ones_k[:, :],
                                sq[:, tc * 512:(tc + 1) * 512],
                                dt == 0, dt == ADT - 1)
                    bb.inc(li, "sPE", 1, (tag, "st", dt))
            on("tensor", f_pe)

            def f_v1(v):
                bb.wait(v, "sPE", (tag, "st", ADT - 1))
                for tc in range(TC):
                    v.tensor_copy(r_mean[0:1, tc * 512:(tc + 1) * 512],
                                  psum[tc][0:1, :])
                    v.tensor_copy(r_sq[0:1, tc * 512:(tc + 1) * 512],
                                  psum[4 + tc][0:1, :])
                v.tensor_scalar_mul(r_mean[0:1, :], r_mean[0:1, :], 1.0 / 512.0)
                bb.inc(v.tensor_scalar_mul(r_sq[0:1, :], r_sq[0:1, :],
                                           1.0 / 512.0),
                       "sV", 1, (tag, "ex2"))
            on("vector", f_v1)

            def f_msq(sc):
                bb.wait(sc, "sV", (tag, "ex2"))
                for tc in range(TC):
                    bb.inc(sc.square(psum[4 + tc][64:65, :],
                                     r_mean[0:1, tc * 512:(tc + 1) * 512]),
                           "sS", 1, (tag, "msq", tc))
            on("scalar", f_msq)

            def f_var(v):
                bb.wait(v, "sS", (tag, "msq", TC - 1))
                for tc in range(TC):
                    v.tensor_sub(r_sq[0:1, tc * 512:(tc + 1) * 512],
                                 r_sq[0:1, tc * 512:(tc + 1) * 512],
                                 psum[4 + tc][64:65, :])
                bb.inc(v.tensor_scalar_add(r_sq[0:1, :], r_sq[0:1, :], EPS),
                       "sV", 1, (tag, "var"))
            on("vector", f_var)

            def f_sc2(sc):
                bb.wait(sc, "sV", (tag, "var"))
                sc.activation(r_tmp[0:1, :], r_sq[0:1, :], AF.Ln)
                bb.inc(sc.activation(r_rstd[0:1, :], r_tmp[0:1, :], AF.Exp,
                                     scale=pcol1("c_neghalf")),
                       "sS", 1, (tag, "rstd"))
            on("scalar", f_sc2)

            def f_pe2(t):
                bb.wait(t, "sS", (tag, "rstd"))
                for tc in range(TC):
                    mm(t, psum[tc][:, :], ones_b[0:1, :],
                       r_rstd[0:1, tc * 512:(tc + 1) * 512], True, True)
                    li = mm(t, psum[4 + tc][:, :], ones_b[0:1, :],
                            r_mean[0:1, tc * 512:(tc + 1) * 512], True, True)
                bb.inc(li, "sPE", 1, (tag, "bcast"))
            on("tensor", f_pe2)

        def emit_attn_norm_out(tag, wname, bname, func, outf):
            def f_vn(v):
                bb.wait(v, "sPE", (tag, "bcast"))
                for dt in range(ADT):
                    for tc in range(TC):
                        i = dt * TC + tc
                        ra, rb = ((rs_[0], rs_[1]) if i % 2 == 0
                                  else (rs_[2], rs_[3]))
                        if i >= 2:
                            bb.wait(v, "sS", (tag, "out", i - 2))
                        s5 = slice(tc * 512, (tc + 1) * 512)
                        v.tensor_sub(ra[:, :], h_t[dt][:, s5], psum[4 + tc][:, :])
                        bb.inc(v.tensor_mul(rb[:, :], ra[:, :], psum[tc][:, :]),
                               "sV", 1, (tag, "nt", i))
            on("vector", f_vn)

            def f_so(sc):
                for dt in range(ADT):
                    for tc in range(TC):
                        i = dt * TC + tc
                        rb = rs_[1] if i % 2 == 0 else rs_[3]
                        bb.wait(sc, "sV", (tag, "nt", i))
                        kw = {}
                        if wname is not None:
                            kw = dict(scale=pcol(f"{wname}{dt}"),
                                      bias=pcol(f"{bname}{dt}"))
                        bb.inc(sc.activation(outf(dt)[:, tc * 512:(tc + 1) * 512],
                                             rb[:, :], func, **kw),
                               "sS", 1, (tag, "out", i))
                bb.mark("sS", (tag, "outdone"))
            on("scalar", f_so)
            set_bank_free(range(8), "sV", (tag, "nt", ADT * TC - 1))

        def emit_scan(l, ct):
            tag = ("sc", l, ct)
            pw = pwt[:, ct * 32:(ct + 1) * 32]
            prg_r = prgs_r[:, ct * CH:(ct + 1) * CH]
            prg_i = prgs_i[:, ct * CH:(ct + 1) * CH]

            def f_tables(v):
                bb.wait(v, "sS", ("y", l, ct))
                pr = pcol(f"phr_{l}_{ct}")
                pi = pcol(f"phi_{l}_{ct}")
                v.tensor_mul(s64a[:, 0:1], pr, pr)
                v.tensor_mul(s64b[:, 0:1], pi, pi)
                v.drain()
                bb.inc(v.scalar_tensor_tensor(s64a[:, 1:2], s64a[:, 0:1],
                                              1e-38, s64b[:, 0:1],
                                              op0=ALU.add, op1=ALU.add),
                       "sV", 1, (tag, "r2"))
            on("vector", f_tables)

            def f_pr(sc):
                bb.wait(sc, "sV", (tag, "r2"))
                sc.activation(s64a[:, 2:3], s64a[:, 1:2], AF.Ln)
                sc.activation(s64a[:, 3:4], s64a[:, 2:3], AF.Exp,
                              scale=pcol("c_half"))
                bb.inc(sc.activation(s64a[:, 4:5], s64a[:, 3:4], AF.Tanh),
                       "sS", 1, (tag, "tanh"))
            on("scalar", f_pr)

            def f_pw(v):
                bb.wait(v, "sS", (tag, "tanh"))
                v.tensor_scalar_max(s64a[:, 3:4], s64a[:, 3:4], 1e-30)
                v.drain()
                v.reciprocal(s64b[:, 0:1], s64a[:, 3:4])
                v.drain()
                v.tensor_mul(s64a[:, 5:6], s64a[:, 4:5], s64b[:, 0:1])
                v.drain()
                v.tensor_mul(pw[:, 0:1], pcol(f"phr_{l}_{ct}"), s64a[:, 5:6])
                v.tensor_mul(pw[:, 1:2], pcol(f"phi_{l}_{ct}"), s64a[:, 5:6])
                for m in range(1, 12):
                    a = pw[:, 2 * m - 2:2 * m - 1]
                    b = pw[:, 2 * m - 1:2 * m]
                    v.tensor_mul(s64a[:, 6:7], a, a)
                    v.tensor_mul(s64b[:, 1:2], b, b)
                    v.tensor_mul(s64a[:, 7:8], a, b)
                    v.tensor_sub(pw[:, 2 * m:2 * m + 1], s64a[:, 6:7],
                                 s64b[:, 1:2])
                    v.tensor_add(pw[:, 2 * m + 1:2 * m + 2], s64a[:, 7:8],
                                 s64a[:, 7:8])
                v.tensor_copy(prg_i[:, 0:1], pw[:, 1:2])
                v.tensor_copy(prg_r[:, 0:1], pw[:, 0:1])
                for m in range(7):
                    w = 1 << m
                    ar = pw[:, 2 * m:2 * m + 1]
                    ai = pw[:, 2 * m + 1:2 * m + 2]
                    v.tensor_scalar_mul(s64b[:, 0:w], prg_i[:, 0:w], ar)
                    v.tensor_scalar_mul(s64a[:, 0:w], prg_i[:, 0:w], ai)
                    v.scalar_tensor_tensor(prg_i[:, w:2 * w], prg_r[:, 0:w],
                                           ai, s64b[:, 0:w],
                                           op0=ALU.mult, op1=ALU.add)
                    li = v.scalar_tensor_tensor(prg_r[:, w:2 * w],
                                                prg_r[:, 0:w],
                                                ar, s64a[:, 0:w],
                                                op0=ALU.mult, op1=ALU.subtract)
                bb.inc(li, "sV", 1, (tag, "tables"))
            on("vector", f_pw)

            def f_intra(v):
                y3 = y_ct.rearrange("p (c j) -> p c j", c=NCH)
                Ar = LAr.rearrange("p (c j) -> p c j", c=NCH)
                Ai = LAi.rearrange("p (c j) -> p c j", c=NCH)
                Br = LBr.rearrange("p (c j) -> p c j", c=NCH)
                Bi = LBi.rearrange("p (c j) -> p c j", c=NCH)
                v.scalar_tensor_tensor(Ar[:, :, 1:], y3[:, :, :CH - 1],
                                       pw[:, 0:1], y3[:, :, 1:],
                                       op0=ALU.mult, op1=ALU.add)
                v.tensor_scalar_mul(Ai[:, :, 1:], y3[:, :, :CH - 1], pw[:, 1:2])
                v.tensor_copy(Ar[:, :, 0:1], y3[:, :, 0:1])
                v.memset(Ai[:, :, 0:1], 0.0)
                cur = (Ar, Ai)
                oth = (Br, Bi)
                for m in range(1, 7):
                    d = 1 << m
                    sr, si = cur
                    dr, di = oth
                    ar = pw[:, 2 * m:2 * m + 1]
                    ai = pw[:, 2 * m + 1:2 * m + 2]
                    nai = s64a[:, 8:9]
                    v.tensor_scalar_mul(nai, ai, -1.0)
                    v.scalar_tensor_tensor(dr[:, :, d:], sr[:, :, :CH - d], ar,
                                           sr[:, :, d:],
                                           op0=ALU.mult, op1=ALU.add)
                    v.scalar_tensor_tensor(di[:, :, d:], si[:, :, :CH - d], ar,
                                           si[:, :, d:],
                                           op0=ALU.mult, op1=ALU.add)
                    v.scalar_tensor_tensor(dr[:, :, d:], si[:, :, :CH - d], nai,
                                           dr[:, :, d:],
                                           op0=ALU.mult, op1=ALU.add)
                    v.scalar_tensor_tensor(di[:, :, d:], sr[:, :, :CH - d], ai,
                                           di[:, :, d:],
                                           op0=ALU.mult, op1=ALU.add)
                    v.tensor_copy(dr[:, :, 0:d], sr[:, :, 0:d])
                    v.tensor_copy(di[:, :, 0:d], si[:, :, 0:d])
                    cur, oth = oth, cur
                fr, fi = cur
                v.tensor_copy(e0r[:, 1:NCH + 1],
                              fr[:, :, CH - 1:CH].rearrange("p c j -> p (c j)"))
                bb.inc(v.tensor_copy(
                    e0i[:, 1:NCH + 1],
                    fi[:, :, CH - 1:CH].rearrange("p c j -> p (c j)")),
                    "sV", 1, (tag, "intra"))
            on("vector", f_intra)

            def f_escan(v):
                lcr = pcol(f"lcr_{l}_{ct}")
                lci = pcol(f"lci_{l}_{ct}")
                v.tensor_scalar_mul(e0r[:, 0:1], lcr, emask[:, 0:1])
                v.tensor_scalar_mul(e0i[:, 0:1], lci, emask[:, 0:1])
                cur = (e0r, e0i)
                oth = (e1r, e1i)
                N = NCH + 1
                for mi, m in enumerate(range(7, 12)):
                    d = 1 << mi
                    sr, si = cur
                    dr, di = oth
                    ar = pw[:, 2 * m:2 * m + 1]
                    ai = pw[:, 2 * m + 1:2 * m + 2]
                    nai = s64a[:, 9:10]
                    v.tensor_scalar_mul(nai, ai, -1.0)
                    v.scalar_tensor_tensor(dr[:, d:N], sr[:, 0:N - d], ar,
                                           sr[:, d:N], op0=ALU.mult, op1=ALU.add)
                    v.scalar_tensor_tensor(di[:, d:N], si[:, 0:N - d], ar,
                                           si[:, d:N], op0=ALU.mult, op1=ALU.add)
                    v.scalar_tensor_tensor(dr[:, d:N], si[:, 0:N - d], nai,
                                           dr[:, d:N], op0=ALU.mult, op1=ALU.add)
                    v.scalar_tensor_tensor(di[:, d:N], sr[:, 0:N - d], ai,
                                           di[:, d:N], op0=ALU.mult, op1=ALU.add)
                    v.tensor_copy(dr[:, 0:d], sr[:, 0:d])
                    v.tensor_copy(di[:, 0:d], si[:, 0:d])
                    cur, oth = oth, cur
                fr, fi = cur
                v.tensor_copy(sx[:, 2 * ct:2 * ct + 1], fr[:, NCH:NCH + 1])
                v.tensor_copy(sx[:, 2 * ct + 1:2 * ct + 2], fi[:, NCH:NCH + 1])
                bb.inc(v.tensor_scalar_mul(esn[:, :], fi[:, :], -1.0),
                       "sV", 1, (tag, "escan"))
            on("vector", f_escan)

            FINr3 = LAr.rearrange("p (c j) -> p c j", c=NCH)

            def f_apply(v):
                for c in range(NCH):
                    v.scalar_tensor_tensor(FINr3[:, c, :], prg_r[:, :],
                                           e1r[:, c:c + 1], FINr3[:, c, :],
                                           op0=ALU.mult, op1=ALU.add)
                for c in range(NCH):
                    v.scalar_tensor_tensor(FINr3[:, c, :], prg_i[:, :],
                                           esn[:, c:c + 1], FINr3[:, c, :],
                                           op0=ALU.mult, op1=ALU.add)
                bb.inc(v.scalar_tensor_tensor(
                    h_t[ct], FINr3.rearrange("p c j -> p (c j)"),
                    pcol(f"scl_{l}_{ct}"), h_t[ct],
                    op0=ALU.mult, op1=ALU.add),
                    "sV", 1, ("hupd", l, ct))
            on("vector", f_apply)

        def emit_scan_corr(l, ct):
            pw = pwt[:, ct * 32:(ct + 1) * 32]
            prg_r = prgs_r[:, ct * CH:(ct + 1) * CH]
            prg_i = prgs_i[:, ct * CH:(ct + 1) * CH]

            def f_corr(v):
                if ct == 0:
                    bb.wait(v, "sEx", ("exr", l))
                    bb.inc(v.tensor_scalar_mul(nrd[:, :], rdat[:, :], -1.0),
                           "sV", 1, ("nrd", l))
                ptr = LBr
                pti = LBi
                v.tensor_copy(pti[:, 0:CH], prg_i[:, :])
                v.tensor_copy(ptr[:, 0:CH], prg_r[:, :])
                for mi, m in enumerate(range(7, 11)):
                    w = CH << mi
                    ar = pw[:, 2 * m:2 * m + 1]
                    ai = pw[:, 2 * m + 1:2 * m + 2]
                    ya = y_ct[:, 0:w]
                    yb = y_ct[:, 1024:1024 + w]
                    v.tensor_scalar_mul(yb, pti[:, 0:w], ar)
                    v.tensor_scalar_mul(ya, pti[:, 0:w], ai)
                    v.scalar_tensor_tensor(pti[:, w:2 * w], ptr[:, 0:w], ai,
                                           yb, op0=ALU.mult, op1=ALU.add)
                    v.scalar_tensor_tensor(ptr[:, w:2 * w], ptr[:, 0:w], ar,
                                           ya, op0=ALU.mult,
                                           op1=ALU.subtract)
                v.tensor_scalar_mul(y_ct[:, 0:T], ptr[:, 0:T],
                                    rdat[:, 2 * ct:2 * ct + 1])
                v.scalar_tensor_tensor(y_ct[:, 0:T], pti[:, 0:T],
                                       nrd[:, 2 * ct + 1:2 * ct + 2],
                                       y_ct[:, 0:T], op0=ALU.mult, op1=ALU.add)
                bb.inc(v.scalar_tensor_tensor(
                    h_t[ct], y_ct[:, 0:T], pcol(f"scl_{l}_{ct}"), h_t[ct],
                    op0=ALU.mult, op1=ALU.add),
                    "sV", 1, ("hcor", l, ct))
            on("vector", f_corr)

        def emit_attn(pre_bar, stopat=None):
            emit_ln("lnA", resid_s, pre_bar=pre_bar)

            def f_winstream(g):
                bb.bar(g, pre_bar)
                for half in range(2):
                    for k in range(DT):
                        i = half * DT + k
                        if i >= 2:
                            bb.wait(g, "sPE", ("winmm", i - 2))
                        bb.inc(g.dma_start((sil0, sil1)[i % 2],
                                           win_t[k, :, :]),
                               ("sWa", "sWb")[i % 2], 16, ("winld", i))
            on("gpsimd", f_winstream)

            bcap_win = cap_banks()

            def f_win(t):
                bb.wait(t, "sV", ("lnA", "done"))
                for half in range(2):
                    if half == 0:
                        wait_banks(t, bcap_win)
                    else:
                        bb.wait(t, "sS", ("winev", 0, 7))
                    for k in range(DT):
                        i = half * DT + k
                        bb.wait(t, ("sWa", "sWb")[i % 2], ("winld", i))
                        wt = (sil0, sil1)[i % 2]
                        for gi in range(8):
                            gidx = half * 8 + gi
                            dt4, tc = gidx // TC, gidx % TC
                            li = mm(t, psum[gi][:, :],
                                    wt[:, dt4 * 128:(dt4 + 1) * 128],
                                    ybf[:, k * T + tc * 512:
                                        k * T + (tc + 1) * 512],
                                    k == 0, k == DT - 1)
                        bb.inc(li, "sPE", 1, ("winmm", i))
                    bb.mark("sPE", ("win", half))
            on("tensor", f_win)

            def f_winev(sc):
                for half in range(2):
                    bb.wait(sc, "sPE", ("win", half))
                    for gi in range(8):
                        gidx = half * 8 + gi
                        dt4, tc = gidx // TC, gidx % TC
                        bb.inc(sc.activation(
                            h_t[dt4][:, tc * 512:(tc + 1) * 512],
                            psum[gi][:, :], AF.Identity, bias=pcol(f"bin{dt4}")),
                            "sS", 1, ("winev", half, gi))
            on("scalar", f_winev)
            set_bank_free(range(8), "sS", ("winev", 1, 7))

            def f_hrdy(v):
                bb.wait(v, "sS", ("winev", 1, 7))
                bb.inc(v.nop(), "sV", 1, ("hrdy", 0))
            on("vector", f_hrdy)

            def dump_h(gatekey):
                def f_d(g):
                    bb.wait(g, "sV", gatekey)
                    for dt in range(ADT):
                        bb.inc(g.dma_start(dbg[dt, :, :], h_t[dt]), "sDB", 16)
                    bb.mark("sDB", "fin_sO")
                    bb.wait(g, "sDB", "fin_sO")
                on("gpsimd", f_d)

            if stopat in ("y00", "in00", "tb00"):
                pass
            if stopat == "win":
                def f_dw(g):
                    bb.wait(g, "sV", ("hrdy", 0))
                    for dt in range(ADT):
                        bb.inc(g.dma_start(dbg[dt, :, :], h_t[dt]), "sDB", 16)
                    bb.mark("sDB", "fin_sO")
                    bb.wait(g, "sDB", "fin_sO")
                on("gpsimd", f_dw)
                return

            for l in range(DEPTH):
                t1 = ("aln", l, 1)
                emit_attn_ln(t1, lambda dt, l=l: ("hrdy", l))

                for ct in range(ADT):
                    def f_vn(v, l=l, ct=ct, t1=t1):
                        if ct == 0:
                            bb.wait(v, "sPE", (t1, "bcast"))
                        for tc in range(TC):
                            i = ct * TC + tc
                            ra, rb = ((rs_[0], rs_[1]) if i % 2 == 0
                                      else (rs_[2], rs_[3]))
                            if i >= 2:
                                bb.wait(v, "sS", (t1, "yout", i - 2))
                            s5 = slice(tc * 512, (tc + 1) * 512)
                            v.tensor_sub(ra[:, :], h_t[ct][:, s5],
                                         psum[4 + tc][:, :])
                            bb.inc(v.tensor_mul(rb[:, :], ra[:, :],
                                                psum[tc][:, :]),
                                   "sV", 1, (t1, "nt", i))
                    on("vector", f_vn)

                    def f_ys(sc, l=l, ct=ct, t1=t1):
                        if ct >= 1:
                            bb.wait(sc, "sV", (("sc", l, ct - 1), "intra"))
                        elif l >= 1:
                            bb.wait(sc, "sV", (("sc", l - 1, ADT - 1), "intra"))
                        for tc in range(TC):
                            i = ct * TC + tc
                            rb = rs_[1] if i % 2 == 0 else rs_[3]
                            bb.wait(sc, "sV", (t1, "nt", i))
                            bb.inc(sc.activation(
                                y_ct[:, tc * 512:(tc + 1) * 512], rb[:, :],
                                AF.Silu, scale=pcol(f"ln1w_{l}_{ct}"),
                                bias=pcol(f"ln1b_{l}_{ct}")),
                                "sS", 1, (t1, "yout", i))
                        bb.mark("sS", ("y", l, ct))
                    on("scalar", f_ys)

                    emit_scan(l, ct)
                    if l == 0 and ct == 0 and stopat in ("y00", "in00", "tb00"):
                        key = {"y00": None, "in00": (("sc", 0, 0), "intra"),
                               "tb00": (("sc", 0, 0), "tables")}[stopat]

                        def f_md(g, key=key, stopat=stopat):
                            if stopat == "y00":
                                bb.wait(g, "sS", ("y", 0, 0))
                                bb.inc(g.dma_start(dbg[0, :, :], y_ct[:, 0:T]),
                                       "sDB", 16)
                            elif stopat == "tb00":
                                bb.wait(g, "sV", key)
                                bb.inc(g.dma_start(
                                    dbg[0, :, 0:CH],
                                    prgs_r[:, 0:CH]), "sDB", 16)
                                bb.inc(g.dma_start(
                                    dbg[0, :, CH:2 * CH],
                                    prgs_i[:, 0:CH]), "sDB", 16)
                                bb.inc(g.dma_start(
                                    dbg[0, :, 2 * CH:2 * CH + 128],
                                    pwt[:, 0:128]), "sDB", 16)
                            else:
                                bb.wait(g, "sV", key)
                                bb.inc(g.dma_start(dbg[0, :, :], LAr[:, 0:T]),
                                       "sDB", 16)
                                bb.inc(g.dma_start(dbg[1, :, :], LAi[:, 0:T]),
                                       "sDB", 16)
                                bb.inc(g.dma_start(dbg[2, :, 0:NCH + 1],
                                                   e0r[:, :]), "sDB", 16)
                                bb.inc(g.dma_start(dbg[2, :, 32:32 + NCH + 1],
                                                   e0i[:, :]), "sDB", 16)
                            bb.mark("sDB", "fin_sO")
                            bb.wait(g, "sDB", "fin_sO")
                        on("gpsimd", f_md)
                        return
                set_bank_free(range(8), "sV", (t1, "nt", ADT * TC - 1))

                sp = f"sP{l}"

                def f_ex(g, l=l, sp=sp):
                    bb.wait(g, "sV", (("sc", l, ADT - 1), "escan"))
                    if bb.mode == "emit":
                        with (g.register(f"fev{l}") as fev,
                              g.register(f"cnt{l}") as cnt,
                              g.register(f"fl{l}") as fl,
                              g.register(f"nd{l}") as nd):
                            g.reg_load(fev, rflag[0:1, 0:1])
                            g.reg_mov(cnt, 0)
                            with g.If_ne(fev, 0):
                                g.dma_start(xch[l, :, :],
                                            sx[:, :]).then_inc(bb.h[sp], 16)
                                g.reg_add(cnt, cnt, 16)
                                g.wait_ge(bb.h[sp], cnt)
                                g.dma_start(xfl[l:l + 1, 0:1],
                                            magict[:, :]).then_inc(bb.h[sp], 16)
                                g.reg_add(cnt, cnt, 16)
                                g.wait_ge(bb.h[sp], cnt)
                            with g.Else():
                                g.reg_mov(nd, 1)
                                with g.While(nd):
                                    g.dma_start(pollt[:, :],
                                                xfl[l:l + 1, 0:1]).then_inc(
                                        bb.h[sp], 16)
                                    g.reg_add(cnt, cnt, 16)
                                    g.wait_ge(bb.h[sp], cnt)
                                    g.reg_load(fl, pollt[0:1, 0:1])
                                    g.reg_mov(nd, MAGIC)
                                    g.reg_sub(nd, nd, fl)
                                g.dma_start(rdat[:, :],
                                            xch[l, :, :]).then_inc(bb.h[sp], 16)
                                g.reg_add(cnt, cnt, 16)
                                g.wait_ge(bb.h[sp], cnt)
                                g.dma_start(xfl[l:l + 1, 0:1],
                                            zerot[:, :]).then_inc(bb.h[sp], 16)
                                g.reg_add(cnt, cnt, 16)
                                g.wait_ge(bb.h[sp], cnt)
                    bb.inc(g.nop(), "sEx", 1, ("exr", l))
                on("gpsimd", f_ex)

                for ct in range(ADT):
                    emit_scan_corr(l, ct)

                t2 = ("aln", l, 2)
                t2 = ("aln", l, 2)

                def f_h2rdy(v, l=l):
                    bb.wait(v, "sV", ("hcor", l, ADT - 1))
                    bb.inc(v.nop(), "sV", 1, ("h2rdy", l))
                on("vector", f_h2rdy)
                emit_attn_ln(t2, lambda dt, l=l: ("h2rdy", l))
                emit_attn_norm_out(t2, f"ln2w_{l}_", f"ln2b_{l}_", AF.Identity,
                                   lambda dt: y2bf[dt])

                def f_w12(g, l=l):
                    if l == 0:
                        bb.wait(g, "sPE", ("win", 1))
                    else:
                        bb.wait(g, "sPE", (("ffnin", l - 1), "f2done"))
                    for k in range(4):
                        bb.inc(g.dma_start(w1bf[:, k * 2048:(k + 1) * 2048],
                                           w1t[l, k, :, :]), "sW12", 16)
                    for k in range(16):
                        bb.inc(g.dma_start(w2bf[:, k * 512:(k + 1) * 512],
                                           w2t[l, k, :, :]), "sW12", 16,
                               ("w12", l) if k == 15 else None)
                on("gpsimd", f_w12)

                tf = ("ffnin", l)
                bcap_f = cap_banks()

                def f_ffn_pe(t, l=l, tf=tf, t2=t2, bcap_f=bcap_f):
                    bb.wait(t, "sW12", ("w12", l))
                    bb.wait(t, "sS", (t2, "outdone"))
                    wait_banks(t, bcap_f)
                    for tc in range(TC):
                        if tc >= 1:
                            bb.wait(t, "sV", (tf, "f2ev", tc - 1))
                        for ft in range(FT):
                            fb = psum[4 + ft % 2]
                            if ft >= 2:
                                bb.wait(t, "sS", (tf, "sil", tc, ft - 2))
                            for k in range(4):
                                li = mm(t, fb[:, :],
                                        w1bf[:, k * 2048 + ft * 128:
                                             k * 2048 + (ft + 1) * 128],
                                        y2bf[k][:, tc * 512:(tc + 1) * 512],
                                        k == 0, k == 3)
                            bb.inc(li, "sPE", 1, (tf, "f1", tc, ft))
                            if ft >= 1:
                                fp = ft - 1
                                silb = (sil0, sil1)[fp % 2]
                                bb.wait(t, "sS", (tf, "sil", tc, fp))
                                for dt in range(ADT):
                                    mm(t, psum[dt][:, :],
                                       w2bf[:, fp * 512 + dt * 128:
                                            fp * 512 + (dt + 1) * 128],
                                       silb[:, :], fp == 0, False)
                        fp = FT - 1
                        silb = (sil0, sil1)[fp % 2]
                        bb.wait(t, "sS", (tf, "sil", tc, fp))
                        for dt in range(ADT):
                            li = mm(t, psum[dt][:, :],
                                    w2bf[:, fp * 512 + dt * 128:
                                         fp * 512 + (dt + 1) * 128],
                                    silb[:, :], False, True)
                        bb.inc(li, "sPE", 1, (tf, "f2", tc))
                    bb.mark("sPE", (tf, "f2done"))
                on("tensor", f_ffn_pe)

                def f_ffn_sil(sc, l=l, tf=tf):
                    for tc in range(TC):
                        for ft in range(FT):
                            silb = (sil0, sil1)[ft % 2]
                            bb.wait(sc, "sPE", (tf, "f1", tc, ft))
                            bb.inc(sc.activation(silb[:, :],
                                                 psum[4 + ft % 2][:, :], AF.Silu,
                                                 bias=pcol(f"b1_{l}_{ft}")),
                                   "sS", 1, (tf, "sil", tc, ft))
                on("scalar", f_ffn_sil)

                def f_ffn_ev(v, l=l, tf=tf):
                    for tc in range(TC):
                        bb.wait(v, "sPE", (tf, "f2", tc))
                        for dt in range(ADT):
                            bb.inc(v.scalar_tensor_tensor(
                                h_t[dt][:, tc * 512:(tc + 1) * 512],
                                psum[dt][:, :], pcol(f"b2_{l}_{dt}"),
                                h_t[dt][:, tc * 512:(tc + 1) * 512],
                                op0=ALU.add, op1=ALU.add),
                                "sV", 1,
                                (tf, "f2ev", tc) if dt == ADT - 1 else None)
                on("vector", f_ffn_ev)
                set_bank_free(range(8), "sV", (tf, "f2ev", TC - 1))

                def f_h3rdy(v, l=l):
                    bb.wait(v, "sV", (("ffnin", l), "f2ev", TC - 1))
                    bb.inc(v.nop(), "sV", 1, ("hrdy", l + 1))
                on("vector", f_h3rdy)

                if stopat == f"l{l}":
                    dump_h(("hrdy", l + 1))
                    return
                if stopat == f"s{l}":
                    dump_h(("hcor", l, ADT - 1))
                    return

            tl = ("aln", "lnl", 0)
            emit_attn_ln(tl, lambda dt: ("hrdy", DEPTH))
            emit_attn_norm_out(tl, "lnlw", "lnlb", AF.Identity, lambda dt: y2bf[dt])

            def f_woutld(g):
                bb.wait(g, "sPE", ("win", 1))
                for k in range(4):
                    bb.inc(g.dma_start(woutbf[:, k * 4096:(k + 1) * 4096],
                                       wout_t[k, :, :]), "sW12", 16,
                           ("woutld",) if k == 3 else None)
            on("gpsimd", f_woutld)

            def f_wout_sync(s):
                prevuse = {}
                for vt in range(DT):
                    for tc in range(TC):
                        m4 = (vt * TC + tc) % 4
                        pk = prevuse.get(m4)
                        if pk is not None:
                            bb.wait(s, "sV", pk[0])
                            bb.wait(s, f"sST{m4}", pk[1])
                        else:
                            bb.wait(s, "sS", (tl, "outdone"))
                        bb.inc(s.dma_start(
                            rs_[m4][:, :],
                            resid_s[vt, :, tc * 512:(tc + 1) * 512]),
                            f"sR{m4}", 16, ("wo_rld", vt, tc))
                        prevuse[m4] = (("wo_ev", vt, tc), ("wo_st", vt, tc))
            on("sync", f_wout_sync)

            bcap_wo = cap_banks()

            def f_wout_pe(t, bcap_wo=bcap_wo):
                bb.wait(t, "sW12", ("woutld",))
                bb.wait(t, "sS", (tl, "outdone"))
                gi = 0
                for vt in range(DT):
                    for tc in range(TC):
                        b_ = gi % 8
                        if gi >= 8:
                            bb.wait(t, "sV", ("wo_ev_g", gi - 8))
                        elif gi == 0:
                            wait_banks(t, bcap_wo)
                        for k in range(4):
                            li = mm(t, psum[b_][:, :],
                                    woutbf[:, k * 4096 + vt * 128:
                                           k * 4096 + (vt + 1) * 128],
                                    y2bf[k][:, tc * 512:(tc + 1) * 512],
                                    k == 0, k == 3)
                        bb.inc(li, "sPE", 1, ("wo_pe", gi))
                        gi += 1
            on("tensor", f_wout_pe)

            def f_wout_ev(v):
                gi = 0
                for vt in range(DT):
                    for tc in range(TC):
                        b_ = gi % 8
                        m4 = gi % 4
                        bb.wait(v, "sPE", ("wo_pe", gi))
                        bb.wait(v, f"sR{gi % 4}", ("wo_rld", vt, tc))
                        bb.inc(v.scalar_tensor_tensor(
                            rs_[m4][:, :], psum[b_][:, :], pcol(f"bout{vt}"),
                            rs_[m4][:, :], op0=ALU.add, op1=ALU.add),
                            "sV", 1, ("wo_ev", vt, tc))
                        bb.mark("sV", ("wo_ev_g", gi))
                        gi += 1
            on("vector", f_wout_ev)

            def f_wout_st(sc):
                for vt in range(DT):
                    for tc in range(TC):
                        bb.wait(sc, "sV", ("wo_ev", vt, tc))
                        bb.inc(sc.dma_start(
                            resid_s[vt, :, tc * 512:(tc + 1) * 512],
                            rs_[(vt * TC + tc) % 4][:, :]),
                            f"sST{(vt * TC + tc) % 4}", 16, ("wo_st", vt, tc))
            on("scalar", f_wout_st)
            set_bank_free(range(8), "sV", ("wo_ev", DT - 1, TC - 1))

        emit_ln("ln1", resid_in)
        if stop_after == "ln":
            def f_snap(e):
                bb.snap(("bar", "x"))
            on("sync", f_snap)

            def f_dumpln(g):
                bb.wait(g, "sV", ("ln1", "done"))
                for dt in range(DT):
                    bb.inc(g.dma_start(dbg[dt, :, :], ybf[:, dt * T:(dt + 1) * T]),
                           "sDB", 16)
                bb.mark("sDB", "fin_sO")
                bb.wait(g, "sDB", "fin_sO")
            on("gpsimd", f_dumpln)
        else:
            emit_gateup("gu1", 1, "ln1")

            def f_snapgu1(e):
                bb.snap(("bar", "gu1"))
            on("sync", f_snapgu1)
            if stop_after == "gu":
                def f_dumpgu(g):
                    bb.bar(g, ("bar", "gu1"))
                    for tc in range(TC):
                        bb.inc(g.dma_start(
                            dbg[tc, :, :, :].rearrange("p k j -> p (k j)"),
                            at_s[tc, :, :, :].rearrange("p k j -> p (k j)")),
                            "sDB", 16)
                    bb.mark("sDB", "fin_sO")
                    bb.wait(g, "sDB", "fin_sO")
                on("gpsimd", f_dumpgu)
            else:
                emit_down("dn1", 1, resid_in, resid_s, ("bar", "gu1"), "sY")

                def f_snapffn1(e):
                    bb.snap(("bar", "ffn1"))
                on("sync", f_snapffn1)
                if stop_after == "ffn1":
                    def f_dumpf1(g):
                        bb.bar(g, ("bar", "ffn1"))
                        for dt in range(DT):
                            bb.inc(g.dma_start(dbg[dt, :, :], resid_s[dt, :, :]),
                                   "sDB", 16)
                        bb.mark("sDB", "fin_sO")
                        bb.wait(g, "sDB", "fin_sO")
                    on("gpsimd", f_dumpf1)
                else:
                    if stop_after in ("win", "l0", "l1", "l2", "l3",
                                      "s0", "s1", "s2", "s3",
                                      "y00", "in00", "tb00"):
                        emit_attn(("bar", "ffn1"), stopat=stop_after)
                        bb.run_plan()
                        for nm in BB.SEMS:
                            bb.h[nm] = es.enter_context(nc.semaphore(nm))

                        @blk.sync
                        def _(s):
                            bb.run_emit("sync", s)

                        @blk.scalar
                        def _(sc):
                            bb.run_emit("scalar", sc)

                        @blk.vector
                        def _(v):
                            bb.run_emit("vector", v)

                        @blk.tensor
                        def _(t):
                            bb.run_emit("tensor", t)

                        @blk.gpsimd
                        def _(g):
                            bb.run_emit("gpsimd", g)
                        return nc
                    emit_attn(("bar", "ffn1"))

                    def f_snapattn(e):
                        bb.snap(("bar", "attn"))
                    on("sync", f_snapattn)
                    if stop_after == "attn":
                        def f_dumpat(g):
                            bb.bar(g, ("bar", "attn"))
                            for dt in range(DT):
                                bb.inc(g.dma_start(dbg[dt, :, :],
                                                   resid_s[dt, :, :]), "sDB", 16)
                            bb.mark("sDB", "fin_sO")
                            bb.wait(g, "sDB", "fin_sO")
                        on("gpsimd", f_dumpat)
                    else:
                        emit_ln("lnF", resid_s, pre_bar=("bar", "attn"))
                        emit_gateup("gu2", 2, "lnF", pre_bar=("bar", "attn"))

                        def f_snapgu2(e):
                            bb.snap(("bar", "gu2"))
                        on("sync", f_snapgu2)
                        emit_down("dn2", 2, resid_s, resid_out,
                                  ("bar", "gu2"), "sO")

                        def f_fin(g):
                            bb.snap(("bar", "fin"))
                            bb.bar(g, ("bar", "fin"))
                        on("gpsimd", f_fin)

        bb.run_plan()
        for nm in BB.SEMS:
            bb.h[nm] = es.enter_context(nc.semaphore(nm))

        @blk.sync
        def _(s):
            bb.run_emit("sync", s)

        @blk.scalar
        def _(sc):
            bb.run_emit("scalar", sc)

        @blk.vector
        def _(v):
            bb.run_emit("vector", v)

        @blk.tensor
        def _(t):
            bb.run_emit("tensor", t)

        @blk.gpsimd
        def _(g):
            bb.run_emit("gpsimd", g)

    return nc


def _in_maps(staged):
    maps = []
    for c in range(8):
        m = {k: staged[k] for k in
             ("wg8", "wu8", "wd8", "gsr", "usr", "dsr", "win_t", "w1t", "w2t",
              "wout_t", "params")}
        m["resid_in"] = staged["_resid_shards"][c]
        ev = 1 - (c % 2)
        m["evenflag"] = np.array([[ev]], np.int32)
        em = np.zeros((128, 2), np.float32)
        em[:, 0] = ev
        em[:, 1] = 1 - ev
        m["emaskin"] = em
        maps.append(m)
    return maps


def kernel(**inputs):
    staged = _stage_inputs(inputs)
    nc = build_program()
    res = run_bass_kernel_spmd(nc, _in_maps(staged), core_ids=list(range(8)))
    out = np.empty((4, 4096, 4096), np.float32)
    for c in range(8):
        b, h = c // 2, c % 2
        sh = np.asarray(res.results[c]["resid_out"]).reshape(4096, T)
        out[b, h * T:(h + 1) * T, :] = sh.T
    return out


# version 03
# speedup vs baseline: 1.0752x; 1.0752x over previous
import numpy as np
import ml_dtypes
from contextlib import ExitStack
import concourse.bass as bass
import concourse.mybir as mybir
from concourse.bass_utils import run_bass_kernel_spmd

F32 = mybir.dt.float32
BF16 = mybir.dt.bfloat16
U8 = mybir.dt.uint8
I32 = mybir.dt.int32
AF = mybir.ActivationFunctionType
ALU = mybir.AluOpType

T = 2048
TC = 4
DT = 32
IT = 86
DIM = 512
ADT = 4
FT = 16
DEPTH = 4
CH = 128
NCH = 16
EPS = 1e-5
MAGIC = 1234567

_P = {}


def _build_pmap():
    idx = [0]

    def add(name):
        _P[name] = idx[0]
        idx[0] += 1

    def addvec(name, ntile):
        for i in range(ntile):
            add(f"{name}{i}")

    _P.clear()
    addvec("bin", 4)
    for l in range(4):
        addvec(f"ln1w_{l}_", 4)
        addvec(f"ln1b_{l}_", 4)
        addvec(f"ln2w_{l}_", 4)
        addvec(f"ln2b_{l}_", 4)
        addvec(f"phr_{l}_", 4)
        addvec(f"phi_{l}_", 4)
        addvec(f"lcr_{l}_", 4)
        addvec(f"lci_{l}_", 4)
        addvec(f"scl_{l}_", 4)
        addvec(f"b1_{l}_", 16)
        addvec(f"b2_{l}_", 4)
    addvec("lnlw", 4)
    addvec("lnlb", 4)
    addvec("bout", 32)
    add("c_neghalf")
    add("c_half")


_build_pmap()



def _stage_inputs(inp):
    out = {}
    x = np.ascontiguousarray(np.asarray(inp["L0_attn_x"], np.float32))

    def lowbyte(a):
        a = np.ascontiguousarray(np.asarray(a, np.int32))
        return np.ascontiguousarray(a.view(np.uint8)[:, ::4])

    gq = lowbyte(inp["L0_ffn_gate_q"])
    uq = lowbyte(inp["L0_ffn_up_q"])
    dq = lowbyte(inp["L0_ffn_down_q"])
    out["wg8"] = np.ascontiguousarray(
        gq.reshape(32, 128, 86, 128).transpose(2, 1, 0, 3)).reshape(86, 128, 4096)
    out["wu8"] = np.ascontiguousarray(
        uq.reshape(32, 128, 86, 128).transpose(2, 1, 0, 3)).reshape(86, 128, 4096)
    out["wd8"] = np.ascontiguousarray(dq.reshape(86, 128, 4096))
    gs = np.asarray(inp["L0_ffn_gate_s"], np.float32)
    us = np.asarray(inp["L0_ffn_up_s"], np.float32)
    out["gsr"] = np.ascontiguousarray(
        gs.reshape(32, 86, 128).transpose(1, 0, 2)).reshape(86, 4096)
    out["usr"] = np.ascontiguousarray(
        us.reshape(32, 86, 128).transpose(1, 0, 2)).reshape(86, 4096)
    out["dsr"] = np.ascontiguousarray(np.asarray(inp["L0_ffn_down_s"], np.float32))
    out["win_t"] = np.ascontiguousarray(
        np.asarray(inp["L0_attn_Win"], np.float32).T).reshape(32, 128, 512)
    out["w1t"] = np.ascontiguousarray(
        np.transpose(np.asarray(inp["L0_attn_W1"], np.float32), (0, 2, 1))
    ).reshape(4, 4, 128, 2048)
    out["w2t"] = np.ascontiguousarray(
        np.transpose(np.asarray(inp["L0_attn_W2"], np.float32), (0, 2, 1))
    ).reshape(4, 16, 128, DIM)
    out["wout_t"] = np.ascontiguousarray(
        np.asarray(inp["L0_attn_Wout"], np.float32).T).reshape(4, 128, 4096)
    out["params"] = _pack_params(inp)
    shards = []
    for c in range(8):
        b, h = c // 2, c % 2
        xt = np.ascontiguousarray(x[b, h * T:(h + 1) * T, :].T)
        shards.append(np.ascontiguousarray(xt.reshape(32, 128, T)))
    out["_resid_shards"] = shards
    return out


def _pack_params(inp):
    arr = np.zeros((128, 288), np.float32)

    def put(name, vec128):
        arr[:, _P[name]] = np.asarray(vec128, np.float32).reshape(128)

    def putvec(name, v, ntile):
        v = np.asarray(v, np.float32).reshape(ntile, 128)
        for i in range(ntile):
            put(f"{name}{i}", v[i])

    putvec("bin", inp["L0_attn_bin"], 4)
    for l in range(DEPTH):
        putvec(f"ln1w_{l}_", inp["L0_attn_ln1_w"][l], 4)
        putvec(f"ln1b_{l}_", inp["L0_attn_ln1_b"][l], 4)
        putvec(f"ln2w_{l}_", inp["L0_attn_ln2_w"][l], 4)
        putvec(f"ln2b_{l}_", inp["L0_attn_ln2_b"][l], 4)
        putvec(f"phr_{l}_", inp["L0_attn_phz_re"][l], 4)
        putvec(f"phi_{l}_", inp["L0_attn_phz_im"][l], 4)
        putvec(f"lcr_{l}_", inp["L0_attn_lc_re"][l], 4)
        putvec(f"lci_{l}_", inp["L0_attn_lc_im"][l], 4)
        putvec(f"scl_{l}_", inp["L0_attn_sc_lin"][l], 4)
        putvec(f"b1_{l}_", inp["L0_attn_b1"][l], 16)
        putvec(f"b2_{l}_", inp["L0_attn_b2"][l], 4)
    putvec("lnlw", inp["L0_attn_lnl_w"], 4)
    putvec("lnlb", inp["L0_attn_lnl_b"], 4)
    putvec("bout", inp["L0_attn_bout"], 32)
    put("c_neghalf", np.full(128, -0.5))
    put("c_half", np.full(128, 0.5))
    return arr



class _Any:
    def __call__(self, *a, **k):
        return _Any()

    def __getattr__(self, nm):
        return _Any()

    def __enter__(self):
        return _Any()

    def __exit__(self, *a):
        return False


class BB:
    SEMS = ("sWa", "sWb", "sBg", "sBu", "sX", "sXa", "sXb", "sYa", "sYb",
            "sAa", "sAb", "sLB", "sLC", "sWc", "sWd", "sYc", "sYd",
            "sLA", "sLN0", "sLN1", "sW12", "sDB",
            "sR0", "sR1", "sR2", "sR3", "sST0", "sST1", "sST2", "sST3",
            "sPE", "sV", "sS", "sEx", "sP0", "sP1", "sP2", "sP3")

    def __init__(self):
        self.n = {s: 0 for s in self.SEMS}
        self.h = {}
        self.L = {}
        self.seq = []
        self.mode = "plan"

    def on(self, eng, f):
        self.seq.append((eng, f))

    def inc(self, inst, nm, amt, key=None):
        if self.mode == "plan":
            self.n[nm] += amt
            if key is not None:
                self.L[key] = self.n[nm]
        else:
            inst.then_inc(self.h[nm], amt)

    def mark(self, nm, key):
        if self.mode == "plan":
            self.L[key] = self.n[nm]

    def wait(self, e, nm, key):
        if self.mode == "emit":
            v = key if isinstance(key, int) else self.L[key]
            if v > 0:
                e.wait_ge(self.h[nm], v)

    def snap(self, key):
        if self.mode == "plan":
            self.L[key] = dict(self.n)

    def bar(self, e, key):
        if self.mode == "emit":
            for nm, v in self.L[key].items():
                if v > 0:
                    e.wait_ge(self.h[nm], v)

    def run_plan(self):
        self.mode = "plan"
        fake = _Any()
        for eng, f in self.seq:
            f(fake)

    def run_emit(self, eng, e):
        self.mode = "emit"
        for en, f in self.seq:
            if en == eng:
                f(e)


def build_program(stop_after="full", debug_dump=None, det=False):
    nc = bass.Bass("TRN2", target_bir_lowering=False,
                   detect_race_conditions=det)
    NP = 288

    resid_in = nc.dram_tensor("resid_in", [DT, 128, T], F32, kind="ExternalInput").ap()
    wg8 = nc.dram_tensor("wg8", [IT, 128, 4096], U8, kind="ExternalInput").ap()
    wu8 = nc.dram_tensor("wu8", [IT, 128, 4096], U8, kind="ExternalInput").ap()
    wd8 = nc.dram_tensor("wd8", [IT, 128, 4096], U8, kind="ExternalInput").ap()
    gsr = nc.dram_tensor("gsr", [IT, 4096], F32, kind="ExternalInput").ap()
    usr = nc.dram_tensor("usr", [IT, 4096], F32, kind="ExternalInput").ap()
    dsr = nc.dram_tensor("dsr", [IT, 4096], F32, kind="ExternalInput").ap()
    win_t = nc.dram_tensor("win_t", [DT, 128, DIM], F32, kind="ExternalInput").ap()
    w1t = nc.dram_tensor("w1t", [DEPTH, 4, 128, 2048], F32, kind="ExternalInput").ap()
    w2t = nc.dram_tensor("w2t", [DEPTH, 16, 128, DIM], F32, kind="ExternalInput").ap()
    wout_t = nc.dram_tensor("wout_t", [DEPTH, 128, 4096], F32,
                            kind="ExternalInput").ap()
    params = nc.dram_tensor("params", [128, NP], F32, kind="ExternalInput").ap()
    evenflag = nc.dram_tensor("evenflag", [1, 1], I32, kind="ExternalInput").ap()
    emaskin = nc.dram_tensor("emaskin", [128, 2], F32, kind="ExternalInput").ap()

    resid_out = nc.dram_tensor("resid_out", [DT, 128, T], F32,
                               kind="ExternalOutput").ap()

    wgbf = nc.dram_tensor("wgbf", [IT, 128, 4096], BF16, kind="Internal").ap()
    wubf = nc.dram_tensor("wubf", [IT, 128, 4096], BF16, kind="Internal").ap()
    wdbf = nc.dram_tensor("wdbf", [IT, 128, 4096], BF16, kind="Internal").ap()
    at_s = nc.dram_tensor("at_s", [TC, 128, IT, 512], BF16, kind="Internal").ap()
    resid_s = nc.dram_tensor("resid_s", [DT, 128, T], F32, kind="Internal").ap()
    xch = nc.dram_tensor("xch", [DEPTH, 128, 8], F32, kind="Internal",
                         addr_space="Shared").ap()
    xfl = nc.dram_tensor("xfl", [DEPTH, 1], I32, kind="Internal",
                         addr_space="Shared").ap()
    dbg = None
    if debug_dump:
        shp = {"ybf": ([DT, 128, T], BF16), "at": ([TC, 128, IT, 512], BF16),
               "resid": ([DT, 128, T], F32), "h": ([ADT, 128, T], F32),
               "yct": ([1, 128, T], F32)}[debug_dump]
        dbg = nc.dram_tensor("dbg", shp[0], shp[1], kind="ExternalOutput").ap()

    bb = BB()
    L = bb.L
    on = bb.on
    es = ExitStack()
    with es:
        def sbt(name, shape, dt):
            return es.enter_context(nc.sbuf_tensor(name, shape, dt))

        par = sbt("par", [128, NP], F32)
        ones_k = sbt("ones_k", [128, 1], BF16)
        ones_b = sbt("ones_b", [1, 128], F32)
        rowsA = sbt("rowsA", [128, T], F32)
        rowsB = sbt("rowsB", [128, T], F32)
        r_rstd = rowsA[0:1, :]
        r_sq = rowsA[32:33, :]
        r_mean = rowsB[0:1, :]
        r_tmp = rowsB[32:33, :]
        magict = sbt("magict", [1, 1], I32)
        zerot = sbt("zerot", [1, 1], I32)
        rflag = sbt("rflag", [1, 1], I32)
        pollt = sbt("pollt", [1, 1], I32)
        emask = sbt("emask", [128, 2], F32)
        sx = sbt("sx", [128, 8], F32)
        rdat = sbt("rdat", [128, 8], F32)
        pwt = sbt("pwt", [128, 128], F32)
        prgs_r = sbt("prgs_r", [128, 512], F32)
        prgs_i = sbt("prgs_i", [128, 512], F32)
        nrd = sbt("nrd", [128, 8], F32)
        e0r = sbt("e0r", [128, NCH + 1], F32)
        e0i = sbt("e0i", [128, NCH + 1], F32)
        e1r = sbt("e1r", [128, NCH + 1], F32)
        e1i = sbt("e1i", [128, NCH + 1], F32)
        esn = sbt("esn", [128, NCH + 1], F32)
        s64a = sbt("s64a", [128, 64], F32)
        s64b = sbt("s64b", [128, 64], F32)

        ybf = sbt("ybf", [128, DT * T], BF16)
        bufA = sbt("bufA", [128, 4096], BF16)
        bufB = sbt("bufB", [128, 4096], BF16)
        bufC = sbt("bufC", [128, 4096], BF16)
        bufD = sbt("bufD", [128, 4096], BF16)
        sb_g = sbt("sb_g", [128, 2048], BF16)
        sb_u = sbt("sb_u", [128, 2048], BF16)
        sq0 = sbt("sq0", [128, T], BF16)
        sq1 = sbt("sq1", [128, T], BF16)
        rs_ = [sbt(f"rs{i}", [128, 512], F32) for i in range(4)]
        at0, at1 = sq0, sq1
        sil0 = rs_[3][:, :].bitcast(BF16)[:, 0:512]
        sil1 = rs_[3][:, :].bitcast(BF16)[:, 512:1024]

        psum = [es.enter_context(nc.psum_tensor(f"ps{i}", [128, 512], F32))
                for i in range(8)]

        def yb(c0, n):
            return ybf[:, c0:c0 + n]

        h_t = [b_[:, :].bitcast(F32) for b_ in (bufA, bufB, bufC, bufD)]
        y2bf = [yb(i * T, T) for i in range(ADT)]
        LAr = yb(8192, 4096).bitcast(F32)
        LBr = yb(12288, 4096).bitcast(F32)
        LAi = yb(16384, 4096).bitcast(F32)
        LBi = yb(20480, 4096).bitcast(F32)
        y_ct = yb(24576, 4096).bitcast(F32)
        w1bf = yb(28672, 8192)
        w2bf = yb(36864, 8192)
        woutbf = yb(45056, 16384)
        hbf = yb(61440, 2048)
        hbf2 = yb(63488, 2048)
        at_res = ybf[:, 0:IT * 512]


        blk = es.enter_context(nc.Block())

        def pcol(name):
            return par[:, _P[name]:_P[name] + 1]

        def pcol1(name):
            return par[0:1, _P[name]:_P[name] + 1]

        def mm(t, out, lhsT, rhs, start, stop):
            return t.matmul(out, lhsT, rhs, start=start, stop=stop,
                            skip_group_check=True)

        bank_free = [("sV", "p0_const")] * 8

        def cap_banks(idxs=range(8)):
            return [tuple(bank_free[b_]) for b_ in idxs]

        def wait_banks(t, cap):
            if bb.mode == "emit":
                seen = set()
                for nm, key in cap:
                    v = key if isinstance(key, int) else L.get(key, 0)
                    if v > 0 and (nm, v) not in seen:
                        seen.add((nm, v))
                        t.wait_ge(bb.h[nm], v)

        def set_bank_free(idxs, nm, key):
            for b_ in idxs:
                bank_free[b_] = (nm, key)

        def p0_sync(s):
            bb.inc(s.dma_start(par[:, :], params[:, :]), "sX", 16)
            bb.inc(s.dma_start(emask[:, :], emaskin[:, :]), "sX", 16)
            bb.inc(s.dma_start(rflag[:, :], evenflag[:, :]), "sX", 16, "p0_ld")

        def p0_vec(v):
            bb.wait(v, "sX", "p0_ld")
            v.memset(ones_k[:, :], 1.0)
            v.memset(ones_b[:, :], 1.0)
            v.memset(magict[:, :], MAGIC)
            v.memset(rdat[:, :], 0.0)
            bb.inc(v.memset(zerot[:, :], 0), "sV", 1, "p0_const")

        on("sync", p0_sync)
        on("vector", p0_vec)

        def emit_ln(tag, src, pre_bar=None):
            def f_gp(g):
                bb.wait(g, "sV", "p0_const")
                if pre_bar:
                    bb.bar(g, pre_bar)
                for dt in range(DT):
                    if dt >= 2:
                        bb.wait(g, "sPE", (tag, "st", dt - 2))
                    bb.inc(g.dma_start(ybf[:, dt * T:(dt + 1) * T], src[dt, :, :]),
                           ("sLN0", "sLN1")[dt % 2], 16, (tag, "x", dt))
            on("gpsimd", f_gp)

            def f_sq(sc):
                for dt in range(DT):
                    sq = (sq0, sq1)[dt % 2]
                    bb.wait(sc, ("sLN0", "sLN1")[dt % 2], (tag, "x", dt))
                    if dt >= 2:
                        bb.wait(sc, "sPE", (tag, "st", dt - 2))
                    bb.inc(sc.square(sq[:, :], ybf[:, dt * T:(dt + 1) * T]),
                           "sS", 1, (tag, "sq", dt))
            on("scalar", f_sq)

            bcap = cap_banks()

            def f_pe(t):
                bb.wait(t, "sV", "p0_const")
                for dt in range(DT):
                    sq = (sq0, sq1)[dt % 2]
                    bb.wait(t, "sS", (tag, "sq", dt))
                    if dt == 0:
                        wait_banks(t, bcap)
                    for tc in range(TC):
                        mm(t, psum[tc][0:1, :], ones_k[:, :],
                           ybf[:, dt * T + tc * 512:dt * T + (tc + 1) * 512],
                           dt == 0, dt == DT - 1)
                        li = mm(t, psum[4 + tc][0:1, :], ones_k[:, :],
                                sq[:, tc * 512:(tc + 1) * 512],
                                dt == 0, dt == DT - 1)
                    bb.inc(li, "sPE", 1, (tag, "st", dt))
            on("tensor", f_pe)

            def f_v1(v):
                bb.wait(v, "sPE", (tag, "st", DT - 1))
                for tc in range(TC):
                    v.tensor_copy(r_mean[0:1, tc * 512:(tc + 1) * 512],
                                  psum[tc][0:1, :])
                    v.tensor_copy(r_sq[0:1, tc * 512:(tc + 1) * 512],
                                  psum[4 + tc][0:1, :])
                v.tensor_scalar_mul(r_mean[0:1, :], r_mean[0:1, :], 1.0 / 4096.0)
                bb.inc(v.tensor_scalar_mul(r_sq[0:1, :], r_sq[0:1, :],
                                           1.0 / 4096.0),
                       "sV", 1, (tag, "ex2"))
            on("vector", f_v1)

            def f_msq(sc):
                bb.wait(sc, "sV", (tag, "ex2"))
                for tc in range(TC):
                    bb.inc(sc.square(psum[4 + tc][64:65, :],
                                     r_mean[0:1, tc * 512:(tc + 1) * 512]),
                           "sS", 1, (tag, "msq", tc))
            on("scalar", f_msq)

            def f_var(v):
                bb.wait(v, "sS", (tag, "msq", TC - 1))
                for tc in range(TC):
                    v.tensor_sub(r_sq[0:1, tc * 512:(tc + 1) * 512],
                                 r_sq[0:1, tc * 512:(tc + 1) * 512],
                                 psum[4 + tc][64:65, :])
                bb.inc(v.tensor_scalar_add(r_sq[0:1, :], r_sq[0:1, :], EPS),
                       "sV", 1, (tag, "var"))
            on("vector", f_var)

            def f_sc2(sc):
                bb.wait(sc, "sV", (tag, "var"))
                sc.activation(r_tmp[0:1, :], r_sq[0:1, :], AF.Ln)
                bb.inc(sc.activation(r_rstd[0:1, :], r_tmp[0:1, :], AF.Exp,
                                     scale=pcol1("c_neghalf")),
                       "sS", 1, (tag, "rstd"))
            on("scalar", f_sc2)

            def f_pe2(t):
                bb.wait(t, "sS", (tag, "rstd"))
                for tc in range(TC):
                    mm(t, psum[tc][:, :], ones_b[0:1, :],
                       r_rstd[0:1, tc * 512:(tc + 1) * 512], True, True)
                    li = mm(t, psum[4 + tc][:, :], ones_b[0:1, :],
                            r_mean[0:1, tc * 512:(tc + 1) * 512], True, True)
                bb.inc(li, "sPE", 1, (tag, "bcast"))
            on("tensor", f_pe2)

            def f_v3(v):
                bb.wait(v, "sPE", (tag, "bcast"))
                for dt in range(DT):
                    for tc in range(TC):
                        s5 = slice(dt * T + tc * 512, dt * T + (tc + 1) * 512)
                        v.tensor_sub(sq1[:, tc * 512:(tc + 1) * 512],
                                     ybf[:, s5], psum[4 + tc][:, :])
                        li = v.tensor_mul(ybf[:, s5],
                                          sq1[:, tc * 512:(tc + 1) * 512],
                                          psum[tc][:, :])
                bb.inc(li, "sV", 1, (tag, "done"))
            on("vector", f_v3)
            set_bank_free(range(8), "sV", (tag, "done"))

        def emit_gateup(tag, call, ln_tag, pre_bar=None):
            parbufs = [(bufA, bufC), (bufB, bufD)]

            if call == 1:
                def f_gp(g):
                    if pre_bar:
                        bb.bar(g, pre_bar)
                    for k in range(IT):
                        gq, uq = parbufs[k % 2]
                        if k >= 2:
                            bb.wait(g, "sPE", (tag, "udone", k - 2))
                            bb.wait(g, ("sYa", "sYb")[k % 2], (tag, "wb", k - 2))
                        sw = ("sWa", "sWb")[k % 2]
                        bb.inc(g.dma_start(gq[:, :], wg8[k, :, :]), sw, 16)
                        bb.inc(g.dma_start(uq[:, :], wu8[k, :, :]), sw, 16,
                               (tag, "cast", k))
                        for hh in range(2):
                            if k >= 1 or hh == 1:
                                pk, ph = (k, 0) if hh == 1 else (k - 1, 1)
                                bb.wait(g, "sV", (tag, "dqg", pk, ph))
                                bb.wait(g, "sV", (tag, "dqu", pk, ph))
                            src_g = bass.AP(gsr.tensor, k * 4096 + hh * 2048,
                                            [[0, 128], [1, 2048]])
                            src_u = bass.AP(usr.tensor, k * 4096 + hh * 2048,
                                            [[0, 128], [1, 2048]])
                            bb.inc(g.dma_start(sb_g[:, :], src_g), "sBg", 16,
                                   (tag, "bcg", k, hh))
                            bb.inc(g.dma_start(sb_u[:, :], src_u), "sBu", 16,
                                   (tag, "bcu", k, hh))
                on("gpsimd", f_gp)

                def f_vdq_atm(v):
                    def atm(k):
                        att = (at0, at1)[k % 2]
                        bb.wait(v, "sPE", (tag, "udone", k))
                        if k >= 2:
                            bb.wait(v, ("sAa", "sAb")[k % 2], (tag, "ats", k - 2))
                        for tc in range(TC):
                            sil = (sil0, sil1)[tc % 2]
                            bb.wait(v, "sS", (tag, "silt", k, tc))
                            bb.inc(v.tensor_mul(att[:, tc * 512:(tc + 1) * 512],
                                                sil[:, :], psum[4 + tc][:, :]),
                                   "sV", 1, (tag, "atmt", k, tc))
                    for k in range(IT):
                        gq, uq = parbufs[k % 2]
                        bb.wait(v, ("sWa", "sWb")[k % 2], (tag, "cast", k))
                        for hh in range(2):
                            sl = slice(hh * 2048, (hh + 1) * 2048)
                            bb.wait(v, "sBg", (tag, "bcg", k, hh))
                            bb.inc(v.scalar_tensor_tensor(
                                gq[:, sl], gq[:, sl], 8.0, sb_g[:, :],
                                op0=ALU.subtract, op1=ALU.mult),
                                "sV", 1, (tag, "dqg", k, hh))
                            bb.wait(v, "sBu", (tag, "bcu", k, hh))
                            bb.inc(v.scalar_tensor_tensor(
                                uq[:, sl], uq[:, sl], 8.0, sb_u[:, :],
                                op0=ALU.subtract, op1=ALU.mult),
                                "sV", 1, (tag, "dqu", k, hh))
                        bb.mark("sV", (tag, "dq", k))
                        if k >= 1:
                            atm(k - 1)
                    atm(IT - 1)
                on("vector", f_vdq_atm)

                def f_wb_sil(sc):
                    for k in range(IT):
                        gq, uq = parbufs[k % 2]
                        bb.wait(sc, "sV", (tag, "dq", k))
                        sy = ("sYa", "sYb")[k % 2]
                        bb.inc(sc.dma_start(wgbf[k, :, :], gq[:, :]), sy, 16)
                        bb.inc(sc.dma_start(wubf[k, :, :], uq[:, :]), sy, 16,
                               (tag, "wb", k))
                        bb.wait(sc, "sPE", (tag, "gdone", k))
                        for tc in range(TC):
                            sil = (sil0, sil1)[tc % 2]
                            if k > 0 or tc >= 2:
                                kk2, tc2 = ((k, tc - 2) if tc >= 2
                                            else (k - 1, tc + 2))
                                bb.wait(sc, "sV", (tag, "atmt", kk2, tc2))
                            bb.inc(sc.activation(sil[:, :], psum[tc][:, :],
                                                 AF.Silu),
                                   "sS", 1, (tag, "silt", k, tc))
                on("scalar", f_wb_sil)
            else:
                def f_ld(s):
                    if pre_bar:
                        bb.bar(s, pre_bar)
                    for k in range(IT):
                        gq, uq = parbufs[k % 2]
                        if k >= 2:
                            bb.wait(s, "sPE", (tag, "udone", k - 2))
                        sw = ("sXa", "sXb")[k % 2]
                        bb.inc(s.dma_start(gq[:, :], wgbf[k, :, :]), sw, 16)
                        bb.inc(s.dma_start(uq[:, :], wubf[k, :, :]), sw, 16,
                               (tag, "ld", k))
                on("sync", f_ld)

                def f_atm2(v):
                    for k in range(IT):
                        att = (at0, at1)[k % 2]
                        bb.wait(v, "sPE", (tag, "udone", k))
                        if k >= 2:
                            bb.wait(v, ("sAa", "sAb")[k % 2], (tag, "ats", k - 2))
                        for tc in range(TC):
                            sil = (sil0, sil1)[tc % 2]
                            bb.wait(v, "sS", (tag, "silt", k, tc))
                            bb.inc(v.tensor_mul(att[:, tc * 512:(tc + 1) * 512],
                                                sil[:, :], psum[4 + tc][:, :]),
                                   "sV", 1, (tag, "atmt", k, tc))
                on("vector", f_atm2)

                def f_sil_ats(sc):
                    for k in range(IT):
                        att = (at0, at1)[k % 2]
                        bb.wait(sc, "sPE", (tag, "gdone", k))
                        for tc in range(TC):
                            sil = (sil0, sil1)[tc % 2]
                            if k > 0 or tc >= 2:
                                kk2, tc2 = ((k, tc - 2) if tc >= 2
                                            else (k - 1, tc + 2))
                                bb.wait(sc, "sV", (tag, "atmt", kk2, tc2))
                            bb.inc(sc.activation(sil[:, :], psum[tc][:, :],
                                                 AF.Silu),
                                   "sS", 1, (tag, "silt", k, tc))
                        if k >= 1:
                            for tc in range(TC):
                                bb.wait(sc, "sV", (tag, "atmt", k - 1, tc))
                                bb.inc(sc.dma_start(
                                    at_s[tc, :, k - 1, :],
                                    (at0, at1)[(k - 1) % 2]
                                    [:, tc * 512:(tc + 1) * 512]),
                                    ("sAa", "sAb")[(k - 1) % 2], 16,
                                    (tag, "ats", k - 1) if tc == TC - 1
                                    else None)
                    for tc in range(TC):
                        bb.wait(sc, "sV", (tag, "atmt", IT - 1, tc))
                        bb.inc(sc.dma_start(
                            at_s[tc, :, IT - 1, :],
                            (at0, at1)[(IT - 1) % 2][:, tc * 512:(tc + 1) * 512]),
                            ("sAa", "sAb")[(IT - 1) % 2], 16,
                            (tag, "ats", IT - 1) if tc == TC - 1 else None)
                on("scalar", f_sil_ats)

            bcap = cap_banks()

            def f_pe(t):
                bb.wait(t, "sV", (ln_tag, "done"))
                for k in range(IT):
                    gq, uq = parbufs[k % 2]
                    if call == 1:
                        bb.wait(t, "sV", (tag, "dq", k))
                    else:
                        bb.wait(t, ("sXa", "sXb")[k % 2], (tag, "ld", k))
                    if k == 0:
                        wait_banks(t, bcap)
                    if k >= 1:
                        bb.wait(t, "sS", (tag, "silt", k - 1, TC - 1))
                    for kk in range(DT):
                        wsl = gq[:, kk * 128:(kk + 1) * 128]
                        for tc in range(TC):
                            li = mm(t, psum[tc][:, :], wsl,
                                    ybf[:, kk * T + tc * 512:
                                        kk * T + (tc + 1) * 512],
                                    kk == 0, kk == DT - 1)
                    bb.inc(li, "sPE", 1, (tag, "gdone", k))
                    if k >= 1:
                        bb.wait(t, "sV", (tag, "atmt", k - 1, TC - 1))
                    for kk in range(DT):
                        wsl = uq[:, kk * 128:(kk + 1) * 128]
                        for tc in range(TC):
                            li = mm(t, psum[4 + tc][:, :], wsl,
                                    ybf[:, kk * T + tc * 512:
                                        kk * T + (tc + 1) * 512],
                                    kk == 0, kk == DT - 1)
                    bb.inc(li, "sPE", 1, (tag, "udone", k))
            on("tensor", f_pe)

            if call == 1:
                def f_ats(e):
                    for k in range(IT):
                        att = (at0, at1)[k % 2]
                        for tc in range(TC):
                            bb.wait(e, "sV", (tag, "atmt", k, tc))
                            bb.inc(e.dma_start(at_s[tc, :, k, :],
                                               att[:, tc * 512:(tc + 1) * 512]),
                                   ("sAa", "sAb")[k % 2], 16,
                                   (tag, "ats", k) if tc == TC - 1 else None)
                on("sync", f_ats)

            set_bank_free(range(4), "sS", (tag, "silt", IT - 1, TC - 1))
            set_bank_free(range(4, 8), "sV", (tag, "atmt", IT - 1, TC - 1))

        def emit_down(tag, call, rsrc, rdst, pre_bar, out_sem):
            parw = [bufA, bufB, bufC, bufD]
            sbb = [sb_g, sb_u]
            kblocks = []
            kk = 0
            while kk < IT:
                kblocks.append((kk, min(2, IT - kk)))
                kk += 2
            NKB = len(kblocks)
            dqmode = (call == 1)

            def wd_re(ap3):
                return ap3.rearrange("k p j -> p k j")

            KH = 43
            slots = [ybf[:, 0:KH * 512], ybf[:, KH * 512:2 * KH * 512], None]

            def slot_tile(s, t):
                if s < 2:
                    return slots[s][:, t * 512:(t + 1) * 512]
                if t < 42:
                    return ybf[:, (86 + t) * 512:(86 + t + 1) * 512]
                return sq0[:, 0:512]

            def slotA(tc):
                return (2 * tc) % 3

            def slotB(tc):
                return (2 * tc + 1) % 3

            def at_rd(tc, k):
                if k < KH:
                    return slot_tile(slotA(tc), k)
                return slot_tile(slotB(tc), k - KH)

            def lda_emit(e, sem, tc, half, key):
                s = slotA(tc) if half == 0 else slotB(tc)
                k0 = 0 if half == 0 else KH
                if s < 2:
                    bb.inc(e.dma_start(
                        slots[s][:, :],
                        at_s[tc, :, k0:k0 + KH, :].rearrange("p k j -> p (k j)")),
                        sem, 16, key)
                else:
                    bb.inc(e.dma_start(
                        ybf[:, 86 * 512:128 * 512],
                        at_s[tc, :, k0:k0 + 42, :].rearrange("p k j -> p (k j)")),
                        sem, 16)
                    bb.inc(e.dma_start(sq0[:, 0:512], at_s[tc, :, k0 + 42, :]),
                           sem, 16, key)

            def f_sync(s):
                bb.bar(s, pre_bar)
                lda_emit(s, "sLA", 0, 0, (tag, "ldA", 0))
                lda_emit(s, "sLC", 0, 1, None)
                lda_emit(s, "sLC", 1, 0, (tag, "ldC0"))
                prevuse = {}
                for tc in range(TC):
                    for o in range(4):
                        for m in range(8):
                            dt = o * 8 + m
                            rs = rs_[m % 4]
                            pk = prevuse.get(m % 4)
                            if pk is not None:
                                bb.wait(s, "sV", pk[0])
                                bb.wait(s, f"sST{m % 4}", pk[1])
                            bb.inc(s.dma_start(
                                rs[:, :], rsrc[dt, :, tc * 512:(tc + 1) * 512]),
                                f"sR{m % 4}", 16, (tag, "rld", tc, o, m))
                            prevuse[m % 4] = ((tag, "evac", tc, o, m),
                                              (tag, "ost", tc, o, m))
            on("sync", f_sync)

            def f_ldw(g):
                if pre_bar:
                    bb.bar(g, pre_bar)
                bidx = 0
                for tc in range(TC):
                    for o in range(4):
                        for bi, (k0, nk) in enumerate(kblocks):
                            wbuf = parw[bidx % 4]
                            if bidx >= 4:
                                bb.wait(g, "sPE", (tag, "wused", bidx - 4))
                            sw = ("sWa", "sWb", "sWc", "sWd")[bidx % 4]
                            if dqmode and tc == 0:
                                bb.inc(g.dma_start(
                                    wbuf[:, 0:nk * 1024],
                                    wd_re(wd8)[:, k0:k0 + nk,
                                               o * 1024:(o + 1) * 1024]),
                                    sw, 16, (tag, "wld", bidx))
                                if bidx >= 2:
                                    bb.wait(g, "sV", (tag, "wdqp", bidx - 2))
                                src_s = bass.AP(
                                    dsr.tensor, k0 * 4096 + o * 1024,
                                    [[0, 128], [4096, nk], [1, 1024]])
                                bb.inc(g.dma_start(
                                    sbb[bidx % 2][:, 0:nk * 1024], src_s),
                                    ("sBg", "sBu")[bidx % 2], 16,
                                    (tag, "wbc", bidx))
                            else:
                                if dqmode:
                                    bb.wait(g, ("sYa", "sYb", "sYc", "sYd")
                                            [(o * NKB + bi) % 4],
                                            (tag, "wwb", o * NKB + bi))
                                bb.inc(g.dma_start(
                                    wbuf[:, 0:nk * 1024],
                                    wd_re(wdbf)[:, k0:k0 + nk,
                                                o * 1024:(o + 1) * 1024]),
                                    sw, 16, (tag, "wld", bidx))
                            bidx += 1
            on("gpsimd", f_ldw)

            def f_wdq_ev(v):
                for tc in range(TC):
                    for o in range(4):
                        if dqmode and tc == 0:
                            for bi, (k0, nk) in enumerate(kblocks):
                                bidx = o * NKB + bi
                                wbuf = parw[bidx % 4]
                                bb.wait(v, ("sWa", "sWb", "sWc", "sWd")
                                        [bidx % 4], (tag, "wld", bidx))
                                bb.wait(v, ("sBg", "sBu")[bidx % 2],
                                        (tag, "wbc", bidx))
                                bb.inc(v.scalar_tensor_tensor(
                                    wbuf[:, 0:nk * 1024],
                                    wbuf[:, 0:nk * 1024], 8.0,
                                    sbb[bidx % 2][:, 0:nk * 1024],
                                    op0=ALU.subtract, op1=ALU.mult),
                                    "sV", 1, (tag, "wdqp", bidx))
                        bb.wait(v, "sPE", (tag, "odone", tc, o))
                        for m in range(8):
                            rs = rs_[m % 4]
                            bb.wait(v, f"sR{m % 4}", (tag, "rld", tc, o, m))
                            bb.inc(v.scalar_tensor_tensor(
                                rs[:, :], psum[m][:, :], 0.5, rs[:, :],
                                op0=ALU.mult, op1=ALU.add),
                                "sV", 1, (tag, "evac", tc, o, m))
            on("vector", f_wdq_ev)

            bcap = cap_banks()

            def f_pe(t):
                bidx = 0
                for tc in range(TC):
                    awaits = {0: ("sLA", (tag, "ldA", 0)),
                              1: ("sLC", (tag, "ldC0")),
                              2: ("sLB", (tag, "ldpair", 0)),
                              3: ("sLB", (tag, "ldpair", 1))}[tc]
                    bwaits = {0: ("sLC", (tag, "ldC0")),
                              1: ("sLB", (tag, "ldpair", 0)),
                              2: ("sLB", (tag, "ldpair", 1)),
                              3: ("sLB", (tag, "ldpair", 2))}[tc]
                    bb.wait(t, awaits[0], awaits[1])
                    for o in range(4):
                        if (tc, o) == (0, 0):
                            wait_banks(t, bcap)
                        else:
                            po = (tc, o - 1) if o > 0 else (tc - 1, 3)
                            bb.wait(t, "sV", (tag, "evac", po[0], po[1], 7))
                        for bi, (k0, nk) in enumerate(kblocks):
                            wbuf = parw[bidx % 4]
                            if k0 >= KH or (k0 + nk > KH):
                                bb.wait(t, bwaits[0], bwaits[1])
                            if dqmode and tc == 0:
                                bb.wait(t, "sV", (tag, "wdqp", bidx))
                            else:
                                bb.wait(t, ("sWa", "sWb", "sWc", "sWd")
                                        [bidx % 4], (tag, "wld", bidx))
                            for ki in range(nk):
                                k = k0 + ki
                                for m_ in range(8):
                                    li = mm(t, psum[m_][:, :],
                                            wbuf[:, ki * 1024 + m_ * 128:
                                                 ki * 1024 + (m_ + 1) * 128],
                                            at_rd(tc, k),
                                            k == 0, k == IT - 1)
                            bb.inc(li, "sPE", 1, (tag, "wused", bidx))
                            bidx += 1
                        bb.mark("sPE", (tag, "odone", tc, o))
            on("tensor", f_pe)

            def f_wwb_ost(sc):
                for tc in range(TC):
                    for o in range(4):
                        if dqmode and tc == 0:
                            for bi, (k0, nk) in enumerate(kblocks):
                                bidx = o * NKB + bi
                                wbuf = parw[bidx % 4]
                                bb.wait(sc, "sV", (tag, "wdqp", bidx))
                                if bidx >= 4:
                                    bb.wait(sc,
                                            ("sYa", "sYb", "sYc", "sYd")
                                            [bidx % 4], (tag, "wwb", bidx - 4))
                                bb.inc(sc.dma_start(
                                    wd_re(wdbf)[:, k0:k0 + nk,
                                                o * 1024:(o + 1) * 1024],
                                    wbuf[:, 0:nk * 1024]),
                                    ("sYa", "sYb", "sYc", "sYd")[bidx % 4], 16,
                                    (tag, "wwb", bidx))
                        for m in range(8):
                            dt = o * 8 + m
                            rs = rs_[m % 4]
                            bb.wait(sc, "sV", (tag, "evac", tc, o, m))
                            bb.inc(sc.dma_start(
                                rdst[dt, :, tc * 512:(tc + 1) * 512], rs[:, :]),
                                f"sST{m % 4}", 16, (tag, "ost", tc, o, m))
                    if tc + 1 < TC:
                        lda_emit(sc, "sLB", tc + 1, 1, None)
                    if tc + 2 < TC:
                        lda_emit(sc, "sLB", tc + 2, 0, None)
                    if tc + 1 < TC:
                        bb.mark("sLB", (tag, "ldpair", tc))
            on("scalar", f_wwb_ost)
            set_bank_free(range(8), "sV", (tag, "evac", TC - 1, 3, 7))

        def emit_attn_ln(tag, hready_key):
            def f_cast(sc):
                for dt in range(ADT):
                    hb = (hbf, hbf2)[dt % 2]
                    sq = (sq0, sq1)[dt % 2]
                    bb.wait(sc, "sV", hready_key(dt))
                    if dt >= 2:
                        bb.wait(sc, "sPE", (tag, "st", dt - 2))
                    sc.copy(hb[:, :], h_t[dt])
                    bb.inc(sc.square(sq[:, 0:T], h_t[dt]), "sS", 1,
                           (tag, "sq", dt))
            on("scalar", f_cast)

            bcap = cap_banks()

            def f_pe(t):
                for dt in range(ADT):
                    hb = (hbf, hbf2)[dt % 2]
                    sq = (sq0, sq1)[dt % 2]
                    bb.wait(t, "sS", (tag, "sq", dt))
                    if dt == 0:
                        wait_banks(t, bcap)
                    for tc in range(TC):
                        mm(t, psum[tc][0:1, :], ones_k[:, :],
                           hb[:, tc * 512:(tc + 1) * 512], dt == 0, dt == ADT - 1)
                        li = mm(t, psum[4 + tc][0:1, :], ones_k[:, :],
                                sq[:, tc * 512:(tc + 1) * 512],
                                dt == 0, dt == ADT - 1)
                    bb.inc(li, "sPE", 1, (tag, "st", dt))
            on("tensor", f_pe)

            def f_v1(v):
                bb.wait(v, "sPE", (tag, "st", ADT - 1))
                for tc in range(TC):
                    v.tensor_copy(r_mean[0:1, tc * 512:(tc + 1) * 512],
                                  psum[tc][0:1, :])
                    v.tensor_copy(r_sq[0:1, tc * 512:(tc + 1) * 512],
                                  psum[4 + tc][0:1, :])
                v.tensor_scalar_mul(r_mean[0:1, :], r_mean[0:1, :], 1.0 / 512.0)
                bb.inc(v.tensor_scalar_mul(r_sq[0:1, :], r_sq[0:1, :],
                                           1.0 / 512.0),
                       "sV", 1, (tag, "ex2"))
            on("vector", f_v1)

            def f_msq(sc):
                bb.wait(sc, "sV", (tag, "ex2"))
                for tc in range(TC):
                    bb.inc(sc.square(psum[4 + tc][64:65, :],
                                     r_mean[0:1, tc * 512:(tc + 1) * 512]),
                           "sS", 1, (tag, "msq", tc))
            on("scalar", f_msq)

            def f_var(v):
                bb.wait(v, "sS", (tag, "msq", TC - 1))
                for tc in range(TC):
                    v.tensor_sub(r_sq[0:1, tc * 512:(tc + 1) * 512],
                                 r_sq[0:1, tc * 512:(tc + 1) * 512],
                                 psum[4 + tc][64:65, :])
                bb.inc(v.tensor_scalar_add(r_sq[0:1, :], r_sq[0:1, :], EPS),
                       "sV", 1, (tag, "var"))
            on("vector", f_var)

            def f_sc2(sc):
                bb.wait(sc, "sV", (tag, "var"))
                sc.activation(r_tmp[0:1, :], r_sq[0:1, :], AF.Ln)
                bb.inc(sc.activation(r_rstd[0:1, :], r_tmp[0:1, :], AF.Exp,
                                     scale=pcol1("c_neghalf")),
                       "sS", 1, (tag, "rstd"))
            on("scalar", f_sc2)

            def f_pe2(t):
                bb.wait(t, "sS", (tag, "rstd"))
                for tc in range(TC):
                    mm(t, psum[tc][:, :], ones_b[0:1, :],
                       r_rstd[0:1, tc * 512:(tc + 1) * 512], True, True)
                    li = mm(t, psum[4 + tc][:, :], ones_b[0:1, :],
                            r_mean[0:1, tc * 512:(tc + 1) * 512], True, True)
                bb.inc(li, "sPE", 1, (tag, "bcast"))
            on("tensor", f_pe2)

        def emit_attn_norm_out(tag, wname, bname, func, outf):
            def f_vn(v):
                bb.wait(v, "sPE", (tag, "bcast"))
                for dt in range(ADT):
                    for tc in range(TC):
                        i = dt * TC + tc
                        ra, rb = ((rs_[0], rs_[1]) if i % 2 == 0
                                  else (rs_[2], rs_[3]))
                        if i >= 2:
                            bb.wait(v, "sS", (tag, "out", i - 2))
                        s5 = slice(tc * 512, (tc + 1) * 512)
                        v.tensor_sub(ra[:, :], h_t[dt][:, s5], psum[4 + tc][:, :])
                        bb.inc(v.tensor_mul(rb[:, :], ra[:, :], psum[tc][:, :]),
                               "sV", 1, (tag, "nt", i))
            on("vector", f_vn)

            def f_so(sc):
                for dt in range(ADT):
                    for tc in range(TC):
                        i = dt * TC + tc
                        rb = rs_[1] if i % 2 == 0 else rs_[3]
                        bb.wait(sc, "sV", (tag, "nt", i))
                        kw = {}
                        if wname is not None:
                            kw = dict(scale=pcol(f"{wname}{dt}"),
                                      bias=pcol(f"{bname}{dt}"))
                        bb.inc(sc.activation(outf(dt)[:, tc * 512:(tc + 1) * 512],
                                             rb[:, :], func, **kw),
                               "sS", 1, (tag, "out", i))
                bb.mark("sS", (tag, "outdone"))
            on("scalar", f_so)
            set_bank_free(range(8), "sV", (tag, "nt", ADT * TC - 1))

        def emit_scan(l, ct):
            tag = ("sc", l, ct)
            pw = pwt[:, ct * 32:(ct + 1) * 32]
            prg_r = prgs_r[:, ct * CH:(ct + 1) * CH]
            prg_i = prgs_i[:, ct * CH:(ct + 1) * CH]

            def f_tables(v):
                bb.wait(v, "sS", ("y", l, ct))
                pr = pcol(f"phr_{l}_{ct}")
                pi = pcol(f"phi_{l}_{ct}")
                v.tensor_mul(s64a[:, 0:1], pr, pr)
                v.tensor_mul(s64b[:, 0:1], pi, pi)
                v.drain()
                bb.inc(v.scalar_tensor_tensor(s64a[:, 1:2], s64a[:, 0:1],
                                              1e-38, s64b[:, 0:1],
                                              op0=ALU.add, op1=ALU.add),
                       "sV", 1, (tag, "r2"))
            on("vector", f_tables)

            def f_pr(sc):
                bb.wait(sc, "sV", (tag, "r2"))
                sc.activation(s64a[:, 2:3], s64a[:, 1:2], AF.Ln)
                sc.activation(s64a[:, 3:4], s64a[:, 2:3], AF.Exp,
                              scale=pcol("c_half"))
                bb.inc(sc.activation(s64a[:, 4:5], s64a[:, 3:4], AF.Tanh),
                       "sS", 1, (tag, "tanh"))
            on("scalar", f_pr)

            def f_pw(v):
                bb.wait(v, "sS", (tag, "tanh"))
                v.tensor_scalar_max(s64a[:, 3:4], s64a[:, 3:4], 1e-30)
                v.drain()
                v.reciprocal(s64b[:, 0:1], s64a[:, 3:4])
                v.drain()
                v.tensor_mul(s64a[:, 5:6], s64a[:, 4:5], s64b[:, 0:1])
                v.drain()
                v.tensor_mul(pw[:, 0:1], pcol(f"phr_{l}_{ct}"), s64a[:, 5:6])
                v.tensor_mul(pw[:, 1:2], pcol(f"phi_{l}_{ct}"), s64a[:, 5:6])
                for m in range(1, 12):
                    a = pw[:, 2 * m - 2:2 * m - 1]
                    b = pw[:, 2 * m - 1:2 * m]
                    v.tensor_mul(s64a[:, 6:7], a, a)
                    v.tensor_mul(s64b[:, 1:2], b, b)
                    v.tensor_mul(s64a[:, 7:8], a, b)
                    v.tensor_sub(pw[:, 2 * m:2 * m + 1], s64a[:, 6:7],
                                 s64b[:, 1:2])
                    v.tensor_add(pw[:, 2 * m + 1:2 * m + 2], s64a[:, 7:8],
                                 s64a[:, 7:8])
                v.tensor_copy(prg_i[:, 0:1], pw[:, 1:2])
                v.tensor_copy(prg_r[:, 0:1], pw[:, 0:1])
                for m in range(7):
                    w = 1 << m
                    ar = pw[:, 2 * m:2 * m + 1]
                    ai = pw[:, 2 * m + 1:2 * m + 2]
                    v.tensor_scalar_mul(s64b[:, 0:w], prg_i[:, 0:w], ar)
                    v.tensor_scalar_mul(s64a[:, 0:w], prg_i[:, 0:w], ai)
                    v.scalar_tensor_tensor(prg_i[:, w:2 * w], prg_r[:, 0:w],
                                           ai, s64b[:, 0:w],
                                           op0=ALU.mult, op1=ALU.add)
                    li = v.scalar_tensor_tensor(prg_r[:, w:2 * w],
                                                prg_r[:, 0:w],
                                                ar, s64a[:, 0:w],
                                                op0=ALU.mult, op1=ALU.subtract)
                bb.inc(li, "sV", 1, (tag, "tables"))
            on("vector", f_pw)

            def f_intra(v):
                y3 = y_ct.rearrange("p (c j) -> p c j", c=NCH)
                Ar = LAr.rearrange("p (c j) -> p c j", c=NCH)
                Ai = LAi.rearrange("p (c j) -> p c j", c=NCH)
                Br = LBr.rearrange("p (c j) -> p c j", c=NCH)
                Bi = LBi.rearrange("p (c j) -> p c j", c=NCH)
                v.scalar_tensor_tensor(Ar[:, :, 1:], y3[:, :, :CH - 1],
                                       pw[:, 0:1], y3[:, :, 1:],
                                       op0=ALU.mult, op1=ALU.add)
                v.tensor_scalar_mul(Ai[:, :, 1:], y3[:, :, :CH - 1], pw[:, 1:2])
                v.tensor_copy(Ar[:, :, 0:1], y3[:, :, 0:1])
                v.memset(Ai[:, :, 0:1], 0.0)
                cur = (Ar, Ai)
                oth = (Br, Bi)
                for m in range(1, 7):
                    d = 1 << m
                    sr, si = cur
                    dr, di = oth
                    ar = pw[:, 2 * m:2 * m + 1]
                    ai = pw[:, 2 * m + 1:2 * m + 2]
                    nai = s64a[:, 8:9]
                    v.tensor_scalar_mul(nai, ai, -1.0)
                    v.scalar_tensor_tensor(dr[:, :, d:], sr[:, :, :CH - d], ar,
                                           sr[:, :, d:],
                                           op0=ALU.mult, op1=ALU.add)
                    v.scalar_tensor_tensor(di[:, :, d:], si[:, :, :CH - d], ar,
                                           si[:, :, d:],
                                           op0=ALU.mult, op1=ALU.add)
                    v.scalar_tensor_tensor(dr[:, :, d:], si[:, :, :CH - d], nai,
                                           dr[:, :, d:],
                                           op0=ALU.mult, op1=ALU.add)
                    v.scalar_tensor_tensor(di[:, :, d:], sr[:, :, :CH - d], ai,
                                           di[:, :, d:],
                                           op0=ALU.mult, op1=ALU.add)
                    v.tensor_copy(dr[:, :, 0:d], sr[:, :, 0:d])
                    v.tensor_copy(di[:, :, 0:d], si[:, :, 0:d])
                    cur, oth = oth, cur
                fr, fi = cur
                v.tensor_copy(e0r[:, 1:NCH + 1],
                              fr[:, :, CH - 1:CH].rearrange("p c j -> p (c j)"))
                bb.inc(v.tensor_copy(
                    e0i[:, 1:NCH + 1],
                    fi[:, :, CH - 1:CH].rearrange("p c j -> p (c j)")),
                    "sV", 1, (tag, "intra"))
            on("vector", f_intra)

            def f_escan(v):
                lcr = pcol(f"lcr_{l}_{ct}")
                lci = pcol(f"lci_{l}_{ct}")
                v.tensor_scalar_mul(e0r[:, 0:1], lcr, emask[:, 0:1])
                v.tensor_scalar_mul(e0i[:, 0:1], lci, emask[:, 0:1])
                cur = (e0r, e0i)
                oth = (e1r, e1i)
                N = NCH + 1
                for mi, m in enumerate(range(7, 12)):
                    d = 1 << mi
                    sr, si = cur
                    dr, di = oth
                    ar = pw[:, 2 * m:2 * m + 1]
                    ai = pw[:, 2 * m + 1:2 * m + 2]
                    nai = s64a[:, 9:10]
                    v.tensor_scalar_mul(nai, ai, -1.0)
                    v.scalar_tensor_tensor(dr[:, d:N], sr[:, 0:N - d], ar,
                                           sr[:, d:N], op0=ALU.mult, op1=ALU.add)
                    v.scalar_tensor_tensor(di[:, d:N], si[:, 0:N - d], ar,
                                           si[:, d:N], op0=ALU.mult, op1=ALU.add)
                    v.scalar_tensor_tensor(dr[:, d:N], si[:, 0:N - d], nai,
                                           dr[:, d:N], op0=ALU.mult, op1=ALU.add)
                    v.scalar_tensor_tensor(di[:, d:N], sr[:, 0:N - d], ai,
                                           di[:, d:N], op0=ALU.mult, op1=ALU.add)
                    v.tensor_copy(dr[:, 0:d], sr[:, 0:d])
                    v.tensor_copy(di[:, 0:d], si[:, 0:d])
                    cur, oth = oth, cur
                fr, fi = cur
                v.tensor_copy(sx[:, 2 * ct:2 * ct + 1], fr[:, NCH:NCH + 1])
                v.tensor_copy(sx[:, 2 * ct + 1:2 * ct + 2], fi[:, NCH:NCH + 1])
                bb.inc(v.tensor_scalar_mul(esn[:, :], fi[:, :], -1.0),
                       "sV", 1, (tag, "escan"))
            on("vector", f_escan)

            FINr3 = LAr.rearrange("p (c j) -> p c j", c=NCH)

            def f_apply(v):
                for c in range(NCH):
                    v.scalar_tensor_tensor(FINr3[:, c, :], prg_r[:, :],
                                           e1r[:, c:c + 1], FINr3[:, c, :],
                                           op0=ALU.mult, op1=ALU.add)
                for c in range(NCH):
                    v.scalar_tensor_tensor(FINr3[:, c, :], prg_i[:, :],
                                           esn[:, c:c + 1], FINr3[:, c, :],
                                           op0=ALU.mult, op1=ALU.add)
                bb.inc(v.scalar_tensor_tensor(
                    h_t[ct], FINr3.rearrange("p c j -> p (c j)"),
                    pcol(f"scl_{l}_{ct}"), h_t[ct],
                    op0=ALU.mult, op1=ALU.add),
                    "sV", 1, ("hupd", l, ct))
            on("vector", f_apply)

        def emit_scan_corr(l, ct):
            pw = pwt[:, ct * 32:(ct + 1) * 32]
            prg_r = prgs_r[:, ct * CH:(ct + 1) * CH]
            prg_i = prgs_i[:, ct * CH:(ct + 1) * CH]

            def f_corr(v):
                if ct == 0:
                    bb.wait(v, "sEx", ("exr", l))
                    bb.inc(v.tensor_scalar_mul(nrd[:, :], rdat[:, :], -1.0),
                           "sV", 1, ("nrd", l))
                ptr = LBr
                pti = LBi
                v.tensor_copy(pti[:, 0:CH], prg_i[:, :])
                v.tensor_copy(ptr[:, 0:CH], prg_r[:, :])
                for mi, m in enumerate(range(7, 11)):
                    w = CH << mi
                    ar = pw[:, 2 * m:2 * m + 1]
                    ai = pw[:, 2 * m + 1:2 * m + 2]
                    ya = y_ct[:, 0:w]
                    yb = y_ct[:, 1024:1024 + w]
                    v.tensor_scalar_mul(yb, pti[:, 0:w], ar)
                    v.tensor_scalar_mul(ya, pti[:, 0:w], ai)
                    v.scalar_tensor_tensor(pti[:, w:2 * w], ptr[:, 0:w], ai,
                                           yb, op0=ALU.mult, op1=ALU.add)
                    v.scalar_tensor_tensor(ptr[:, w:2 * w], ptr[:, 0:w], ar,
                                           ya, op0=ALU.mult,
                                           op1=ALU.subtract)
                v.tensor_scalar_mul(y_ct[:, 0:T], ptr[:, 0:T],
                                    rdat[:, 2 * ct:2 * ct + 1])
                v.scalar_tensor_tensor(y_ct[:, 0:T], pti[:, 0:T],
                                       nrd[:, 2 * ct + 1:2 * ct + 2],
                                       y_ct[:, 0:T], op0=ALU.mult, op1=ALU.add)
                bb.inc(v.scalar_tensor_tensor(
                    h_t[ct], y_ct[:, 0:T], pcol(f"scl_{l}_{ct}"), h_t[ct],
                    op0=ALU.mult, op1=ALU.add),
                    "sV", 1, ("hcor", l, ct))
            on("vector", f_corr)

        def emit_attn(pre_bar, stopat=None):
            emit_ln("lnA", resid_s, pre_bar=pre_bar)

            def f_winstream(g):
                bb.bar(g, pre_bar)
                for half in range(2):
                    for k in range(DT):
                        i = half * DT + k
                        if i >= 2:
                            bb.wait(g, "sPE", ("winmm", i - 2))
                        bb.inc(g.dma_start((sil0, sil1)[i % 2],
                                           win_t[k, :, :]),
                               ("sWa", "sWb")[i % 2], 16, ("winld", i))
            on("gpsimd", f_winstream)

            bcap_win = cap_banks()

            def f_win(t):
                bb.wait(t, "sV", ("lnA", "done"))
                for half in range(2):
                    if half == 0:
                        wait_banks(t, bcap_win)
                    else:
                        bb.wait(t, "sS", ("winev", 0, 7))
                    for k in range(DT):
                        i = half * DT + k
                        bb.wait(t, ("sWa", "sWb")[i % 2], ("winld", i))
                        wt = (sil0, sil1)[i % 2]
                        for gi in range(8):
                            gidx = half * 8 + gi
                            dt4, tc = gidx // TC, gidx % TC
                            li = mm(t, psum[gi][:, :],
                                    wt[:, dt4 * 128:(dt4 + 1) * 128],
                                    ybf[:, k * T + tc * 512:
                                        k * T + (tc + 1) * 512],
                                    k == 0, k == DT - 1)
                        bb.inc(li, "sPE", 1, ("winmm", i))
                    bb.mark("sPE", ("win", half))
            on("tensor", f_win)

            def f_winev(sc):
                for half in range(2):
                    bb.wait(sc, "sPE", ("win", half))
                    for gi in range(8):
                        gidx = half * 8 + gi
                        dt4, tc = gidx // TC, gidx % TC
                        bb.inc(sc.activation(
                            h_t[dt4][:, tc * 512:(tc + 1) * 512],
                            psum[gi][:, :], AF.Identity, bias=pcol(f"bin{dt4}")),
                            "sS", 1, ("winev", half, gi))
            on("scalar", f_winev)
            set_bank_free(range(8), "sS", ("winev", 1, 7))

            def f_hrdy(v):
                bb.wait(v, "sS", ("winev", 1, 7))
                bb.inc(v.nop(), "sV", 1, ("hrdy", 0))
            on("vector", f_hrdy)

            def dump_h(gatekey):
                def f_d(g):
                    bb.wait(g, "sV", gatekey)
                    for dt in range(ADT):
                        bb.inc(g.dma_start(dbg[dt, :, :], h_t[dt]), "sDB", 16)
                    bb.mark("sDB", "fin_sO")
                    bb.wait(g, "sDB", "fin_sO")
                on("gpsimd", f_d)

            if stopat in ("y00", "in00", "tb00"):
                pass
            if stopat == "win":
                def f_dw(g):
                    bb.wait(g, "sV", ("hrdy", 0))
                    for dt in range(ADT):
                        bb.inc(g.dma_start(dbg[dt, :, :], h_t[dt]), "sDB", 16)
                    bb.mark("sDB", "fin_sO")
                    bb.wait(g, "sDB", "fin_sO")
                on("gpsimd", f_dw)
                return

            for l in range(DEPTH):
                t1 = ("aln", l, 1)
                emit_attn_ln(t1, lambda dt, l=l: ("hrdy", l))

                for ct in range(ADT):
                    def f_vn(v, l=l, ct=ct, t1=t1):
                        if ct == 0:
                            bb.wait(v, "sPE", (t1, "bcast"))
                        for tc in range(TC):
                            i = ct * TC + tc
                            ra, rb = ((rs_[0], rs_[1]) if i % 2 == 0
                                      else (rs_[2], rs_[3]))
                            if i >= 2:
                                bb.wait(v, "sS", (t1, "yout", i - 2))
                            s5 = slice(tc * 512, (tc + 1) * 512)
                            v.tensor_sub(ra[:, :], h_t[ct][:, s5],
                                         psum[4 + tc][:, :])
                            bb.inc(v.tensor_mul(rb[:, :], ra[:, :],
                                                psum[tc][:, :]),
                                   "sV", 1, (t1, "nt", i))
                    on("vector", f_vn)

                    def f_ys(sc, l=l, ct=ct, t1=t1):
                        if ct >= 1:
                            bb.wait(sc, "sV", (("sc", l, ct - 1), "intra"))
                        elif l >= 1:
                            bb.wait(sc, "sV", (("sc", l - 1, ADT - 1), "intra"))
                        for tc in range(TC):
                            i = ct * TC + tc
                            rb = rs_[1] if i % 2 == 0 else rs_[3]
                            bb.wait(sc, "sV", (t1, "nt", i))
                            bb.inc(sc.activation(
                                y_ct[:, tc * 512:(tc + 1) * 512], rb[:, :],
                                AF.Silu, scale=pcol(f"ln1w_{l}_{ct}"),
                                bias=pcol(f"ln1b_{l}_{ct}")),
                                "sS", 1, (t1, "yout", i))
                        bb.mark("sS", ("y", l, ct))
                    on("scalar", f_ys)

                    emit_scan(l, ct)
                    if l == 0 and ct == 0 and stopat in ("y00", "in00", "tb00"):
                        key = {"y00": None, "in00": (("sc", 0, 0), "intra"),
                               "tb00": (("sc", 0, 0), "tables")}[stopat]

                        def f_md(g, key=key, stopat=stopat):
                            if stopat == "y00":
                                bb.wait(g, "sS", ("y", 0, 0))
                                bb.inc(g.dma_start(dbg[0, :, :], y_ct[:, 0:T]),
                                       "sDB", 16)
                            elif stopat == "tb00":
                                bb.wait(g, "sV", key)
                                bb.inc(g.dma_start(
                                    dbg[0, :, 0:CH],
                                    prgs_r[:, 0:CH]), "sDB", 16)
                                bb.inc(g.dma_start(
                                    dbg[0, :, CH:2 * CH],
                                    prgs_i[:, 0:CH]), "sDB", 16)
                                bb.inc(g.dma_start(
                                    dbg[0, :, 2 * CH:2 * CH + 128],
                                    pwt[:, 0:128]), "sDB", 16)
                            else:
                                bb.wait(g, "sV", key)
                                bb.inc(g.dma_start(dbg[0, :, :], LAr[:, 0:T]),
                                       "sDB", 16)
                                bb.inc(g.dma_start(dbg[1, :, :], LAi[:, 0:T]),
                                       "sDB", 16)
                                bb.inc(g.dma_start(dbg[2, :, 0:NCH + 1],
                                                   e0r[:, :]), "sDB", 16)
                                bb.inc(g.dma_start(dbg[2, :, 32:32 + NCH + 1],
                                                   e0i[:, :]), "sDB", 16)
                            bb.mark("sDB", "fin_sO")
                            bb.wait(g, "sDB", "fin_sO")
                        on("gpsimd", f_md)
                        return
                set_bank_free(range(8), "sV", (t1, "nt", ADT * TC - 1))

                sp = f"sP{l}"

                def f_ex(g, l=l, sp=sp):
                    bb.wait(g, "sV", (("sc", l, ADT - 1), "escan"))
                    if bb.mode == "emit":
                        with (g.register(f"fev{l}") as fev,
                              g.register(f"cnt{l}") as cnt,
                              g.register(f"fl{l}") as fl,
                              g.register(f"nd{l}") as nd):
                            g.reg_load(fev, rflag[0:1, 0:1])
                            g.reg_mov(cnt, 0)
                            with g.If_ne(fev, 0):
                                g.dma_start(xch[l, :, :],
                                            sx[:, :]).then_inc(bb.h[sp], 16)
                                g.reg_add(cnt, cnt, 16)
                                g.wait_ge(bb.h[sp], cnt)
                                g.dma_start(xfl[l:l + 1, 0:1],
                                            magict[:, :]).then_inc(bb.h[sp], 16)
                                g.reg_add(cnt, cnt, 16)
                                g.wait_ge(bb.h[sp], cnt)
                            with g.Else():
                                g.reg_mov(nd, 1)
                                with g.While(nd):
                                    g.dma_start(pollt[:, :],
                                                xfl[l:l + 1, 0:1]).then_inc(
                                        bb.h[sp], 16)
                                    g.reg_add(cnt, cnt, 16)
                                    g.wait_ge(bb.h[sp], cnt)
                                    g.reg_load(fl, pollt[0:1, 0:1])
                                    g.reg_mov(nd, MAGIC)
                                    g.reg_sub(nd, nd, fl)
                                g.dma_start(rdat[:, :],
                                            xch[l, :, :]).then_inc(bb.h[sp], 16)
                                g.reg_add(cnt, cnt, 16)
                                g.wait_ge(bb.h[sp], cnt)
                                g.dma_start(xfl[l:l + 1, 0:1],
                                            zerot[:, :]).then_inc(bb.h[sp], 16)
                                g.reg_add(cnt, cnt, 16)
                                g.wait_ge(bb.h[sp], cnt)
                    bb.inc(g.nop(), "sEx", 1, ("exr", l))
                on("gpsimd", f_ex)

                for ct in range(ADT):
                    emit_scan_corr(l, ct)

                t2 = ("aln", l, 2)
                t2 = ("aln", l, 2)

                def f_h2rdy(v, l=l):
                    bb.wait(v, "sV", ("hcor", l, ADT - 1))
                    bb.inc(v.nop(), "sV", 1, ("h2rdy", l))
                on("vector", f_h2rdy)
                emit_attn_ln(t2, lambda dt, l=l: ("h2rdy", l))
                emit_attn_norm_out(t2, f"ln2w_{l}_", f"ln2b_{l}_", AF.Identity,
                                   lambda dt: y2bf[dt])

                def f_w12(g, l=l):
                    if l == 0:
                        bb.wait(g, "sPE", ("win", 1))
                    else:
                        bb.wait(g, "sPE", (("ffnin", l - 1), "f2done"))
                    for k in range(4):
                        bb.inc(g.dma_start(w1bf[:, k * 2048:(k + 1) * 2048],
                                           w1t[l, k, :, :]), "sW12", 16)
                    for k in range(16):
                        bb.inc(g.dma_start(w2bf[:, k * 512:(k + 1) * 512],
                                           w2t[l, k, :, :]), "sW12", 16,
                               ("w12", l) if k == 15 else None)
                on("gpsimd", f_w12)

                tf = ("ffnin", l)
                bcap_f = cap_banks()

                def f_ffn_pe(t, l=l, tf=tf, t2=t2, bcap_f=bcap_f):
                    bb.wait(t, "sW12", ("w12", l))
                    bb.wait(t, "sS", (t2, "outdone"))
                    wait_banks(t, bcap_f)
                    for tc in range(TC):
                        if tc >= 1:
                            bb.wait(t, "sV", (tf, "f2ev", tc - 1))
                        for ft in range(FT):
                            fb = psum[4 + ft % 2]
                            if ft >= 2:
                                bb.wait(t, "sS", (tf, "sil", tc, ft - 2))
                            for k in range(4):
                                li = mm(t, fb[:, :],
                                        w1bf[:, k * 2048 + ft * 128:
                                             k * 2048 + (ft + 1) * 128],
                                        y2bf[k][:, tc * 512:(tc + 1) * 512],
                                        k == 0, k == 3)
                            bb.inc(li, "sPE", 1, (tf, "f1", tc, ft))
                            if ft >= 1:
                                fp = ft - 1
                                silb = (sil0, sil1)[fp % 2]
                                bb.wait(t, "sS", (tf, "sil", tc, fp))
                                for dt in range(ADT):
                                    mm(t, psum[dt][:, :],
                                       w2bf[:, fp * 512 + dt * 128:
                                            fp * 512 + (dt + 1) * 128],
                                       silb[:, :], fp == 0, False)
                        fp = FT - 1
                        silb = (sil0, sil1)[fp % 2]
                        bb.wait(t, "sS", (tf, "sil", tc, fp))
                        for dt in range(ADT):
                            li = mm(t, psum[dt][:, :],
                                    w2bf[:, fp * 512 + dt * 128:
                                         fp * 512 + (dt + 1) * 128],
                                    silb[:, :], False, True)
                        bb.inc(li, "sPE", 1, (tf, "f2", tc))
                    bb.mark("sPE", (tf, "f2done"))
                on("tensor", f_ffn_pe)

                def f_ffn_sil(sc, l=l, tf=tf):
                    for tc in range(TC):
                        for ft in range(FT):
                            silb = (sil0, sil1)[ft % 2]
                            bb.wait(sc, "sPE", (tf, "f1", tc, ft))
                            bb.inc(sc.activation(silb[:, :],
                                                 psum[4 + ft % 2][:, :], AF.Silu,
                                                 bias=pcol(f"b1_{l}_{ft}")),
                                   "sS", 1, (tf, "sil", tc, ft))
                on("scalar", f_ffn_sil)

                def f_ffn_ev(v, l=l, tf=tf):
                    for tc in range(TC):
                        bb.wait(v, "sPE", (tf, "f2", tc))
                        for dt in range(ADT):
                            bb.inc(v.scalar_tensor_tensor(
                                h_t[dt][:, tc * 512:(tc + 1) * 512],
                                psum[dt][:, :], pcol(f"b2_{l}_{dt}"),
                                h_t[dt][:, tc * 512:(tc + 1) * 512],
                                op0=ALU.add, op1=ALU.add),
                                "sV", 1,
                                (tf, "f2ev", tc) if dt == ADT - 1 else None)
                on("vector", f_ffn_ev)
                set_bank_free(range(8), "sV", (tf, "f2ev", TC - 1))

                def f_h3rdy(v, l=l):
                    bb.wait(v, "sV", (("ffnin", l), "f2ev", TC - 1))
                    bb.inc(v.nop(), "sV", 1, ("hrdy", l + 1))
                on("vector", f_h3rdy)

                if stopat == f"l{l}":
                    dump_h(("hrdy", l + 1))
                    return
                if stopat == f"s{l}":
                    dump_h(("hcor", l, ADT - 1))
                    return

            tl = ("aln", "lnl", 0)
            emit_attn_ln(tl, lambda dt: ("hrdy", DEPTH))
            emit_attn_norm_out(tl, "lnlw", "lnlb", AF.Identity, lambda dt: y2bf[dt])

            def f_woutld(g):
                bb.wait(g, "sPE", ("win", 1))
                for k in range(4):
                    bb.inc(g.dma_start(woutbf[:, k * 4096:(k + 1) * 4096],
                                       wout_t[k, :, :]), "sW12", 16,
                           ("woutld",) if k == 3 else None)
            on("gpsimd", f_woutld)

            def f_wout_sync(s):
                prevuse = {}
                for vt in range(DT):
                    for tc in range(TC):
                        m4 = (vt * TC + tc) % 4
                        pk = prevuse.get(m4)
                        if pk is not None:
                            bb.wait(s, "sV", pk[0])
                            bb.wait(s, f"sST{m4}", pk[1])
                        else:
                            bb.wait(s, "sS", (tl, "outdone"))
                        bb.inc(s.dma_start(
                            rs_[m4][:, :],
                            resid_s[vt, :, tc * 512:(tc + 1) * 512]),
                            f"sR{m4}", 16, ("wo_rld", vt, tc))
                        prevuse[m4] = (("wo_ev", vt, tc), ("wo_st", vt, tc))
            on("sync", f_wout_sync)

            bcap_wo = cap_banks()

            def f_wout_pe(t, bcap_wo=bcap_wo):
                bb.wait(t, "sW12", ("woutld",))
                bb.wait(t, "sS", (tl, "outdone"))
                gi = 0
                for vt in range(DT):
                    for tc in range(TC):
                        b_ = gi % 8
                        if gi >= 8:
                            bb.wait(t, "sV", ("wo_ev_g", gi - 8))
                        elif gi == 0:
                            wait_banks(t, bcap_wo)
                        for k in range(4):
                            li = mm(t, psum[b_][:, :],
                                    woutbf[:, k * 4096 + vt * 128:
                                           k * 4096 + (vt + 1) * 128],
                                    y2bf[k][:, tc * 512:(tc + 1) * 512],
                                    k == 0, k == 3)
                        bb.inc(li, "sPE", 1, ("wo_pe", gi))
                        gi += 1
            on("tensor", f_wout_pe)

            def f_wout_ev(v):
                gi = 0
                for vt in range(DT):
                    for tc in range(TC):
                        b_ = gi % 8
                        m4 = gi % 4
                        bb.wait(v, "sPE", ("wo_pe", gi))
                        bb.wait(v, f"sR{gi % 4}", ("wo_rld", vt, tc))
                        bb.inc(v.scalar_tensor_tensor(
                            rs_[m4][:, :], psum[b_][:, :], pcol(f"bout{vt}"),
                            rs_[m4][:, :], op0=ALU.add, op1=ALU.add),
                            "sV", 1, ("wo_ev", vt, tc))
                        bb.mark("sV", ("wo_ev_g", gi))
                        gi += 1
            on("vector", f_wout_ev)

            def f_wout_st(sc):
                for vt in range(DT):
                    for tc in range(TC):
                        bb.wait(sc, "sV", ("wo_ev", vt, tc))
                        bb.inc(sc.dma_start(
                            resid_s[vt, :, tc * 512:(tc + 1) * 512],
                            rs_[(vt * TC + tc) % 4][:, :]),
                            f"sST{(vt * TC + tc) % 4}", 16, ("wo_st", vt, tc))
            on("scalar", f_wout_st)
            set_bank_free(range(8), "sV", ("wo_ev", DT - 1, TC - 1))

        emit_ln("ln1", resid_in)
        if stop_after == "ln":
            def f_snap(e):
                bb.snap(("bar", "x"))
            on("sync", f_snap)

            def f_dumpln(g):
                bb.wait(g, "sV", ("ln1", "done"))
                for dt in range(DT):
                    bb.inc(g.dma_start(dbg[dt, :, :], ybf[:, dt * T:(dt + 1) * T]),
                           "sDB", 16)
                bb.mark("sDB", "fin_sO")
                bb.wait(g, "sDB", "fin_sO")
            on("gpsimd", f_dumpln)
        else:
            emit_gateup("gu1", 1, "ln1")

            def f_snapgu1(e):
                bb.snap(("bar", "gu1"))
            on("sync", f_snapgu1)
            if stop_after == "gu":
                def f_dumpgu(g):
                    bb.bar(g, ("bar", "gu1"))
                    for tc in range(TC):
                        bb.inc(g.dma_start(
                            dbg[tc, :, :, :].rearrange("p k j -> p (k j)"),
                            at_s[tc, :, :, :].rearrange("p k j -> p (k j)")),
                            "sDB", 16)
                    bb.mark("sDB", "fin_sO")
                    bb.wait(g, "sDB", "fin_sO")
                on("gpsimd", f_dumpgu)
            else:
                emit_down("dn1", 1, resid_in, resid_s, ("bar", "gu1"), "sY")

                def f_snapffn1(e):
                    bb.snap(("bar", "ffn1"))
                on("sync", f_snapffn1)
                if stop_after == "ffn1":
                    def f_dumpf1(g):
                        bb.bar(g, ("bar", "ffn1"))
                        for dt in range(DT):
                            bb.inc(g.dma_start(dbg[dt, :, :], resid_s[dt, :, :]),
                                   "sDB", 16)
                        bb.mark("sDB", "fin_sO")
                        bb.wait(g, "sDB", "fin_sO")
                    on("gpsimd", f_dumpf1)
                else:
                    if stop_after in ("win", "l0", "l1", "l2", "l3",
                                      "s0", "s1", "s2", "s3",
                                      "y00", "in00", "tb00"):
                        emit_attn(("bar", "ffn1"), stopat=stop_after)
                        bb.run_plan()
                        for nm in BB.SEMS:
                            bb.h[nm] = es.enter_context(nc.semaphore(nm))

                        @blk.sync
                        def _(s):
                            bb.run_emit("sync", s)

                        @blk.scalar
                        def _(sc):
                            bb.run_emit("scalar", sc)

                        @blk.vector
                        def _(v):
                            bb.run_emit("vector", v)

                        @blk.tensor
                        def _(t):
                            bb.run_emit("tensor", t)

                        @blk.gpsimd
                        def _(g):
                            bb.run_emit("gpsimd", g)
                        return nc
                    emit_attn(("bar", "ffn1"))

                    def f_snapattn(e):
                        bb.snap(("bar", "attn"))
                    on("sync", f_snapattn)
                    if stop_after == "attn":
                        def f_dumpat(g):
                            bb.bar(g, ("bar", "attn"))
                            for dt in range(DT):
                                bb.inc(g.dma_start(dbg[dt, :, :],
                                                   resid_s[dt, :, :]), "sDB", 16)
                            bb.mark("sDB", "fin_sO")
                            bb.wait(g, "sDB", "fin_sO")
                        on("gpsimd", f_dumpat)
                    else:
                        emit_ln("lnF", resid_s, pre_bar=("bar", "attn"))
                        emit_gateup("gu2", 2, "lnF", pre_bar=("bar", "attn"))

                        def f_snapgu2(e):
                            bb.snap(("bar", "gu2"))
                        on("sync", f_snapgu2)
                        emit_down("dn2", 2, resid_s, resid_out,
                                  ("bar", "gu2"), "sO")

                        def f_fin(g):
                            bb.snap(("bar", "fin"))
                            bb.bar(g, ("bar", "fin"))
                        on("gpsimd", f_fin)

        bb.run_plan()
        for nm in BB.SEMS:
            bb.h[nm] = es.enter_context(nc.semaphore(nm))

        @blk.sync
        def _(s):
            bb.run_emit("sync", s)

        @blk.scalar
        def _(sc):
            bb.run_emit("scalar", sc)

        @blk.vector
        def _(v):
            bb.run_emit("vector", v)

        @blk.tensor
        def _(t):
            bb.run_emit("tensor", t)

        @blk.gpsimd
        def _(g):
            bb.run_emit("gpsimd", g)

    return nc


def _in_maps(staged):
    maps = []
    for c in range(8):
        m = {k: staged[k] for k in
             ("wg8", "wu8", "wd8", "gsr", "usr", "dsr", "win_t", "w1t", "w2t",
              "wout_t", "params")}
        m["resid_in"] = staged["_resid_shards"][c]
        ev = 1 - (c % 2)
        m["evenflag"] = np.array([[ev]], np.int32)
        em = np.zeros((128, 2), np.float32)
        em[:, 0] = ev
        em[:, 1] = 1 - ev
        m["emaskin"] = em
        maps.append(m)
    return maps


def kernel(**inputs):
    staged = _stage_inputs(inputs)
    nc = build_program()
    res = run_bass_kernel_spmd(nc, _in_maps(staged), core_ids=list(range(8)))
    out = np.empty((4, 4096, 4096), np.float32)
    for c in range(8):
        b, h = c // 2, c % 2
        sh = np.asarray(res.results[c]["resid_out"]).reshape(4096, T)
        out[b, h * T:(h + 1) * T, :] = sh.T
    return out


# version 04
# speedup vs baseline: 1.0864x; 1.0104x over previous
import numpy as np
import ml_dtypes
from contextlib import ExitStack
import concourse.bass as bass
import concourse.mybir as mybir
from concourse.bass_utils import run_bass_kernel_spmd

F32 = mybir.dt.float32
BF16 = mybir.dt.bfloat16
U8 = mybir.dt.uint8
I32 = mybir.dt.int32
AF = mybir.ActivationFunctionType
ALU = mybir.AluOpType

T = 2048
TC = 4
DT = 32
IT = 86
DIM = 512
ADT = 4
FT = 16
DEPTH = 4
CH = 128
NCH = 16
EPS = 1e-5
MAGIC = 1234567

_P = {}


def _build_pmap():
    idx = [0]

    def add(name):
        _P[name] = idx[0]
        idx[0] += 1

    def addvec(name, ntile):
        for i in range(ntile):
            add(f"{name}{i}")

    _P.clear()
    addvec("bin", 4)
    for l in range(4):
        addvec(f"ln1w_{l}_", 4)
        addvec(f"ln1b_{l}_", 4)
        addvec(f"ln2w_{l}_", 4)
        addvec(f"ln2b_{l}_", 4)
        addvec(f"phr_{l}_", 4)
        addvec(f"phi_{l}_", 4)
        addvec(f"lcr_{l}_", 4)
        addvec(f"lci_{l}_", 4)
        addvec(f"scl_{l}_", 4)
        addvec(f"b1_{l}_", 16)
        addvec(f"b2_{l}_", 4)
    addvec("lnlw", 4)
    addvec("lnlb", 4)
    addvec("bout", 32)
    add("c_neghalf")
    add("c_half")


_build_pmap()



def _stage_inputs(inp):
    out = {}
    x = np.ascontiguousarray(np.asarray(inp["L0_attn_x"], np.float32))

    def lowbyte(a):
        a = np.ascontiguousarray(np.asarray(a, np.int32))
        return np.ascontiguousarray(a.view(np.uint8)[:, ::4])

    gq = lowbyte(inp["L0_ffn_gate_q"])
    uq = lowbyte(inp["L0_ffn_up_q"])
    dq = lowbyte(inp["L0_ffn_down_q"])
    out["wg8"] = np.ascontiguousarray(
        gq.reshape(32, 128, 86, 128).transpose(2, 1, 0, 3)).reshape(86, 128, 4096)
    out["wu8"] = np.ascontiguousarray(
        uq.reshape(32, 128, 86, 128).transpose(2, 1, 0, 3)).reshape(86, 128, 4096)
    out["wd8"] = np.ascontiguousarray(dq.reshape(86, 128, 4096))
    gs = np.asarray(inp["L0_ffn_gate_s"], np.float32)
    us = np.asarray(inp["L0_ffn_up_s"], np.float32)
    out["gsr"] = np.ascontiguousarray(
        gs.reshape(32, 86, 128).transpose(1, 0, 2)).reshape(86, 4096)
    out["usr"] = np.ascontiguousarray(
        us.reshape(32, 86, 128).transpose(1, 0, 2)).reshape(86, 4096)
    out["dsr"] = np.ascontiguousarray(np.asarray(inp["L0_ffn_down_s"], np.float32))
    out["win_t"] = np.ascontiguousarray(
        np.asarray(inp["L0_attn_Win"], np.float32).T).reshape(32, 128, 512)
    out["w1t"] = np.ascontiguousarray(
        np.transpose(np.asarray(inp["L0_attn_W1"], np.float32), (0, 2, 1))
    ).reshape(4, 4, 128, 2048)
    out["w2t"] = np.ascontiguousarray(
        np.transpose(np.asarray(inp["L0_attn_W2"], np.float32), (0, 2, 1))
    ).reshape(4, 16, 128, DIM)
    out["wout_t"] = np.ascontiguousarray(
        np.asarray(inp["L0_attn_Wout"], np.float32).T).reshape(4, 128, 4096)
    out["params"] = _pack_params(inp)
    shards = []
    for c in range(8):
        b, h = c // 2, c % 2
        xt = np.ascontiguousarray(x[b, h * T:(h + 1) * T, :].T)
        shards.append(np.ascontiguousarray(xt.reshape(32, 128, T)))
    out["_resid_shards"] = shards
    return out


def _pack_params(inp):
    arr = np.zeros((128, 288), np.float32)

    def put(name, vec128):
        arr[:, _P[name]] = np.asarray(vec128, np.float32).reshape(128)

    def putvec(name, v, ntile):
        v = np.asarray(v, np.float32).reshape(ntile, 128)
        for i in range(ntile):
            put(f"{name}{i}", v[i])

    putvec("bin", inp["L0_attn_bin"], 4)
    for l in range(DEPTH):
        putvec(f"ln1w_{l}_", inp["L0_attn_ln1_w"][l], 4)
        putvec(f"ln1b_{l}_", inp["L0_attn_ln1_b"][l], 4)
        putvec(f"ln2w_{l}_", inp["L0_attn_ln2_w"][l], 4)
        putvec(f"ln2b_{l}_", inp["L0_attn_ln2_b"][l], 4)
        putvec(f"phr_{l}_", inp["L0_attn_phz_re"][l], 4)
        putvec(f"phi_{l}_", inp["L0_attn_phz_im"][l], 4)
        putvec(f"lcr_{l}_", inp["L0_attn_lc_re"][l], 4)
        putvec(f"lci_{l}_", inp["L0_attn_lc_im"][l], 4)
        putvec(f"scl_{l}_", inp["L0_attn_sc_lin"][l], 4)
        putvec(f"b1_{l}_", inp["L0_attn_b1"][l], 16)
        putvec(f"b2_{l}_", inp["L0_attn_b2"][l], 4)
    putvec("lnlw", inp["L0_attn_lnl_w"], 4)
    putvec("lnlb", inp["L0_attn_lnl_b"], 4)
    putvec("bout", inp["L0_attn_bout"], 32)
    put("c_neghalf", np.full(128, -0.5))
    put("c_half", np.full(128, 0.5))
    return arr



class _Any:
    def __call__(self, *a, **k):
        return _Any()

    def __getattr__(self, nm):
        return _Any()

    def __enter__(self):
        return _Any()

    def __exit__(self, *a):
        return False


class BB:
    SEMS = ("sWa", "sWb", "sBg", "sBu", "sX", "sXa", "sXb", "sYa", "sYb",
            "sAa", "sAb", "sLB", "sLC", "sWc", "sWd", "sYc", "sYd", "sBd",
            "sLA", "sLN0", "sLN1", "sW12", "sDB",
            "sR0", "sR1", "sR2", "sR3", "sST0", "sST1", "sST2", "sST3",
            "sPE", "sV", "sS", "sEx", "sP0", "sP1", "sP2", "sP3")

    def __init__(self):
        self.n = {s: 0 for s in self.SEMS}
        self.h = {}
        self.L = {}
        self.seq = []
        self.mode = "plan"

    def on(self, eng, f):
        self.seq.append((eng, f))

    def inc(self, inst, nm, amt, key=None):
        if self.mode == "plan":
            self.n[nm] += amt
            if key is not None:
                self.L[key] = self.n[nm]
        else:
            inst.then_inc(self.h[nm], amt)

    def mark(self, nm, key):
        if self.mode == "plan":
            self.L[key] = self.n[nm]

    def wait(self, e, nm, key):
        if self.mode == "emit":
            v = key if isinstance(key, int) else self.L[key]
            if v > 0:
                e.wait_ge(self.h[nm], v)

    def snap(self, key):
        if self.mode == "plan":
            self.L[key] = dict(self.n)

    def bar(self, e, key):
        if self.mode == "emit":
            for nm, v in self.L[key].items():
                if v > 0:
                    e.wait_ge(self.h[nm], v)

    def run_plan(self):
        self.mode = "plan"
        fake = _Any()
        for eng, f in self.seq:
            f(fake)

    def run_emit(self, eng, e):
        self.mode = "emit"
        for en, f in self.seq:
            if en == eng:
                f(e)


def build_program(stop_after="full", debug_dump=None, det=False):
    nc = bass.Bass("TRN2", target_bir_lowering=False,
                   detect_race_conditions=det)
    NP = 288

    resid_in = nc.dram_tensor("resid_in", [DT, 128, T], F32, kind="ExternalInput").ap()
    wg8 = nc.dram_tensor("wg8", [IT, 128, 4096], U8, kind="ExternalInput").ap()
    wu8 = nc.dram_tensor("wu8", [IT, 128, 4096], U8, kind="ExternalInput").ap()
    wd8 = nc.dram_tensor("wd8", [IT, 128, 4096], U8, kind="ExternalInput").ap()
    gsr = nc.dram_tensor("gsr", [IT, 4096], F32, kind="ExternalInput").ap()
    usr = nc.dram_tensor("usr", [IT, 4096], F32, kind="ExternalInput").ap()
    dsr = nc.dram_tensor("dsr", [IT, 4096], F32, kind="ExternalInput").ap()
    win_t = nc.dram_tensor("win_t", [DT, 128, DIM], F32, kind="ExternalInput").ap()
    w1t = nc.dram_tensor("w1t", [DEPTH, 4, 128, 2048], F32, kind="ExternalInput").ap()
    w2t = nc.dram_tensor("w2t", [DEPTH, 16, 128, DIM], F32, kind="ExternalInput").ap()
    wout_t = nc.dram_tensor("wout_t", [DEPTH, 128, 4096], F32,
                            kind="ExternalInput").ap()
    params = nc.dram_tensor("params", [128, NP], F32, kind="ExternalInput").ap()
    evenflag = nc.dram_tensor("evenflag", [1, 1], I32, kind="ExternalInput").ap()
    emaskin = nc.dram_tensor("emaskin", [128, 2], F32, kind="ExternalInput").ap()

    resid_out = nc.dram_tensor("resid_out", [DT, 128, T], F32,
                               kind="ExternalOutput").ap()

    wgbf = nc.dram_tensor("wgbf", [IT, 128, 4096], BF16, kind="Internal").ap()
    wubf = nc.dram_tensor("wubf", [IT, 128, 4096], BF16, kind="Internal").ap()
    wdbf = nc.dram_tensor("wdbf", [IT, 128, 4096], BF16, kind="Internal").ap()
    at_s = nc.dram_tensor("at_s", [TC, 128, IT, 512], BF16, kind="Internal").ap()
    resid_s = nc.dram_tensor("resid_s", [DT, 128, T], F32, kind="Internal").ap()
    xch = nc.dram_tensor("xch", [DEPTH, 128, 8], F32, kind="Internal",
                         addr_space="Shared").ap()
    xfl = nc.dram_tensor("xfl", [DEPTH, 1], I32, kind="Internal",
                         addr_space="Shared").ap()
    dbg = None
    if debug_dump:
        shp = {"ybf": ([DT, 128, T], BF16), "at": ([TC, 128, IT, 512], BF16),
               "resid": ([DT, 128, T], F32), "h": ([ADT, 128, T], F32),
               "yct": ([1, 128, T], F32)}[debug_dump]
        dbg = nc.dram_tensor("dbg", shp[0], shp[1], kind="ExternalOutput").ap()

    bb = BB()
    L = bb.L
    on = bb.on
    es = ExitStack()
    with es:
        def sbt(name, shape, dt):
            return es.enter_context(nc.sbuf_tensor(name, shape, dt))

        par = sbt("par", [128, NP], F32)
        ones_k = sbt("ones_k", [128, 1], BF16)
        ones_b = sbt("ones_b", [1, 128], F32)
        rowsA = sbt("rowsA", [128, T], F32)
        rowsB = sbt("rowsB", [128, T], F32)
        r_rstd = rowsA[0:1, :]
        r_sq = rowsA[32:33, :]
        r_mean = rowsB[0:1, :]
        r_tmp = rowsB[32:33, :]
        magict = sbt("magict", [1, 1], I32)
        zerot = sbt("zerot", [1, 1], I32)
        rflag = sbt("rflag", [1, 1], I32)
        pollt = sbt("pollt", [1, 1], I32)
        emask = sbt("emask", [128, 2], F32)
        sx = sbt("sx", [128, 8], F32)
        rdat = sbt("rdat", [128, 8], F32)
        pwt = sbt("pwt", [128, 128], F32)
        prgs_r = sbt("prgs_r", [128, 512], F32)
        prgs_i = sbt("prgs_i", [128, 512], F32)
        nrd = sbt("nrd", [128, 8], F32)
        e0r = sbt("e0r", [128, NCH + 1], F32)
        e0i = sbt("e0i", [128, NCH + 1], F32)
        e1r = sbt("e1r", [128, NCH + 1], F32)
        e1i = sbt("e1i", [128, NCH + 1], F32)
        esn = sbt("esn", [128, NCH + 1], F32)
        s64a = sbt("s64a", [128, 64], F32)
        s64b = sbt("s64b", [128, 64], F32)

        ybf = sbt("ybf", [128, DT * T], BF16)
        bufA = sbt("bufA", [128, 4096], BF16)
        bufB = sbt("bufB", [128, 4096], BF16)
        bufC = sbt("bufC", [128, 4096], BF16)
        bufD = sbt("bufD", [128, 4096], BF16)
        sb_g = sbt("sb_g", [128, 2048], BF16)
        sb_u = sbt("sb_u", [128, 2048], BF16)
        sq0 = sbt("sq0", [128, T], BF16)
        sq1 = sbt("sq1", [128, T], BF16)
        rs_ = [sbt(f"rs{i}", [128, 512], F32) for i in range(4)]
        at0, at1 = sq0, sq1
        sil0 = rs_[3][:, :].bitcast(BF16)[:, 0:512]
        sil1 = rs_[3][:, :].bitcast(BF16)[:, 512:1024]

        psum = [es.enter_context(nc.psum_tensor(f"ps{i}", [128, 512], F32))
                for i in range(8)]

        def yb(c0, n):
            return ybf[:, c0:c0 + n]

        h_t = [b_[:, :].bitcast(F32) for b_ in (bufA, bufB, bufC, bufD)]
        y2bf = [yb(i * T, T) for i in range(ADT)]
        LAr = yb(8192, 4096).bitcast(F32)
        LBr = yb(12288, 4096).bitcast(F32)
        LAi = yb(16384, 4096).bitcast(F32)
        LBi = yb(20480, 4096).bitcast(F32)
        y_ct = yb(24576, 4096).bitcast(F32)
        w1bf = yb(28672, 8192)
        w2bf = yb(36864, 8192)
        woutbf = yb(45056, 16384)
        hbf = yb(61440, 2048)
        hbf2 = yb(63488, 2048)
        at_res = ybf[:, 0:IT * 512]


        blk = es.enter_context(nc.Block())

        def pcol(name):
            return par[:, _P[name]:_P[name] + 1]

        def pcol1(name):
            return par[0:1, _P[name]:_P[name] + 1]

        def mm(t, out, lhsT, rhs, start, stop):
            return t.matmul(out, lhsT, rhs, start=start, stop=stop,
                            skip_group_check=True)

        bank_free = [("sV", "p0_const")] * 8

        def cap_banks(idxs=range(8)):
            return [tuple(bank_free[b_]) for b_ in idxs]

        def wait_banks(t, cap):
            if bb.mode == "emit":
                seen = set()
                for nm, key in cap:
                    v = key if isinstance(key, int) else L.get(key, 0)
                    if v > 0 and (nm, v) not in seen:
                        seen.add((nm, v))
                        t.wait_ge(bb.h[nm], v)

        def set_bank_free(idxs, nm, key):
            for b_ in idxs:
                bank_free[b_] = (nm, key)

        def p0_sync(s):
            bb.inc(s.dma_start(par[:, :], params[:, :]), "sX", 16)
            bb.inc(s.dma_start(emask[:, :], emaskin[:, :]), "sX", 16)
            bb.inc(s.dma_start(rflag[:, :], evenflag[:, :]), "sX", 16, "p0_ld")

        def p0_vec(v):
            bb.wait(v, "sX", "p0_ld")
            v.memset(ones_k[:, :], 1.0)
            v.memset(ones_b[:, :], 1.0)
            v.memset(magict[:, :], MAGIC)
            v.memset(rdat[:, :], 0.0)
            bb.inc(v.memset(zerot[:, :], 0), "sV", 1, "p0_const")

        on("sync", p0_sync)
        on("vector", p0_vec)

        def emit_ln(tag, src, pre_bar=None):
            def f_gp(g):
                bb.wait(g, "sV", "p0_const")
                if pre_bar:
                    bb.bar(g, pre_bar)
                for dt in range(DT):
                    if dt >= 2:
                        bb.wait(g, "sPE", (tag, "st", dt - 2))
                    bb.inc(g.dma_start(ybf[:, dt * T:(dt + 1) * T], src[dt, :, :]),
                           ("sLN0", "sLN1")[dt % 2], 16, (tag, "x", dt))
            on("gpsimd", f_gp)

            def f_sq(sc):
                for dt in range(DT):
                    sq = (sq0, sq1)[dt % 2]
                    bb.wait(sc, ("sLN0", "sLN1")[dt % 2], (tag, "x", dt))
                    if dt >= 2:
                        bb.wait(sc, "sPE", (tag, "st", dt - 2))
                    bb.inc(sc.square(sq[:, :], ybf[:, dt * T:(dt + 1) * T]),
                           "sS", 1, (tag, "sq", dt))
            on("scalar", f_sq)

            bcap = cap_banks()

            def f_pe(t):
                bb.wait(t, "sV", "p0_const")
                for dt in range(DT):
                    sq = (sq0, sq1)[dt % 2]
                    bb.wait(t, "sS", (tag, "sq", dt))
                    if dt == 0:
                        wait_banks(t, bcap)
                    for tc in range(TC):
                        mm(t, psum[tc][0:1, :], ones_k[:, :],
                           ybf[:, dt * T + tc * 512:dt * T + (tc + 1) * 512],
                           dt == 0, dt == DT - 1)
                        li = mm(t, psum[4 + tc][0:1, :], ones_k[:, :],
                                sq[:, tc * 512:(tc + 1) * 512],
                                dt == 0, dt == DT - 1)
                    bb.inc(li, "sPE", 1, (tag, "st", dt))
            on("tensor", f_pe)

            def f_v1(v):
                bb.wait(v, "sPE", (tag, "st", DT - 1))
                for tc in range(TC):
                    v.tensor_copy(r_mean[0:1, tc * 512:(tc + 1) * 512],
                                  psum[tc][0:1, :])
                    v.tensor_copy(r_sq[0:1, tc * 512:(tc + 1) * 512],
                                  psum[4 + tc][0:1, :])
                v.tensor_scalar_mul(r_mean[0:1, :], r_mean[0:1, :], 1.0 / 4096.0)
                bb.inc(v.tensor_scalar_mul(r_sq[0:1, :], r_sq[0:1, :],
                                           1.0 / 4096.0),
                       "sV", 1, (tag, "ex2"))
            on("vector", f_v1)

            def f_msq(sc):
                bb.wait(sc, "sV", (tag, "ex2"))
                for tc in range(TC):
                    bb.inc(sc.square(psum[4 + tc][64:65, :],
                                     r_mean[0:1, tc * 512:(tc + 1) * 512]),
                           "sS", 1, (tag, "msq", tc))
            on("scalar", f_msq)

            def f_var(v):
                bb.wait(v, "sS", (tag, "msq", TC - 1))
                for tc in range(TC):
                    v.tensor_sub(r_sq[0:1, tc * 512:(tc + 1) * 512],
                                 r_sq[0:1, tc * 512:(tc + 1) * 512],
                                 psum[4 + tc][64:65, :])
                bb.inc(v.tensor_scalar_add(r_sq[0:1, :], r_sq[0:1, :], EPS),
                       "sV", 1, (tag, "var"))
            on("vector", f_var)

            def f_sc2(sc):
                bb.wait(sc, "sV", (tag, "var"))
                sc.activation(r_tmp[0:1, :], r_sq[0:1, :], AF.Ln)
                bb.inc(sc.activation(r_rstd[0:1, :], r_tmp[0:1, :], AF.Exp,
                                     scale=pcol1("c_neghalf")),
                       "sS", 1, (tag, "rstd"))
            on("scalar", f_sc2)

            def f_pe2(t):
                bb.wait(t, "sS", (tag, "rstd"))
                for tc in range(TC):
                    mm(t, psum[tc][:, :], ones_b[0:1, :],
                       r_rstd[0:1, tc * 512:(tc + 1) * 512], True, True)
                    li = mm(t, psum[4 + tc][:, :], ones_b[0:1, :],
                            r_mean[0:1, tc * 512:(tc + 1) * 512], True, True)
                bb.inc(li, "sPE", 1, (tag, "bcast"))
            on("tensor", f_pe2)

            def f_v3(v):
                bb.wait(v, "sPE", (tag, "bcast"))
                for dt in range(DT):
                    for tc in range(TC):
                        s5 = slice(dt * T + tc * 512, dt * T + (tc + 1) * 512)
                        v.tensor_sub(sq1[:, tc * 512:(tc + 1) * 512],
                                     ybf[:, s5], psum[4 + tc][:, :])
                        li = v.tensor_mul(ybf[:, s5],
                                          sq1[:, tc * 512:(tc + 1) * 512],
                                          psum[tc][:, :])
                bb.inc(li, "sV", 1, (tag, "done"))
            on("vector", f_v3)
            set_bank_free(range(8), "sV", (tag, "done"))

        def emit_gateup(tag, call, ln_tag, pre_bar=None):
            parbufs = [(bufA, bufC), (bufB, bufD)]

            if call == 1:
                def f_gp(g):
                    if pre_bar:
                        bb.bar(g, pre_bar)
                    for k in range(IT):
                        gq, uq = parbufs[k % 2]
                        if k >= 2:
                            bb.wait(g, "sPE", (tag, "udone", k - 2))
                            bb.wait(g, ("sYa", "sYb")[k % 2], (tag, "wb", k - 2))
                        sw = ("sWa", "sWb")[k % 2]
                        bb.inc(g.dma_start(gq[:, :], wg8[k, :, :]), sw, 16)
                        bb.inc(g.dma_start(uq[:, :], wu8[k, :, :]), sw, 16,
                               (tag, "cast", k))
                        for hh in range(2):
                            if k >= 1 or hh == 1:
                                pk, ph = (k, 0) if hh == 1 else (k - 1, 1)
                                bb.wait(g, "sV", (tag, "dqg", pk, ph))
                                bb.wait(g, "sV", (tag, "dqu", pk, ph))
                            src_g = bass.AP(gsr.tensor, k * 4096 + hh * 2048,
                                            [[0, 128], [1, 2048]])
                            src_u = bass.AP(usr.tensor, k * 4096 + hh * 2048,
                                            [[0, 128], [1, 2048]])
                            bb.inc(g.dma_start(sb_g[:, :], src_g), "sBg", 16,
                                   (tag, "bcg", k, hh))
                            bb.inc(g.dma_start(sb_u[:, :], src_u), "sBu", 16,
                                   (tag, "bcu", k, hh))
                        for j in range(4):
                            q = 4 * k + j
                            wq = (rs_[0], rs_[2])[q % 2][:, :].bitcast(BF16)
                            if q >= 2:
                                bb.wait(g, ("sYc", "sYd")[q % 2],
                                        (tag, "wdw", q - 2))
                            bb.inc(g.dma_start(wq, wd8[k, :, j * 1024:
                                                       (j + 1) * 1024]),
                                   ("sWc", "sWd")[q % 2], 16, (tag, "wdc", q))
                            if q >= 1:
                                bb.wait(g, "sV", (tag, "wds", q - 1))
                            src_d = bass.AP(dsr.tensor, k * 4096 + j * 1024,
                                            [[0, 128], [1, 1024]])
                            bb.inc(g.dma_start(rs_[1][:, :].bitcast(BF16),
                                               src_d), "sBd", 16,
                                   (tag, "wdb", q))
                on("gpsimd", f_gp)

                def f_vdq_atm(v):
                    def atm(k):
                        att = (at0, at1)[k % 2]
                        bb.wait(v, "sPE", (tag, "udone", k))
                        if k >= 2:
                            bb.wait(v, ("sAa", "sAb")[k % 2], (tag, "ats", k - 2))
                        for tc in range(TC):
                            sil = (sil0, sil1)[tc % 2]
                            bb.wait(v, "sS", (tag, "silt", k, tc))
                            bb.inc(v.tensor_mul(att[:, tc * 512:(tc + 1) * 512],
                                                sil[:, :], psum[4 + tc][:, :]),
                                   "sV", 1, (tag, "atmt", k, tc))
                    for k in range(IT):
                        gq, uq = parbufs[k % 2]
                        bb.wait(v, ("sWa", "sWb")[k % 2], (tag, "cast", k))
                        for hh in range(2):
                            sl = slice(hh * 2048, (hh + 1) * 2048)
                            bb.wait(v, "sBg", (tag, "bcg", k, hh))
                            bb.inc(v.scalar_tensor_tensor(
                                gq[:, sl], gq[:, sl], 8.0, sb_g[:, :],
                                op0=ALU.subtract, op1=ALU.mult),
                                "sV", 1, (tag, "dqg", k, hh))
                            bb.wait(v, "sBu", (tag, "bcu", k, hh))
                            bb.inc(v.scalar_tensor_tensor(
                                uq[:, sl], uq[:, sl], 8.0, sb_u[:, :],
                                op0=ALU.subtract, op1=ALU.mult),
                                "sV", 1, (tag, "dqu", k, hh))
                        bb.mark("sV", (tag, "dq", k))
                        if k >= 1:
                            atm(k - 1)
                        for j in range(4):
                            q = 4 * k + j
                            wq = (rs_[0], rs_[2])[q % 2][:, :].bitcast(BF16)
                            bb.wait(v, ("sWc", "sWd")[q % 2], (tag, "wdc", q))
                            bb.wait(v, "sBd", (tag, "wdb", q))
                            bb.inc(v.scalar_tensor_tensor(
                                wq, wq, 8.0, rs_[1][:, :].bitcast(BF16),
                                op0=ALU.subtract, op1=ALU.mult),
                                "sV", 1, (tag, "wds", q))
                    atm(IT - 1)
                on("vector", f_vdq_atm)

                def f_wb_sil(sc):
                    for k in range(IT):
                        gq, uq = parbufs[k % 2]
                        bb.wait(sc, "sV", (tag, "dq", k))
                        sy = ("sYa", "sYb")[k % 2]
                        bb.inc(sc.dma_start(wgbf[k, :, :], gq[:, :]), sy, 16)
                        bb.inc(sc.dma_start(wubf[k, :, :], uq[:, :]), sy, 16,
                               (tag, "wb", k))
                        for j in range(4):
                            q = 4 * k + j
                            wq = (rs_[0], rs_[2])[q % 2][:, :].bitcast(BF16)
                            bb.wait(sc, "sV", (tag, "wds", q))
                            if q >= 2:
                                bb.wait(sc, ("sYc", "sYd")[q % 2],
                                        (tag, "wdw", q - 2))
                            bb.inc(sc.dma_start(
                                wdbf[k, :, j * 1024:(j + 1) * 1024], wq),
                                ("sYc", "sYd")[q % 2], 16, (tag, "wdw", q))
                        bb.wait(sc, "sPE", (tag, "gdone", k))
                        for tc in range(TC):
                            sil = (sil0, sil1)[tc % 2]
                            if k > 0 or tc >= 2:
                                kk2, tc2 = ((k, tc - 2) if tc >= 2
                                            else (k - 1, tc + 2))
                                bb.wait(sc, "sV", (tag, "atmt", kk2, tc2))
                            bb.inc(sc.activation(sil[:, :], psum[tc][:, :],
                                                 AF.Silu),
                                   "sS", 1, (tag, "silt", k, tc))
                on("scalar", f_wb_sil)
            else:
                def f_ld(s):
                    if pre_bar:
                        bb.bar(s, pre_bar)
                    for k in range(IT):
                        gq, uq = parbufs[k % 2]
                        if k >= 2:
                            bb.wait(s, "sPE", (tag, "udone", k - 2))
                        sw = ("sXa", "sXb")[k % 2]
                        bb.inc(s.dma_start(gq[:, :], wgbf[k, :, :]), sw, 16)
                        bb.inc(s.dma_start(uq[:, :], wubf[k, :, :]), sw, 16,
                               (tag, "ld", k))
                on("sync", f_ld)

                def f_atm2(v):
                    for k in range(IT):
                        att = (at0, at1)[k % 2]
                        bb.wait(v, "sPE", (tag, "udone", k))
                        if k >= 2:
                            bb.wait(v, ("sAa", "sAb")[k % 2], (tag, "ats", k - 2))
                        for tc in range(TC):
                            sil = (sil0, sil1)[tc % 2]
                            bb.wait(v, "sS", (tag, "silt", k, tc))
                            bb.inc(v.tensor_mul(att[:, tc * 512:(tc + 1) * 512],
                                                sil[:, :], psum[4 + tc][:, :]),
                                   "sV", 1, (tag, "atmt", k, tc))
                on("vector", f_atm2)

                def f_sil_ats(sc):
                    for k in range(IT):
                        att = (at0, at1)[k % 2]
                        bb.wait(sc, "sPE", (tag, "gdone", k))
                        for tc in range(TC):
                            sil = (sil0, sil1)[tc % 2]
                            if k > 0 or tc >= 2:
                                kk2, tc2 = ((k, tc - 2) if tc >= 2
                                            else (k - 1, tc + 2))
                                bb.wait(sc, "sV", (tag, "atmt", kk2, tc2))
                            bb.inc(sc.activation(sil[:, :], psum[tc][:, :],
                                                 AF.Silu),
                                   "sS", 1, (tag, "silt", k, tc))
                        if k >= 1:
                            for tc in range(TC):
                                bb.wait(sc, "sV", (tag, "atmt", k - 1, tc))
                                bb.inc(sc.dma_start(
                                    at_s[tc, :, k - 1, :],
                                    (at0, at1)[(k - 1) % 2]
                                    [:, tc * 512:(tc + 1) * 512]),
                                    ("sAa", "sAb")[(k - 1) % 2], 16,
                                    (tag, "ats", k - 1) if tc == TC - 1
                                    else None)
                    for tc in range(TC):
                        bb.wait(sc, "sV", (tag, "atmt", IT - 1, tc))
                        bb.inc(sc.dma_start(
                            at_s[tc, :, IT - 1, :],
                            (at0, at1)[(IT - 1) % 2][:, tc * 512:(tc + 1) * 512]),
                            ("sAa", "sAb")[(IT - 1) % 2], 16,
                            (tag, "ats", IT - 1) if tc == TC - 1 else None)
                on("scalar", f_sil_ats)

            bcap = cap_banks()

            def f_pe(t):
                bb.wait(t, "sV", (ln_tag, "done"))
                for k in range(IT):
                    gq, uq = parbufs[k % 2]
                    if call == 1:
                        bb.wait(t, "sV", (tag, "dq", k))
                    else:
                        bb.wait(t, ("sXa", "sXb")[k % 2], (tag, "ld", k))
                    if k == 0:
                        wait_banks(t, bcap)
                    if k >= 1:
                        bb.wait(t, "sS", (tag, "silt", k - 1, TC - 1))
                    for kk in range(DT):
                        wsl = gq[:, kk * 128:(kk + 1) * 128]
                        for tc in range(TC):
                            li = mm(t, psum[tc][:, :], wsl,
                                    ybf[:, kk * T + tc * 512:
                                        kk * T + (tc + 1) * 512],
                                    kk == 0, kk == DT - 1)
                    bb.inc(li, "sPE", 1, (tag, "gdone", k))
                    if k >= 1:
                        bb.wait(t, "sV", (tag, "atmt", k - 1, TC - 1))
                    for kk in range(DT):
                        wsl = uq[:, kk * 128:(kk + 1) * 128]
                        for tc in range(TC):
                            li = mm(t, psum[4 + tc][:, :], wsl,
                                    ybf[:, kk * T + tc * 512:
                                        kk * T + (tc + 1) * 512],
                                    kk == 0, kk == DT - 1)
                    bb.inc(li, "sPE", 1, (tag, "udone", k))
            on("tensor", f_pe)

            if call == 1:
                def f_ats(e):
                    for k in range(IT):
                        att = (at0, at1)[k % 2]
                        for tc in range(TC):
                            bb.wait(e, "sV", (tag, "atmt", k, tc))
                            bb.inc(e.dma_start(at_s[tc, :, k, :],
                                               att[:, tc * 512:(tc + 1) * 512]),
                                   ("sAa", "sAb")[k % 2], 16,
                                   (tag, "ats", k) if tc == TC - 1 else None)
                on("sync", f_ats)

            set_bank_free(range(4), "sS", (tag, "silt", IT - 1, TC - 1))
            set_bank_free(range(4, 8), "sV", (tag, "atmt", IT - 1, TC - 1))

        def emit_down(tag, call, rsrc, rdst, pre_bar, out_sem):
            parw = [bufA, bufB, bufC, bufD]
            sbb = [sb_g, sb_u]
            kblocks = []
            kk = 0
            while kk < IT:
                kblocks.append((kk, min(2, IT - kk)))
                kk += 2
            NKB = len(kblocks)
            dqmode = False

            def wd_re(ap3):
                return ap3.rearrange("k p j -> p k j")

            KH = 43
            slots = [ybf[:, 0:KH * 512], ybf[:, KH * 512:2 * KH * 512], None]

            def slot_tile(s, t):
                if s < 2:
                    return slots[s][:, t * 512:(t + 1) * 512]
                if t < 42:
                    return ybf[:, (86 + t) * 512:(86 + t + 1) * 512]
                return sq0[:, 0:512]

            def slotA(tc):
                return (2 * tc) % 3

            def slotB(tc):
                return (2 * tc + 1) % 3

            def at_rd(tc, k):
                if k < KH:
                    return slot_tile(slotA(tc), k)
                return slot_tile(slotB(tc), k - KH)

            def lda_emit(e, sem, tc, half, key):
                s = slotA(tc) if half == 0 else slotB(tc)
                k0 = 0 if half == 0 else KH
                if s < 2:
                    bb.inc(e.dma_start(
                        slots[s][:, :],
                        at_s[tc, :, k0:k0 + KH, :].rearrange("p k j -> p (k j)")),
                        sem, 16, key)
                else:
                    bb.inc(e.dma_start(
                        ybf[:, 86 * 512:128 * 512],
                        at_s[tc, :, k0:k0 + 42, :].rearrange("p k j -> p (k j)")),
                        sem, 16)
                    bb.inc(e.dma_start(sq0[:, 0:512], at_s[tc, :, k0 + 42, :]),
                           sem, 16, key)

            def f_sync(s):
                bb.bar(s, pre_bar)
                lda_emit(s, "sLA", 0, 0, (tag, "ldA", 0))
                lda_emit(s, "sLC", 0, 1, None)
                lda_emit(s, "sLC", 1, 0, (tag, "ldC0"))
                prevuse = {}
                for tc in range(TC):
                    for o in range(4):
                        for m in range(8):
                            dt = o * 8 + m
                            rs = rs_[m % 4]
                            pk = prevuse.get(m % 4)
                            if pk is not None:
                                bb.wait(s, "sV", pk[0])
                                bb.wait(s, f"sST{m % 4}", pk[1])
                            bb.inc(s.dma_start(
                                rs[:, :], rsrc[dt, :, tc * 512:(tc + 1) * 512]),
                                f"sR{m % 4}", 16, (tag, "rld", tc, o, m))
                            prevuse[m % 4] = ((tag, "evac", tc, o, m),
                                              (tag, "ost", tc, o, m))
            on("sync", f_sync)

            def f_ldw(g):
                if pre_bar:
                    bb.bar(g, pre_bar)
                bidx = 0
                for tc in range(TC):
                    for o in range(4):
                        for bi, (k0, nk) in enumerate(kblocks):
                            wbuf = parw[bidx % 4]
                            if bidx >= 4:
                                bb.wait(g, "sPE", (tag, "wused", bidx - 4))
                            sw = ("sWa", "sWb", "sWc", "sWd")[bidx % 4]
                            if dqmode and tc == 0:
                                bb.inc(g.dma_start(
                                    wbuf[:, 0:nk * 1024],
                                    wd_re(wd8)[:, k0:k0 + nk,
                                               o * 1024:(o + 1) * 1024]),
                                    sw, 16, (tag, "wld", bidx))
                                if bidx >= 2:
                                    bb.wait(g, "sV", (tag, "wdqp", bidx - 2))
                                src_s = bass.AP(
                                    dsr.tensor, k0 * 4096 + o * 1024,
                                    [[0, 128], [4096, nk], [1, 1024]])
                                bb.inc(g.dma_start(
                                    sbb[bidx % 2][:, 0:nk * 1024], src_s),
                                    ("sBg", "sBu")[bidx % 2], 16,
                                    (tag, "wbc", bidx))
                            else:
                                if dqmode:
                                    bb.wait(g, ("sYa", "sYb", "sYc", "sYd")
                                            [(o * NKB + bi) % 4],
                                            (tag, "wwb", o * NKB + bi))
                                bb.inc(g.dma_start(
                                    wbuf[:, 0:nk * 1024],
                                    wd_re(wdbf)[:, k0:k0 + nk,
                                                o * 1024:(o + 1) * 1024]),
                                    sw, 16, (tag, "wld", bidx))
                            bidx += 1
            on("gpsimd", f_ldw)

            def f_wdq_ev(v):
                for tc in range(TC):
                    for o in range(4):
                        if dqmode and tc == 0:
                            for bi, (k0, nk) in enumerate(kblocks):
                                bidx = o * NKB + bi
                                wbuf = parw[bidx % 4]
                                bb.wait(v, ("sWa", "sWb", "sWc", "sWd")
                                        [bidx % 4], (tag, "wld", bidx))
                                bb.wait(v, ("sBg", "sBu")[bidx % 2],
                                        (tag, "wbc", bidx))
                                bb.inc(v.scalar_tensor_tensor(
                                    wbuf[:, 0:nk * 1024],
                                    wbuf[:, 0:nk * 1024], 8.0,
                                    sbb[bidx % 2][:, 0:nk * 1024],
                                    op0=ALU.subtract, op1=ALU.mult),
                                    "sV", 1, (tag, "wdqp", bidx))
                        bb.wait(v, "sPE", (tag, "odone", tc, o))
                        for m in range(8):
                            rs = rs_[m % 4]
                            bb.wait(v, f"sR{m % 4}", (tag, "rld", tc, o, m))
                            bb.inc(v.scalar_tensor_tensor(
                                rs[:, :], psum[m][:, :], 0.5, rs[:, :],
                                op0=ALU.mult, op1=ALU.add),
                                "sV", 1, (tag, "evac", tc, o, m))
            on("vector", f_wdq_ev)

            bcap = cap_banks()

            def f_pe(t):
                bidx = 0
                for tc in range(TC):
                    awaits = {0: ("sLA", (tag, "ldA", 0)),
                              1: ("sLC", (tag, "ldC0")),
                              2: ("sLB", (tag, "ldpair", 0)),
                              3: ("sLB", (tag, "ldpair", 1))}[tc]
                    bwaits = {0: ("sLC", (tag, "ldC0")),
                              1: ("sLB", (tag, "ldpair", 0)),
                              2: ("sLB", (tag, "ldpair", 1)),
                              3: ("sLB", (tag, "ldpair", 2))}[tc]
                    bb.wait(t, awaits[0], awaits[1])
                    for o in range(4):
                        if (tc, o) == (0, 0):
                            wait_banks(t, bcap)
                        else:
                            po = (tc, o - 1) if o > 0 else (tc - 1, 3)
                            bb.wait(t, "sV", (tag, "evac", po[0], po[1], 7))
                        for bi, (k0, nk) in enumerate(kblocks):
                            wbuf = parw[bidx % 4]
                            if k0 >= KH or (k0 + nk > KH):
                                bb.wait(t, bwaits[0], bwaits[1])
                            if dqmode and tc == 0:
                                bb.wait(t, "sV", (tag, "wdqp", bidx))
                            else:
                                bb.wait(t, ("sWa", "sWb", "sWc", "sWd")
                                        [bidx % 4], (tag, "wld", bidx))
                            for ki in range(nk):
                                k = k0 + ki
                                for m_ in range(8):
                                    li = mm(t, psum[m_][:, :],
                                            wbuf[:, ki * 1024 + m_ * 128:
                                                 ki * 1024 + (m_ + 1) * 128],
                                            at_rd(tc, k),
                                            k == 0, k == IT - 1)
                            bb.inc(li, "sPE", 1, (tag, "wused", bidx))
                            bidx += 1
                        bb.mark("sPE", (tag, "odone", tc, o))
            on("tensor", f_pe)

            def f_wwb_ost(sc):
                for tc in range(TC):
                    for o in range(4):
                        if dqmode and tc == 0:
                            for bi, (k0, nk) in enumerate(kblocks):
                                bidx = o * NKB + bi
                                wbuf = parw[bidx % 4]
                                bb.wait(sc, "sV", (tag, "wdqp", bidx))
                                if bidx >= 4:
                                    bb.wait(sc,
                                            ("sYa", "sYb", "sYc", "sYd")
                                            [bidx % 4], (tag, "wwb", bidx - 4))
                                bb.inc(sc.dma_start(
                                    wd_re(wdbf)[:, k0:k0 + nk,
                                                o * 1024:(o + 1) * 1024],
                                    wbuf[:, 0:nk * 1024]),
                                    ("sYa", "sYb", "sYc", "sYd")[bidx % 4], 16,
                                    (tag, "wwb", bidx))
                        for m in range(8):
                            dt = o * 8 + m
                            rs = rs_[m % 4]
                            bb.wait(sc, "sV", (tag, "evac", tc, o, m))
                            bb.inc(sc.dma_start(
                                rdst[dt, :, tc * 512:(tc + 1) * 512], rs[:, :]),
                                f"sST{m % 4}", 16, (tag, "ost", tc, o, m))
                    if tc + 1 < TC:
                        lda_emit(sc, "sLB", tc + 1, 1, None)
                    if tc + 2 < TC:
                        lda_emit(sc, "sLB", tc + 2, 0, None)
                    if tc + 1 < TC:
                        bb.mark("sLB", (tag, "ldpair", tc))
            on("scalar", f_wwb_ost)
            set_bank_free(range(8), "sV", (tag, "evac", TC - 1, 3, 7))

        def emit_attn_ln(tag, hready_key):
            def f_cast(sc):
                for dt in range(ADT):
                    hb = (hbf, hbf2)[dt % 2]
                    sq = (sq0, sq1)[dt % 2]
                    bb.wait(sc, "sV", hready_key(dt))
                    if dt >= 2:
                        bb.wait(sc, "sPE", (tag, "st", dt - 2))
                    sc.copy(hb[:, :], h_t[dt])
                    bb.inc(sc.square(sq[:, 0:T], h_t[dt]), "sS", 1,
                           (tag, "sq", dt))
            on("scalar", f_cast)

            bcap = cap_banks()

            def f_pe(t):
                for dt in range(ADT):
                    hb = (hbf, hbf2)[dt % 2]
                    sq = (sq0, sq1)[dt % 2]
                    bb.wait(t, "sS", (tag, "sq", dt))
                    if dt == 0:
                        wait_banks(t, bcap)
                    for tc in range(TC):
                        mm(t, psum[tc][0:1, :], ones_k[:, :],
                           hb[:, tc * 512:(tc + 1) * 512], dt == 0, dt == ADT - 1)
                        li = mm(t, psum[4 + tc][0:1, :], ones_k[:, :],
                                sq[:, tc * 512:(tc + 1) * 512],
                                dt == 0, dt == ADT - 1)
                    bb.inc(li, "sPE", 1, (tag, "st", dt))
            on("tensor", f_pe)

            def f_v1(v):
                bb.wait(v, "sPE", (tag, "st", ADT - 1))
                for tc in range(TC):
                    v.tensor_copy(r_mean[0:1, tc * 512:(tc + 1) * 512],
                                  psum[tc][0:1, :])
                    v.tensor_copy(r_sq[0:1, tc * 512:(tc + 1) * 512],
                                  psum[4 + tc][0:1, :])
                v.tensor_scalar_mul(r_mean[0:1, :], r_mean[0:1, :], 1.0 / 512.0)
                bb.inc(v.tensor_scalar_mul(r_sq[0:1, :], r_sq[0:1, :],
                                           1.0 / 512.0),
                       "sV", 1, (tag, "ex2"))
            on("vector", f_v1)

            def f_msq(sc):
                bb.wait(sc, "sV", (tag, "ex2"))
                for tc in range(TC):
                    bb.inc(sc.square(psum[4 + tc][64:65, :],
                                     r_mean[0:1, tc * 512:(tc + 1) * 512]),
                           "sS", 1, (tag, "msq", tc))
            on("scalar", f_msq)

            def f_var(v):
                bb.wait(v, "sS", (tag, "msq", TC - 1))
                for tc in range(TC):
                    v.tensor_sub(r_sq[0:1, tc * 512:(tc + 1) * 512],
                                 r_sq[0:1, tc * 512:(tc + 1) * 512],
                                 psum[4 + tc][64:65, :])
                bb.inc(v.tensor_scalar_add(r_sq[0:1, :], r_sq[0:1, :], EPS),
                       "sV", 1, (tag, "var"))
            on("vector", f_var)

            def f_sc2(sc):
                bb.wait(sc, "sV", (tag, "var"))
                sc.activation(r_tmp[0:1, :], r_sq[0:1, :], AF.Ln)
                bb.inc(sc.activation(r_rstd[0:1, :], r_tmp[0:1, :], AF.Exp,
                                     scale=pcol1("c_neghalf")),
                       "sS", 1, (tag, "rstd"))
            on("scalar", f_sc2)

            def f_pe2(t):
                bb.wait(t, "sS", (tag, "rstd"))
                for tc in range(TC):
                    mm(t, psum[tc][:, :], ones_b[0:1, :],
                       r_rstd[0:1, tc * 512:(tc + 1) * 512], True, True)
                    li = mm(t, psum[4 + tc][:, :], ones_b[0:1, :],
                            r_mean[0:1, tc * 512:(tc + 1) * 512], True, True)
                bb.inc(li, "sPE", 1, (tag, "bcast"))
            on("tensor", f_pe2)

        def emit_attn_norm_out(tag, wname, bname, func, outf):
            def f_vn(v):
                bb.wait(v, "sPE", (tag, "bcast"))
                for dt in range(ADT):
                    for tc in range(TC):
                        i = dt * TC + tc
                        ra, rb = ((rs_[0], rs_[1]) if i % 2 == 0
                                  else (rs_[2], rs_[3]))
                        if i >= 2:
                            bb.wait(v, "sS", (tag, "out", i - 2))
                        s5 = slice(tc * 512, (tc + 1) * 512)
                        v.tensor_sub(ra[:, :], h_t[dt][:, s5], psum[4 + tc][:, :])
                        bb.inc(v.tensor_mul(rb[:, :], ra[:, :], psum[tc][:, :]),
                               "sV", 1, (tag, "nt", i))
            on("vector", f_vn)

            def f_so(sc):
                for dt in range(ADT):
                    for tc in range(TC):
                        i = dt * TC + tc
                        rb = rs_[1] if i % 2 == 0 else rs_[3]
                        bb.wait(sc, "sV", (tag, "nt", i))
                        kw = {}
                        if wname is not None:
                            kw = dict(scale=pcol(f"{wname}{dt}"),
                                      bias=pcol(f"{bname}{dt}"))
                        bb.inc(sc.activation(outf(dt)[:, tc * 512:(tc + 1) * 512],
                                             rb[:, :], func, **kw),
                               "sS", 1, (tag, "out", i))
                bb.mark("sS", (tag, "outdone"))
            on("scalar", f_so)
            set_bank_free(range(8), "sV", (tag, "nt", ADT * TC - 1))

        def emit_scan(l, ct):
            tag = ("sc", l, ct)
            pw = pwt[:, ct * 32:(ct + 1) * 32]
            prg_r = prgs_r[:, ct * CH:(ct + 1) * CH]
            prg_i = prgs_i[:, ct * CH:(ct + 1) * CH]

            def f_tables(v):
                bb.wait(v, "sS", ("y", l, ct))
                pr = pcol(f"phr_{l}_{ct}")
                pi = pcol(f"phi_{l}_{ct}")
                v.tensor_mul(s64a[:, 0:1], pr, pr)
                v.tensor_mul(s64b[:, 0:1], pi, pi)
                v.drain()
                bb.inc(v.scalar_tensor_tensor(s64a[:, 1:2], s64a[:, 0:1],
                                              1e-38, s64b[:, 0:1],
                                              op0=ALU.add, op1=ALU.add),
                       "sV", 1, (tag, "r2"))
            on("vector", f_tables)

            def f_pr(sc):
                bb.wait(sc, "sV", (tag, "r2"))
                sc.activation(s64a[:, 2:3], s64a[:, 1:2], AF.Ln)
                sc.activation(s64a[:, 3:4], s64a[:, 2:3], AF.Exp,
                              scale=pcol("c_half"))
                bb.inc(sc.activation(s64a[:, 4:5], s64a[:, 3:4], AF.Tanh),
                       "sS", 1, (tag, "tanh"))
            on("scalar", f_pr)

            def f_pw(v):
                bb.wait(v, "sS", (tag, "tanh"))
                v.tensor_scalar_max(s64a[:, 3:4], s64a[:, 3:4], 1e-30)
                v.drain()
                v.reciprocal(s64b[:, 0:1], s64a[:, 3:4])
                v.drain()
                v.tensor_mul(s64a[:, 5:6], s64a[:, 4:5], s64b[:, 0:1])
                v.drain()
                v.tensor_mul(pw[:, 0:1], pcol(f"phr_{l}_{ct}"), s64a[:, 5:6])
                v.tensor_mul(pw[:, 1:2], pcol(f"phi_{l}_{ct}"), s64a[:, 5:6])
                for m in range(1, 12):
                    a = pw[:, 2 * m - 2:2 * m - 1]
                    b = pw[:, 2 * m - 1:2 * m]
                    v.tensor_mul(s64a[:, 6:7], a, a)
                    v.tensor_mul(s64b[:, 1:2], b, b)
                    v.tensor_mul(s64a[:, 7:8], a, b)
                    v.tensor_sub(pw[:, 2 * m:2 * m + 1], s64a[:, 6:7],
                                 s64b[:, 1:2])
                    v.tensor_add(pw[:, 2 * m + 1:2 * m + 2], s64a[:, 7:8],
                                 s64a[:, 7:8])
                v.tensor_copy(prg_i[:, 0:1], pw[:, 1:2])
                v.tensor_copy(prg_r[:, 0:1], pw[:, 0:1])
                for m in range(7):
                    w = 1 << m
                    ar = pw[:, 2 * m:2 * m + 1]
                    ai = pw[:, 2 * m + 1:2 * m + 2]
                    v.tensor_scalar_mul(s64b[:, 0:w], prg_i[:, 0:w], ar)
                    v.tensor_scalar_mul(s64a[:, 0:w], prg_i[:, 0:w], ai)
                    v.scalar_tensor_tensor(prg_i[:, w:2 * w], prg_r[:, 0:w],
                                           ai, s64b[:, 0:w],
                                           op0=ALU.mult, op1=ALU.add)
                    li = v.scalar_tensor_tensor(prg_r[:, w:2 * w],
                                                prg_r[:, 0:w],
                                                ar, s64a[:, 0:w],
                                                op0=ALU.mult, op1=ALU.subtract)
                bb.inc(li, "sV", 1, (tag, "tables"))
            on("vector", f_pw)

            def f_intra(v):
                y3 = y_ct.rearrange("p (c j) -> p c j", c=NCH)
                Ar = LAr.rearrange("p (c j) -> p c j", c=NCH)
                Ai = LAi.rearrange("p (c j) -> p c j", c=NCH)
                Br = LBr.rearrange("p (c j) -> p c j", c=NCH)
                Bi = LBi.rearrange("p (c j) -> p c j", c=NCH)
                v.scalar_tensor_tensor(Ar[:, :, 1:], y3[:, :, :CH - 1],
                                       pw[:, 0:1], y3[:, :, 1:],
                                       op0=ALU.mult, op1=ALU.add)
                v.tensor_scalar_mul(Ai[:, :, 1:], y3[:, :, :CH - 1], pw[:, 1:2])
                v.tensor_copy(Ar[:, :, 0:1], y3[:, :, 0:1])
                v.memset(Ai[:, :, 0:1], 0.0)
                cur = (Ar, Ai)
                oth = (Br, Bi)
                for m in range(1, 7):
                    d = 1 << m
                    sr, si = cur
                    dr, di = oth
                    ar = pw[:, 2 * m:2 * m + 1]
                    ai = pw[:, 2 * m + 1:2 * m + 2]
                    nai = s64a[:, 8:9]
                    v.tensor_scalar_mul(nai, ai, -1.0)
                    v.scalar_tensor_tensor(dr[:, :, d:], sr[:, :, :CH - d], ar,
                                           sr[:, :, d:],
                                           op0=ALU.mult, op1=ALU.add)
                    v.scalar_tensor_tensor(di[:, :, d:], si[:, :, :CH - d], ar,
                                           si[:, :, d:],
                                           op0=ALU.mult, op1=ALU.add)
                    v.scalar_tensor_tensor(dr[:, :, d:], si[:, :, :CH - d], nai,
                                           dr[:, :, d:],
                                           op0=ALU.mult, op1=ALU.add)
                    v.scalar_tensor_tensor(di[:, :, d:], sr[:, :, :CH - d], ai,
                                           di[:, :, d:],
                                           op0=ALU.mult, op1=ALU.add)
                    v.tensor_copy(dr[:, :, 0:d], sr[:, :, 0:d])
                    v.tensor_copy(di[:, :, 0:d], si[:, :, 0:d])
                    cur, oth = oth, cur
                fr, fi = cur
                v.tensor_copy(e0r[:, 1:NCH + 1],
                              fr[:, :, CH - 1:CH].rearrange("p c j -> p (c j)"))
                bb.inc(v.tensor_copy(
                    e0i[:, 1:NCH + 1],
                    fi[:, :, CH - 1:CH].rearrange("p c j -> p (c j)")),
                    "sV", 1, (tag, "intra"))
            on("vector", f_intra)

            def f_escan(v):
                lcr = pcol(f"lcr_{l}_{ct}")
                lci = pcol(f"lci_{l}_{ct}")
                v.tensor_scalar_mul(e0r[:, 0:1], lcr, emask[:, 0:1])
                v.tensor_scalar_mul(e0i[:, 0:1], lci, emask[:, 0:1])
                cur = (e0r, e0i)
                oth = (e1r, e1i)
                N = NCH + 1
                for mi, m in enumerate(range(7, 12)):
                    d = 1 << mi
                    sr, si = cur
                    dr, di = oth
                    ar = pw[:, 2 * m:2 * m + 1]
                    ai = pw[:, 2 * m + 1:2 * m + 2]
                    nai = s64a[:, 9:10]
                    v.tensor_scalar_mul(nai, ai, -1.0)
                    v.scalar_tensor_tensor(dr[:, d:N], sr[:, 0:N - d], ar,
                                           sr[:, d:N], op0=ALU.mult, op1=ALU.add)
                    v.scalar_tensor_tensor(di[:, d:N], si[:, 0:N - d], ar,
                                           si[:, d:N], op0=ALU.mult, op1=ALU.add)
                    v.scalar_tensor_tensor(dr[:, d:N], si[:, 0:N - d], nai,
                                           dr[:, d:N], op0=ALU.mult, op1=ALU.add)
                    v.scalar_tensor_tensor(di[:, d:N], sr[:, 0:N - d], ai,
                                           di[:, d:N], op0=ALU.mult, op1=ALU.add)
                    v.tensor_copy(dr[:, 0:d], sr[:, 0:d])
                    v.tensor_copy(di[:, 0:d], si[:, 0:d])
                    cur, oth = oth, cur
                fr, fi = cur
                v.tensor_copy(sx[:, 2 * ct:2 * ct + 1], fr[:, NCH:NCH + 1])
                v.tensor_copy(sx[:, 2 * ct + 1:2 * ct + 2], fi[:, NCH:NCH + 1])
                bb.inc(v.tensor_scalar_mul(esn[:, :], fi[:, :], -1.0),
                       "sV", 1, (tag, "escan"))
            on("vector", f_escan)

            FINr3 = LAr.rearrange("p (c j) -> p c j", c=NCH)

            def f_apply(v):
                for c in range(NCH):
                    v.scalar_tensor_tensor(FINr3[:, c, :], prg_r[:, :],
                                           e1r[:, c:c + 1], FINr3[:, c, :],
                                           op0=ALU.mult, op1=ALU.add)
                for c in range(NCH):
                    v.scalar_tensor_tensor(FINr3[:, c, :], prg_i[:, :],
                                           esn[:, c:c + 1], FINr3[:, c, :],
                                           op0=ALU.mult, op1=ALU.add)
                bb.inc(v.scalar_tensor_tensor(
                    h_t[ct], FINr3.rearrange("p c j -> p (c j)"),
                    pcol(f"scl_{l}_{ct}"), h_t[ct],
                    op0=ALU.mult, op1=ALU.add),
                    "sV", 1, ("hupd", l, ct))
            on("vector", f_apply)

        def emit_scan_corr(l, ct):
            pw = pwt[:, ct * 32:(ct + 1) * 32]
            prg_r = prgs_r[:, ct * CH:(ct + 1) * CH]
            prg_i = prgs_i[:, ct * CH:(ct + 1) * CH]

            def f_corr(v):
                if ct == 0:
                    bb.wait(v, "sEx", ("exr", l))
                    bb.inc(v.tensor_scalar_mul(nrd[:, :], rdat[:, :], -1.0),
                           "sV", 1, ("nrd", l))
                ptr = LBr
                pti = LBi
                v.tensor_copy(pti[:, 0:CH], prg_i[:, :])
                v.tensor_copy(ptr[:, 0:CH], prg_r[:, :])
                for mi, m in enumerate(range(7, 11)):
                    w = CH << mi
                    ar = pw[:, 2 * m:2 * m + 1]
                    ai = pw[:, 2 * m + 1:2 * m + 2]
                    ya = y_ct[:, 0:w]
                    yb = y_ct[:, 1024:1024 + w]
                    v.tensor_scalar_mul(yb, pti[:, 0:w], ar)
                    v.tensor_scalar_mul(ya, pti[:, 0:w], ai)
                    v.scalar_tensor_tensor(pti[:, w:2 * w], ptr[:, 0:w], ai,
                                           yb, op0=ALU.mult, op1=ALU.add)
                    v.scalar_tensor_tensor(ptr[:, w:2 * w], ptr[:, 0:w], ar,
                                           ya, op0=ALU.mult,
                                           op1=ALU.subtract)
                v.tensor_scalar_mul(y_ct[:, 0:T], ptr[:, 0:T],
                                    rdat[:, 2 * ct:2 * ct + 1])
                v.scalar_tensor_tensor(y_ct[:, 0:T], pti[:, 0:T],
                                       nrd[:, 2 * ct + 1:2 * ct + 2],
                                       y_ct[:, 0:T], op0=ALU.mult, op1=ALU.add)
                bb.inc(v.scalar_tensor_tensor(
                    h_t[ct], y_ct[:, 0:T], pcol(f"scl_{l}_{ct}"), h_t[ct],
                    op0=ALU.mult, op1=ALU.add),
                    "sV", 1, ("hcor", l, ct))
            on("vector", f_corr)

        def emit_attn(pre_bar, stopat=None):
            emit_ln("lnA", resid_s, pre_bar=pre_bar)

            def f_winstream(g):
                bb.bar(g, pre_bar)
                for half in range(2):
                    for k in range(DT):
                        i = half * DT + k
                        if i >= 2:
                            bb.wait(g, "sPE", ("winmm", i - 2))
                        bb.inc(g.dma_start((sil0, sil1)[i % 2],
                                           win_t[k, :, :]),
                               ("sWa", "sWb")[i % 2], 16, ("winld", i))
            on("gpsimd", f_winstream)

            bcap_win = cap_banks()

            def f_win(t):
                bb.wait(t, "sV", ("lnA", "done"))
                for half in range(2):
                    if half == 0:
                        wait_banks(t, bcap_win)
                    else:
                        bb.wait(t, "sS", ("winev", 0, 7))
                    for k in range(DT):
                        i = half * DT + k
                        bb.wait(t, ("sWa", "sWb")[i % 2], ("winld", i))
                        wt = (sil0, sil1)[i % 2]
                        for gi in range(8):
                            gidx = half * 8 + gi
                            dt4, tc = gidx // TC, gidx % TC
                            li = mm(t, psum[gi][:, :],
                                    wt[:, dt4 * 128:(dt4 + 1) * 128],
                                    ybf[:, k * T + tc * 512:
                                        k * T + (tc + 1) * 512],
                                    k == 0, k == DT - 1)
                        bb.inc(li, "sPE", 1, ("winmm", i))
                    bb.mark("sPE", ("win", half))
            on("tensor", f_win)

            def f_winev(sc):
                for half in range(2):
                    bb.wait(sc, "sPE", ("win", half))
                    for gi in range(8):
                        gidx = half * 8 + gi
                        dt4, tc = gidx // TC, gidx % TC
                        bb.inc(sc.activation(
                            h_t[dt4][:, tc * 512:(tc + 1) * 512],
                            psum[gi][:, :], AF.Identity, bias=pcol(f"bin{dt4}")),
                            "sS", 1, ("winev", half, gi))
            on("scalar", f_winev)
            set_bank_free(range(8), "sS", ("winev", 1, 7))

            def f_hrdy(v):
                bb.wait(v, "sS", ("winev", 1, 7))
                bb.inc(v.nop(), "sV", 1, ("hrdy", 0))
            on("vector", f_hrdy)

            def dump_h(gatekey):
                def f_d(g):
                    bb.wait(g, "sV", gatekey)
                    for dt in range(ADT):
                        bb.inc(g.dma_start(dbg[dt, :, :], h_t[dt]), "sDB", 16)
                    bb.mark("sDB", "fin_sO")
                    bb.wait(g, "sDB", "fin_sO")
                on("gpsimd", f_d)

            if stopat in ("y00", "in00", "tb00"):
                pass
            if stopat == "win":
                def f_dw(g):
                    bb.wait(g, "sV", ("hrdy", 0))
                    for dt in range(ADT):
                        bb.inc(g.dma_start(dbg[dt, :, :], h_t[dt]), "sDB", 16)
                    bb.mark("sDB", "fin_sO")
                    bb.wait(g, "sDB", "fin_sO")
                on("gpsimd", f_dw)
                return

            for l in range(DEPTH):
                t1 = ("aln", l, 1)
                emit_attn_ln(t1, lambda dt, l=l: ("hrdy", l))

                for ct in range(ADT):
                    def f_vn(v, l=l, ct=ct, t1=t1):
                        if ct == 0:
                            bb.wait(v, "sPE", (t1, "bcast"))
                        for tc in range(TC):
                            i = ct * TC + tc
                            ra, rb = ((rs_[0], rs_[1]) if i % 2 == 0
                                      else (rs_[2], rs_[3]))
                            if i >= 2:
                                bb.wait(v, "sS", (t1, "yout", i - 2))
                            s5 = slice(tc * 512, (tc + 1) * 512)
                            v.tensor_sub(ra[:, :], h_t[ct][:, s5],
                                         psum[4 + tc][:, :])
                            bb.inc(v.tensor_mul(rb[:, :], ra[:, :],
                                                psum[tc][:, :]),
                                   "sV", 1, (t1, "nt", i))
                    on("vector", f_vn)

                    def f_ys(sc, l=l, ct=ct, t1=t1):
                        if ct >= 1:
                            bb.wait(sc, "sV", (("sc", l, ct - 1), "intra"))
                        elif l >= 1:
                            bb.wait(sc, "sV", (("sc", l - 1, ADT - 1), "intra"))
                        for tc in range(TC):
                            i = ct * TC + tc
                            rb = rs_[1] if i % 2 == 0 else rs_[3]
                            bb.wait(sc, "sV", (t1, "nt", i))
                            bb.inc(sc.activation(
                                y_ct[:, tc * 512:(tc + 1) * 512], rb[:, :],
                                AF.Silu, scale=pcol(f"ln1w_{l}_{ct}"),
                                bias=pcol(f"ln1b_{l}_{ct}")),
                                "sS", 1, (t1, "yout", i))
                        bb.mark("sS", ("y", l, ct))
                    on("scalar", f_ys)

                    emit_scan(l, ct)
                    if l == 0 and ct == 0 and stopat in ("y00", "in00", "tb00"):
                        key = {"y00": None, "in00": (("sc", 0, 0), "intra"),
                               "tb00": (("sc", 0, 0), "tables")}[stopat]

                        def f_md(g, key=key, stopat=stopat):
                            if stopat == "y00":
                                bb.wait(g, "sS", ("y", 0, 0))
                                bb.inc(g.dma_start(dbg[0, :, :], y_ct[:, 0:T]),
                                       "sDB", 16)
                            elif stopat == "tb00":
                                bb.wait(g, "sV", key)
                                bb.inc(g.dma_start(
                                    dbg[0, :, 0:CH],
                                    prgs_r[:, 0:CH]), "sDB", 16)
                                bb.inc(g.dma_start(
                                    dbg[0, :, CH:2 * CH],
                                    prgs_i[:, 0:CH]), "sDB", 16)
                                bb.inc(g.dma_start(
                                    dbg[0, :, 2 * CH:2 * CH + 128],
                                    pwt[:, 0:128]), "sDB", 16)
                            else:
                                bb.wait(g, "sV", key)
                                bb.inc(g.dma_start(dbg[0, :, :], LAr[:, 0:T]),
                                       "sDB", 16)
                                bb.inc(g.dma_start(dbg[1, :, :], LAi[:, 0:T]),
                                       "sDB", 16)
                                bb.inc(g.dma_start(dbg[2, :, 0:NCH + 1],
                                                   e0r[:, :]), "sDB", 16)
                                bb.inc(g.dma_start(dbg[2, :, 32:32 + NCH + 1],
                                                   e0i[:, :]), "sDB", 16)
                            bb.mark("sDB", "fin_sO")
                            bb.wait(g, "sDB", "fin_sO")
                        on("gpsimd", f_md)
                        return
                set_bank_free(range(8), "sV", (t1, "nt", ADT * TC - 1))

                sp = f"sP{l}"

                def f_ex(g, l=l, sp=sp):
                    bb.wait(g, "sV", (("sc", l, ADT - 1), "escan"))
                    if bb.mode == "emit":
                        with (g.register(f"fev{l}") as fev,
                              g.register(f"cnt{l}") as cnt,
                              g.register(f"fl{l}") as fl,
                              g.register(f"nd{l}") as nd):
                            g.reg_load(fev, rflag[0:1, 0:1])
                            g.reg_mov(cnt, 0)
                            with g.If_ne(fev, 0):
                                g.dma_start(xch[l, :, :],
                                            sx[:, :]).then_inc(bb.h[sp], 16)
                                g.reg_add(cnt, cnt, 16)
                                g.wait_ge(bb.h[sp], cnt)
                                g.dma_start(xfl[l:l + 1, 0:1],
                                            magict[:, :]).then_inc(bb.h[sp], 16)
                                g.reg_add(cnt, cnt, 16)
                                g.wait_ge(bb.h[sp], cnt)
                            with g.Else():
                                g.reg_mov(nd, 1)
                                with g.While(nd):
                                    g.dma_start(pollt[:, :],
                                                xfl[l:l + 1, 0:1]).then_inc(
                                        bb.h[sp], 16)
                                    g.reg_add(cnt, cnt, 16)
                                    g.wait_ge(bb.h[sp], cnt)
                                    g.reg_load(fl, pollt[0:1, 0:1])
                                    g.reg_mov(nd, MAGIC)
                                    g.reg_sub(nd, nd, fl)
                                g.dma_start(rdat[:, :],
                                            xch[l, :, :]).then_inc(bb.h[sp], 16)
                                g.reg_add(cnt, cnt, 16)
                                g.wait_ge(bb.h[sp], cnt)
                                g.dma_start(xfl[l:l + 1, 0:1],
                                            zerot[:, :]).then_inc(bb.h[sp], 16)
                                g.reg_add(cnt, cnt, 16)
                                g.wait_ge(bb.h[sp], cnt)
                    bb.inc(g.nop(), "sEx", 1, ("exr", l))
                on("gpsimd", f_ex)

                for ct in range(ADT):
                    emit_scan_corr(l, ct)

                t2 = ("aln", l, 2)
                t2 = ("aln", l, 2)

                def f_h2rdy(v, l=l):
                    bb.wait(v, "sV", ("hcor", l, ADT - 1))
                    bb.inc(v.nop(), "sV", 1, ("h2rdy", l))
                on("vector", f_h2rdy)
                emit_attn_ln(t2, lambda dt, l=l: ("h2rdy", l))
                emit_attn_norm_out(t2, f"ln2w_{l}_", f"ln2b_{l}_", AF.Identity,
                                   lambda dt: y2bf[dt])

                def f_w12(g, l=l):
                    if l == 0:
                        bb.wait(g, "sPE", ("win", 1))
                    else:
                        bb.wait(g, "sPE", (("ffnin", l - 1), "f2done"))
                    for k in range(4):
                        bb.inc(g.dma_start(w1bf[:, k * 2048:(k + 1) * 2048],
                                           w1t[l, k, :, :]), "sW12", 16)
                    for k in range(16):
                        bb.inc(g.dma_start(w2bf[:, k * 512:(k + 1) * 512],
                                           w2t[l, k, :, :]), "sW12", 16,
                               ("w12", l) if k == 15 else None)
                on("gpsimd", f_w12)

                tf = ("ffnin", l)
                bcap_f = cap_banks()

                def f_ffn_pe(t, l=l, tf=tf, t2=t2, bcap_f=bcap_f):
                    bb.wait(t, "sW12", ("w12", l))
                    bb.wait(t, "sS", (t2, "outdone"))
                    wait_banks(t, bcap_f)
                    for tc in range(TC):
                        if tc >= 1:
                            bb.wait(t, "sV", (tf, "f2ev", tc - 1))
                        for ft in range(FT):
                            fb = psum[4 + ft % 2]
                            if ft >= 2:
                                bb.wait(t, "sS", (tf, "sil", tc, ft - 2))
                            for k in range(4):
                                li = mm(t, fb[:, :],
                                        w1bf[:, k * 2048 + ft * 128:
                                             k * 2048 + (ft + 1) * 128],
                                        y2bf[k][:, tc * 512:(tc + 1) * 512],
                                        k == 0, k == 3)
                            bb.inc(li, "sPE", 1, (tf, "f1", tc, ft))
                            if ft >= 1:
                                fp = ft - 1
                                silb = (sil0, sil1)[fp % 2]
                                bb.wait(t, "sS", (tf, "sil", tc, fp))
                                for dt in range(ADT):
                                    mm(t, psum[dt][:, :],
                                       w2bf[:, fp * 512 + dt * 128:
                                            fp * 512 + (dt + 1) * 128],
                                       silb[:, :], fp == 0, False)
                        fp = FT - 1
                        silb = (sil0, sil1)[fp % 2]
                        bb.wait(t, "sS", (tf, "sil", tc, fp))
                        for dt in range(ADT):
                            li = mm(t, psum[dt][:, :],
                                    w2bf[:, fp * 512 + dt * 128:
                                         fp * 512 + (dt + 1) * 128],
                                    silb[:, :], False, True)
                        bb.inc(li, "sPE", 1, (tf, "f2", tc))
                    bb.mark("sPE", (tf, "f2done"))
                on("tensor", f_ffn_pe)

                def f_ffn_sil(sc, l=l, tf=tf):
                    for tc in range(TC):
                        for ft in range(FT):
                            silb = (sil0, sil1)[ft % 2]
                            bb.wait(sc, "sPE", (tf, "f1", tc, ft))
                            bb.inc(sc.activation(silb[:, :],
                                                 psum[4 + ft % 2][:, :], AF.Silu,
                                                 bias=pcol(f"b1_{l}_{ft}")),
                                   "sS", 1, (tf, "sil", tc, ft))
                on("scalar", f_ffn_sil)

                def f_ffn_ev(v, l=l, tf=tf):
                    for tc in range(TC):
                        bb.wait(v, "sPE", (tf, "f2", tc))
                        for dt in range(ADT):
                            bb.inc(v.scalar_tensor_tensor(
                                h_t[dt][:, tc * 512:(tc + 1) * 512],
                                psum[dt][:, :], pcol(f"b2_{l}_{dt}"),
                                h_t[dt][:, tc * 512:(tc + 1) * 512],
                                op0=ALU.add, op1=ALU.add),
                                "sV", 1,
                                (tf, "f2ev", tc) if dt == ADT - 1 else None)
                on("vector", f_ffn_ev)
                set_bank_free(range(8), "sV", (tf, "f2ev", TC - 1))

                def f_h3rdy(v, l=l):
                    bb.wait(v, "sV", (("ffnin", l), "f2ev", TC - 1))
                    bb.inc(v.nop(), "sV", 1, ("hrdy", l + 1))
                on("vector", f_h3rdy)

                if stopat == f"l{l}":
                    dump_h(("hrdy", l + 1))
                    return
                if stopat == f"s{l}":
                    dump_h(("hcor", l, ADT - 1))
                    return

            tl = ("aln", "lnl", 0)
            emit_attn_ln(tl, lambda dt: ("hrdy", DEPTH))
            emit_attn_norm_out(tl, "lnlw", "lnlb", AF.Identity, lambda dt: y2bf[dt])

            def f_woutld(g):
                bb.wait(g, "sPE", ("win", 1))
                for k in range(4):
                    bb.inc(g.dma_start(woutbf[:, k * 4096:(k + 1) * 4096],
                                       wout_t[k, :, :]), "sW12", 16,
                           ("woutld",) if k == 3 else None)
            on("gpsimd", f_woutld)

            def f_wout_sync(s):
                prevuse = {}
                for vt in range(DT):
                    for tc in range(TC):
                        m4 = (vt * TC + tc) % 4
                        pk = prevuse.get(m4)
                        if pk is not None:
                            bb.wait(s, "sV", pk[0])
                            bb.wait(s, f"sST{m4}", pk[1])
                        else:
                            bb.wait(s, "sS", (tl, "outdone"))
                        bb.inc(s.dma_start(
                            rs_[m4][:, :],
                            resid_s[vt, :, tc * 512:(tc + 1) * 512]),
                            f"sR{m4}", 16, ("wo_rld", vt, tc))
                        prevuse[m4] = (("wo_ev", vt, tc), ("wo_st", vt, tc))
            on("sync", f_wout_sync)

            bcap_wo = cap_banks()

            def f_wout_pe(t, bcap_wo=bcap_wo):
                bb.wait(t, "sW12", ("woutld",))
                bb.wait(t, "sS", (tl, "outdone"))
                gi = 0
                for vt in range(DT):
                    for tc in range(TC):
                        b_ = gi % 8
                        if gi >= 8:
                            bb.wait(t, "sV", ("wo_ev_g", gi - 8))
                        elif gi == 0:
                            wait_banks(t, bcap_wo)
                        for k in range(4):
                            li = mm(t, psum[b_][:, :],
                                    woutbf[:, k * 4096 + vt * 128:
                                           k * 4096 + (vt + 1) * 128],
                                    y2bf[k][:, tc * 512:(tc + 1) * 512],
                                    k == 0, k == 3)
                        bb.inc(li, "sPE", 1, ("wo_pe", gi))
                        gi += 1
            on("tensor", f_wout_pe)

            def f_wout_ev(v):
                gi = 0
                for vt in range(DT):
                    for tc in range(TC):
                        b_ = gi % 8
                        m4 = gi % 4
                        bb.wait(v, "sPE", ("wo_pe", gi))
                        bb.wait(v, f"sR{gi % 4}", ("wo_rld", vt, tc))
                        bb.inc(v.scalar_tensor_tensor(
                            rs_[m4][:, :], psum[b_][:, :], pcol(f"bout{vt}"),
                            rs_[m4][:, :], op0=ALU.add, op1=ALU.add),
                            "sV", 1, ("wo_ev", vt, tc))
                        bb.mark("sV", ("wo_ev_g", gi))
                        gi += 1
            on("vector", f_wout_ev)

            def f_wout_st(sc):
                for vt in range(DT):
                    for tc in range(TC):
                        bb.wait(sc, "sV", ("wo_ev", vt, tc))
                        bb.inc(sc.dma_start(
                            resid_s[vt, :, tc * 512:(tc + 1) * 512],
                            rs_[(vt * TC + tc) % 4][:, :]),
                            f"sST{(vt * TC + tc) % 4}", 16, ("wo_st", vt, tc))
            on("scalar", f_wout_st)
            set_bank_free(range(8), "sV", ("wo_ev", DT - 1, TC - 1))

        emit_ln("ln1", resid_in)
        if stop_after == "ln":
            def f_snap(e):
                bb.snap(("bar", "x"))
            on("sync", f_snap)

            def f_dumpln(g):
                bb.wait(g, "sV", ("ln1", "done"))
                for dt in range(DT):
                    bb.inc(g.dma_start(dbg[dt, :, :], ybf[:, dt * T:(dt + 1) * T]),
                           "sDB", 16)
                bb.mark("sDB", "fin_sO")
                bb.wait(g, "sDB", "fin_sO")
            on("gpsimd", f_dumpln)
        else:
            emit_gateup("gu1", 1, "ln1")

            def f_snapgu1(e):
                bb.snap(("bar", "gu1"))
            on("sync", f_snapgu1)
            if stop_after == "gu":
                def f_dumpgu(g):
                    bb.bar(g, ("bar", "gu1"))
                    for tc in range(TC):
                        bb.inc(g.dma_start(
                            dbg[tc, :, :, :].rearrange("p k j -> p (k j)"),
                            at_s[tc, :, :, :].rearrange("p k j -> p (k j)")),
                            "sDB", 16)
                    bb.mark("sDB", "fin_sO")
                    bb.wait(g, "sDB", "fin_sO")
                on("gpsimd", f_dumpgu)
            else:
                emit_down("dn1", 1, resid_in, resid_s, ("bar", "gu1"), "sY")

                def f_snapffn1(e):
                    bb.snap(("bar", "ffn1"))
                on("sync", f_snapffn1)
                if stop_after == "ffn1":
                    def f_dumpf1(g):
                        bb.bar(g, ("bar", "ffn1"))
                        for dt in range(DT):
                            bb.inc(g.dma_start(dbg[dt, :, :], resid_s[dt, :, :]),
                                   "sDB", 16)
                        bb.mark("sDB", "fin_sO")
                        bb.wait(g, "sDB", "fin_sO")
                    on("gpsimd", f_dumpf1)
                else:
                    if stop_after in ("win", "l0", "l1", "l2", "l3",
                                      "s0", "s1", "s2", "s3",
                                      "y00", "in00", "tb00"):
                        emit_attn(("bar", "ffn1"), stopat=stop_after)
                        bb.run_plan()
                        for nm in BB.SEMS:
                            bb.h[nm] = es.enter_context(nc.semaphore(nm))

                        @blk.sync
                        def _(s):
                            bb.run_emit("sync", s)

                        @blk.scalar
                        def _(sc):
                            bb.run_emit("scalar", sc)

                        @blk.vector
                        def _(v):
                            bb.run_emit("vector", v)

                        @blk.tensor
                        def _(t):
                            bb.run_emit("tensor", t)

                        @blk.gpsimd
                        def _(g):
                            bb.run_emit("gpsimd", g)
                        return nc
                    emit_attn(("bar", "ffn1"))

                    def f_snapattn(e):
                        bb.snap(("bar", "attn"))
                    on("sync", f_snapattn)
                    if stop_after == "attn":
                        def f_dumpat(g):
                            bb.bar(g, ("bar", "attn"))
                            for dt in range(DT):
                                bb.inc(g.dma_start(dbg[dt, :, :],
                                                   resid_s[dt, :, :]), "sDB", 16)
                            bb.mark("sDB", "fin_sO")
                            bb.wait(g, "sDB", "fin_sO")
                        on("gpsimd", f_dumpat)
                    else:
                        emit_ln("lnF", resid_s, pre_bar=("bar", "attn"))
                        emit_gateup("gu2", 2, "lnF", pre_bar=("bar", "attn"))

                        def f_snapgu2(e):
                            bb.snap(("bar", "gu2"))
                        on("sync", f_snapgu2)
                        emit_down("dn2", 2, resid_s, resid_out,
                                  ("bar", "gu2"), "sO")

                        def f_fin(g):
                            bb.snap(("bar", "fin"))
                            bb.bar(g, ("bar", "fin"))
                        on("gpsimd", f_fin)

        bb.run_plan()
        for nm in BB.SEMS:
            bb.h[nm] = es.enter_context(nc.semaphore(nm))

        @blk.sync
        def _(s):
            bb.run_emit("sync", s)

        @blk.scalar
        def _(sc):
            bb.run_emit("scalar", sc)

        @blk.vector
        def _(v):
            bb.run_emit("vector", v)

        @blk.tensor
        def _(t):
            bb.run_emit("tensor", t)

        @blk.gpsimd
        def _(g):
            bb.run_emit("gpsimd", g)

    return nc


def _in_maps(staged):
    maps = []
    for c in range(8):
        m = {k: staged[k] for k in
             ("wg8", "wu8", "wd8", "gsr", "usr", "dsr", "win_t", "w1t", "w2t",
              "wout_t", "params")}
        m["resid_in"] = staged["_resid_shards"][c]
        ev = 1 - (c % 2)
        m["evenflag"] = np.array([[ev]], np.int32)
        em = np.zeros((128, 2), np.float32)
        em[:, 0] = ev
        em[:, 1] = 1 - ev
        m["emaskin"] = em
        maps.append(m)
    return maps


def kernel(**inputs):
    staged = _stage_inputs(inputs)
    nc = build_program()
    res = run_bass_kernel_spmd(nc, _in_maps(staged), core_ids=list(range(8)))
    out = np.empty((4, 4096, 4096), np.float32)
    for c in range(8):
        b, h = c // 2, c % 2
        sh = np.asarray(res.results[c]["resid_out"]).reshape(4096, T)
        out[b, h * T:(h + 1) * T, :] = sh.T
    return out


# version 05 (final)
# speedup vs baseline: 1.1420x; 1.0511x over previous
import numpy as np
import ml_dtypes
from contextlib import ExitStack
import concourse.bass as bass
import concourse.mybir as mybir
from concourse.bass_utils import run_bass_kernel_spmd

F32 = mybir.dt.float32
BF16 = mybir.dt.bfloat16
U8 = mybir.dt.uint8
I32 = mybir.dt.int32
AF = mybir.ActivationFunctionType
ALU = mybir.AluOpType

T = 2048
TC = 4
DT = 32
IT = 86
DIM = 512
ADT = 4
FT = 16
DEPTH = 4
CH = 128
NCH = 16
EPS = 1e-5
MAGIC = 1234567

_P = {}


def _build_pmap():
    idx = [0]

    def add(name):
        _P[name] = idx[0]
        idx[0] += 1

    def addvec(name, ntile):
        for i in range(ntile):
            add(f"{name}{i}")

    _P.clear()
    addvec("bin", 4)
    for l in range(4):
        addvec(f"ln1w_{l}_", 4)
        addvec(f"ln1b_{l}_", 4)
        addvec(f"ln2w_{l}_", 4)
        addvec(f"ln2b_{l}_", 4)
        addvec(f"phr_{l}_", 4)
        addvec(f"phi_{l}_", 4)
        addvec(f"lcr_{l}_", 4)
        addvec(f"lci_{l}_", 4)
        addvec(f"scl_{l}_", 4)
        addvec(f"b1_{l}_", 16)
        addvec(f"b2_{l}_", 4)
    addvec("lnlw", 4)
    addvec("lnlb", 4)
    addvec("bout", 32)
    add("c_neghalf")
    add("c_half")


_build_pmap()



def _stage_inputs(inp):
    out = {}
    x = np.ascontiguousarray(np.asarray(inp["L0_attn_x"], np.float32))

    def lowbyte(a):
        a = np.ascontiguousarray(np.asarray(a, np.int32))
        return np.ascontiguousarray(a.view(np.uint8)[:, ::4])

    gq = lowbyte(inp["L0_ffn_gate_q"])
    uq = lowbyte(inp["L0_ffn_up_q"])
    dq = lowbyte(inp["L0_ffn_down_q"])
    out["wg8"] = np.ascontiguousarray(
        gq.reshape(32, 128, 86, 128).transpose(2, 1, 0, 3)).reshape(86, 128, 4096)
    out["wu8"] = np.ascontiguousarray(
        uq.reshape(32, 128, 86, 128).transpose(2, 1, 0, 3)).reshape(86, 128, 4096)
    out["wd8"] = np.ascontiguousarray(dq.reshape(86, 128, 4096))
    gs = np.asarray(inp["L0_ffn_gate_s"], np.float32)
    us = np.asarray(inp["L0_ffn_up_s"], np.float32)
    out["gsr"] = np.ascontiguousarray(
        gs.reshape(32, 86, 128).transpose(1, 0, 2)).reshape(86, 4096)
    out["usr"] = np.ascontiguousarray(
        us.reshape(32, 86, 128).transpose(1, 0, 2)).reshape(86, 4096)
    out["dsr"] = np.ascontiguousarray(np.asarray(inp["L0_ffn_down_s"], np.float32))
    out["win_t"] = np.ascontiguousarray(
        np.asarray(inp["L0_attn_Win"], np.float32).T).reshape(32, 128, 512)
    out["w1t"] = np.ascontiguousarray(
        np.transpose(np.asarray(inp["L0_attn_W1"], np.float32), (0, 2, 1))
    ).reshape(4, 4, 128, 2048)
    out["w2t"] = np.ascontiguousarray(
        np.transpose(np.asarray(inp["L0_attn_W2"], np.float32), (0, 2, 1))
    ).reshape(4, 16, 128, DIM)
    out["wout_t"] = np.ascontiguousarray(
        np.asarray(inp["L0_attn_Wout"], np.float32).T).reshape(4, 128, 4096)
    out["params"] = _pack_params(inp)
    shards = []
    for c in range(8):
        b, h = c // 2, c % 2
        xt = np.ascontiguousarray(x[b, h * T:(h + 1) * T, :].T)
        shards.append(np.ascontiguousarray(xt.reshape(32, 128, T)))
    out["_resid_shards"] = shards
    return out


def _pack_params(inp):
    arr = np.zeros((128, 288), np.float32)

    def put(name, vec128):
        arr[:, _P[name]] = np.asarray(vec128, np.float32).reshape(128)

    def putvec(name, v, ntile):
        v = np.asarray(v, np.float32).reshape(ntile, 128)
        for i in range(ntile):
            put(f"{name}{i}", v[i])

    putvec("bin", inp["L0_attn_bin"], 4)
    for l in range(DEPTH):
        putvec(f"ln1w_{l}_", inp["L0_attn_ln1_w"][l], 4)
        putvec(f"ln1b_{l}_", inp["L0_attn_ln1_b"][l], 4)
        putvec(f"ln2w_{l}_", inp["L0_attn_ln2_w"][l], 4)
        putvec(f"ln2b_{l}_", inp["L0_attn_ln2_b"][l], 4)
        putvec(f"phr_{l}_", inp["L0_attn_phz_re"][l], 4)
        putvec(f"phi_{l}_", inp["L0_attn_phz_im"][l], 4)
        putvec(f"lcr_{l}_", inp["L0_attn_lc_re"][l], 4)
        putvec(f"lci_{l}_", inp["L0_attn_lc_im"][l], 4)
        putvec(f"scl_{l}_", inp["L0_attn_sc_lin"][l], 4)
        putvec(f"b1_{l}_", inp["L0_attn_b1"][l], 16)
        putvec(f"b2_{l}_", inp["L0_attn_b2"][l], 4)
    putvec("lnlw", inp["L0_attn_lnl_w"], 4)
    putvec("lnlb", inp["L0_attn_lnl_b"], 4)
    putvec("bout", inp["L0_attn_bout"], 32)
    put("c_neghalf", np.full(128, -0.5))
    put("c_half", np.full(128, 0.5))
    return arr



class _Any:
    def __call__(self, *a, **k):
        return _Any()

    def __getattr__(self, nm):
        return _Any()

    def __enter__(self):
        return _Any()

    def __exit__(self, *a):
        return False


class BB:
    SEMS = ("sWa", "sWb", "sBg", "sBu", "sX", "sXa", "sXb", "sYa", "sYb",
            "sAa", "sAb", "sLB", "sLC", "sWc", "sWd", "sYc", "sYd", "sBd",
            "sLA", "sLN0", "sLN1", "sW12", "sDB",
            "sR0", "sR1", "sR2", "sR3", "sST0", "sST1", "sST2", "sST3",
            "sPE", "sV", "sS", "sEx", "sP0", "sP1", "sP2", "sP3")

    def __init__(self):
        self.n = {s: 0 for s in self.SEMS}
        self.h = {}
        self.L = {}
        self.seq = []
        self.mode = "plan"

    def on(self, eng, f):
        self.seq.append((eng, f))

    def inc(self, inst, nm, amt, key=None):
        if self.mode == "plan":
            self.n[nm] += amt
            if key is not None:
                self.L[key] = self.n[nm]
        else:
            inst.then_inc(self.h[nm], amt)

    def mark(self, nm, key):
        if self.mode == "plan":
            self.L[key] = self.n[nm]

    def wait(self, e, nm, key):
        if self.mode == "emit":
            v = key if isinstance(key, int) else self.L[key]
            if v > 0:
                e.wait_ge(self.h[nm], v)

    def snap(self, key):
        if self.mode == "plan":
            self.L[key] = dict(self.n)

    def bar(self, e, key):
        if self.mode == "emit":
            for nm, v in self.L[key].items():
                if v > 0:
                    e.wait_ge(self.h[nm], v)

    def run_plan(self):
        self.mode = "plan"
        fake = _Any()
        for eng, f in self.seq:
            f(fake)

    def run_emit(self, eng, e):
        self.mode = "emit"
        for en, f in self.seq:
            if en == eng:
                f(e)


def build_program(stop_after="full", debug_dump=None, det=False):
    nc = bass.Bass("TRN2", target_bir_lowering=False,
                   detect_race_conditions=det)
    NP = 288

    resid_in = nc.dram_tensor("resid_in", [DT, 128, T], F32, kind="ExternalInput").ap()
    wg8 = nc.dram_tensor("wg8", [IT, 128, 4096], U8, kind="ExternalInput").ap()
    wu8 = nc.dram_tensor("wu8", [IT, 128, 4096], U8, kind="ExternalInput").ap()
    wd8 = nc.dram_tensor("wd8", [IT, 128, 4096], U8, kind="ExternalInput").ap()
    gsr = nc.dram_tensor("gsr", [IT, 4096], F32, kind="ExternalInput").ap()
    usr = nc.dram_tensor("usr", [IT, 4096], F32, kind="ExternalInput").ap()
    dsr = nc.dram_tensor("dsr", [IT, 4096], F32, kind="ExternalInput").ap()
    win_t = nc.dram_tensor("win_t", [DT, 128, DIM], F32, kind="ExternalInput").ap()
    w1t = nc.dram_tensor("w1t", [DEPTH, 4, 128, 2048], F32, kind="ExternalInput").ap()
    w2t = nc.dram_tensor("w2t", [DEPTH, 16, 128, DIM], F32, kind="ExternalInput").ap()
    wout_t = nc.dram_tensor("wout_t", [DEPTH, 128, 4096], F32,
                            kind="ExternalInput").ap()
    params = nc.dram_tensor("params", [128, NP], F32, kind="ExternalInput").ap()
    evenflag = nc.dram_tensor("evenflag", [1, 1], I32, kind="ExternalInput").ap()
    emaskin = nc.dram_tensor("emaskin", [128, 2], F32, kind="ExternalInput").ap()

    resid_out = nc.dram_tensor("resid_out", [DT, 128, T], F32,
                               kind="ExternalOutput").ap()

    wgbf = nc.dram_tensor("wgbf", [IT, 128, 4096], BF16, kind="Internal").ap()
    wubf = nc.dram_tensor("wubf", [IT, 128, 4096], BF16, kind="Internal").ap()
    wdbf = nc.dram_tensor("wdbf", [IT, 128, 4096], BF16, kind="Internal").ap()
    at_s = nc.dram_tensor("at_s", [TC, 128, IT, 512], BF16, kind="Internal").ap()
    resid_s = nc.dram_tensor("resid_s", [DT, 128, T], F32, kind="Internal").ap()
    xch = nc.dram_tensor("xch", [DEPTH, 128, 8], F32, kind="Internal",
                         addr_space="Shared").ap()
    xfl = nc.dram_tensor("xfl", [DEPTH, ADT], I32, kind="Internal",
                         addr_space="Shared").ap()
    dbg = None
    if debug_dump:
        shp = {"ybf": ([DT, 128, T], BF16), "at": ([TC, 128, IT, 512], BF16),
               "resid": ([DT, 128, T], F32), "h": ([ADT, 128, T], F32),
               "yct": ([1, 128, T], F32)}[debug_dump]
        dbg = nc.dram_tensor("dbg", shp[0], shp[1], kind="ExternalOutput").ap()

    bb = BB()
    L = bb.L
    on = bb.on
    es = ExitStack()
    with es:
        def sbt(name, shape, dt):
            return es.enter_context(nc.sbuf_tensor(name, shape, dt))

        par = sbt("par", [128, NP], F32)
        ones_k = sbt("ones_k", [128, 1], BF16)
        ones_b = sbt("ones_b", [1, 128], F32)
        rowsA = sbt("rowsA", [128, T], F32)
        rowsB = sbt("rowsB", [128, T], F32)
        r_rstd = rowsA[0:1, :]
        r_sq = rowsA[32:33, :]
        r_mean = rowsB[0:1, :]
        r_tmp = rowsB[32:33, :]
        magict = sbt("magict", [1, 1], I32)
        zerot = sbt("zerot", [1, 1], I32)
        rflag = sbt("rflag", [1, 1], I32)
        pollt = sbt("pollt", [1, 1], I32)
        emask = sbt("emask", [128, 2], F32)
        sx = sbt("sx", [128, 8], F32)
        rdat = sbt("rdat", [128, 8], F32)
        pwt = sbt("pwt", [128, 128], F32)
        prgs_r = sbt("prgs_r", [128, 512], F32)
        prgs_i = sbt("prgs_i", [128, 512], F32)
        nrd = sbt("nrd", [128, 8], F32)
        e0r = sbt("e0r", [128, NCH + 1], F32)
        e0i = sbt("e0i", [128, NCH + 1], F32)
        e1r = sbt("e1r", [128, NCH + 1], F32)
        e1i = sbt("e1i", [128, NCH + 1], F32)
        esn = sbt("esn", [128, NCH + 1], F32)
        s64a = sbt("s64a", [128, 64], F32)
        s64b = sbt("s64b", [128, 64], F32)

        ybf = sbt("ybf", [128, DT * T], BF16)
        bufA = sbt("bufA", [128, 4096], BF16)
        bufB = sbt("bufB", [128, 4096], BF16)
        bufC = sbt("bufC", [128, 4096], BF16)
        bufD = sbt("bufD", [128, 4096], BF16)
        sb_g = sbt("sb_g", [128, 2048], BF16)
        sb_u = sbt("sb_u", [128, 2048], BF16)
        sq0 = sbt("sq0", [128, T], BF16)
        sq1 = sbt("sq1", [128, T], BF16)
        rs_ = [sbt(f"rs{i}", [128, 512], F32) for i in range(4)]
        at0, at1 = sq0, sq1
        sil0 = rs_[3][:, :].bitcast(BF16)[:, 0:512]
        sil1 = rs_[3][:, :].bitcast(BF16)[:, 512:1024]

        psum = [es.enter_context(nc.psum_tensor(f"ps{i}", [128, 512], F32))
                for i in range(8)]

        def yb(c0, n):
            return ybf[:, c0:c0 + n]

        h_t = [b_[:, :].bitcast(F32) for b_ in (bufA, bufB, bufC, bufD)]
        y2bf = [yb(i * T, T) for i in range(ADT)]
        LAr = yb(8192, 4096).bitcast(F32)
        LBr = yb(12288, 4096).bitcast(F32)
        LAi = yb(16384, 4096).bitcast(F32)
        LBi = yb(20480, 4096).bitcast(F32)
        y_ct = yb(24576, 4096).bitcast(F32)
        w1bf = yb(28672, 8192)
        w2bf = yb(36864, 8192)
        woutbf = yb(45056, 16384)
        hbf = yb(61440, 2048)
        hbf2 = yb(63488, 2048)
        at_res = ybf[:, 0:IT * 512]


        blk = es.enter_context(nc.Block())

        def pcol(name):
            return par[:, _P[name]:_P[name] + 1]

        def pcol1(name):
            return par[0:1, _P[name]:_P[name] + 1]

        def mm(t, out, lhsT, rhs, start, stop):
            return t.matmul(out, lhsT, rhs, start=start, stop=stop,
                            skip_group_check=True)

        bank_free = [("sV", "p0_const")] * 8

        def cap_banks(idxs=range(8)):
            return [tuple(bank_free[b_]) for b_ in idxs]

        def wait_banks(t, cap):
            if bb.mode == "emit":
                seen = set()
                for nm, key in cap:
                    v = key if isinstance(key, int) else L.get(key, 0)
                    if v > 0 and (nm, v) not in seen:
                        seen.add((nm, v))
                        t.wait_ge(bb.h[nm], v)

        def set_bank_free(idxs, nm, key):
            for b_ in idxs:
                bank_free[b_] = (nm, key)

        def p0_sync(s):
            bb.inc(s.dma_start(par[:, :], params[:, :]), "sX", 16)
            bb.inc(s.dma_start(emask[:, :], emaskin[:, :]), "sX", 16)
            bb.inc(s.dma_start(rflag[:, :], evenflag[:, :]), "sX", 16, "p0_ld")

        def p0_vec(v):
            bb.wait(v, "sX", "p0_ld")
            v.memset(ones_k[:, :], 1.0)
            v.memset(ones_b[:, :], 1.0)
            v.memset(magict[:, :], MAGIC)
            v.memset(rdat[:, :], 0.0)
            bb.inc(v.memset(zerot[:, :], 0), "sV", 1, "p0_const")

        on("sync", p0_sync)
        on("vector", p0_vec)

        def emit_ln(tag, src, pre_bar=None):
            def f_gp(g):
                bb.wait(g, "sV", "p0_const")
                if pre_bar:
                    bb.bar(g, pre_bar)
                for dt in range(DT):
                    if dt >= 2:
                        bb.wait(g, "sPE", (tag, "st", dt - 2))
                    bb.inc(g.dma_start(ybf[:, dt * T:(dt + 1) * T], src[dt, :, :]),
                           ("sLN0", "sLN1")[dt % 2], 16, (tag, "x", dt))
            on("gpsimd", f_gp)

            def f_sq(sc):
                for dt in range(DT):
                    sq = (sq0, sq1)[dt % 2]
                    bb.wait(sc, ("sLN0", "sLN1")[dt % 2], (tag, "x", dt))
                    if dt >= 2:
                        bb.wait(sc, "sPE", (tag, "st", dt - 2))
                    bb.inc(sc.square(sq[:, :], ybf[:, dt * T:(dt + 1) * T]),
                           "sS", 1, (tag, "sq", dt))
            on("scalar", f_sq)

            bcap = cap_banks()

            def f_pe(t):
                bb.wait(t, "sV", "p0_const")
                for dt in range(DT):
                    sq = (sq0, sq1)[dt % 2]
                    bb.wait(t, "sS", (tag, "sq", dt))
                    if dt == 0:
                        wait_banks(t, bcap)
                    for tc in range(TC):
                        mm(t, psum[tc][0:1, :], ones_k[:, :],
                           ybf[:, dt * T + tc * 512:dt * T + (tc + 1) * 512],
                           dt == 0, dt == DT - 1)
                        li = mm(t, psum[4 + tc][0:1, :], ones_k[:, :],
                                sq[:, tc * 512:(tc + 1) * 512],
                                dt == 0, dt == DT - 1)
                    bb.inc(li, "sPE", 1, (tag, "st", dt))
            on("tensor", f_pe)

            def f_v1(v):
                bb.wait(v, "sPE", (tag, "st", DT - 1))
                for tc in range(TC):
                    v.tensor_copy(r_mean[0:1, tc * 512:(tc + 1) * 512],
                                  psum[tc][0:1, :])
                    v.tensor_copy(r_sq[0:1, tc * 512:(tc + 1) * 512],
                                  psum[4 + tc][0:1, :])
                v.tensor_scalar_mul(r_mean[0:1, :], r_mean[0:1, :], 1.0 / 4096.0)
                bb.inc(v.tensor_scalar_mul(r_sq[0:1, :], r_sq[0:1, :],
                                           1.0 / 4096.0),
                       "sV", 1, (tag, "ex2"))
            on("vector", f_v1)

            def f_msq(sc):
                bb.wait(sc, "sV", (tag, "ex2"))
                for tc in range(TC):
                    bb.inc(sc.square(psum[4 + tc][64:65, :],
                                     r_mean[0:1, tc * 512:(tc + 1) * 512]),
                           "sS", 1, (tag, "msq", tc))
            on("scalar", f_msq)

            def f_var(v):
                bb.wait(v, "sS", (tag, "msq", TC - 1))
                for tc in range(TC):
                    v.tensor_sub(r_sq[0:1, tc * 512:(tc + 1) * 512],
                                 r_sq[0:1, tc * 512:(tc + 1) * 512],
                                 psum[4 + tc][64:65, :])
                bb.inc(v.tensor_scalar_add(r_sq[0:1, :], r_sq[0:1, :], EPS),
                       "sV", 1, (tag, "var"))
            on("vector", f_var)

            def f_sc2(sc):
                bb.wait(sc, "sV", (tag, "var"))
                sc.activation(r_tmp[0:1, :], r_sq[0:1, :], AF.Ln)
                bb.inc(sc.activation(r_rstd[0:1, :], r_tmp[0:1, :], AF.Exp,
                                     scale=pcol1("c_neghalf")),
                       "sS", 1, (tag, "rstd"))
            on("scalar", f_sc2)

            def f_pe2(t):
                bb.wait(t, "sS", (tag, "rstd"))
                for tc in range(TC):
                    mm(t, psum[tc][:, :], ones_b[0:1, :],
                       r_rstd[0:1, tc * 512:(tc + 1) * 512], True, True)
                    li = mm(t, psum[4 + tc][:, :], ones_b[0:1, :],
                            r_mean[0:1, tc * 512:(tc + 1) * 512], True, True)
                bb.inc(li, "sPE", 1, (tag, "bcast"))
            on("tensor", f_pe2)

            def f_v3(v):
                bb.wait(v, "sPE", (tag, "bcast"))
                for dt in range(DT):
                    for tc in range(TC):
                        s5 = slice(dt * T + tc * 512, dt * T + (tc + 1) * 512)
                        v.tensor_sub(sq1[:, tc * 512:(tc + 1) * 512],
                                     ybf[:, s5], psum[4 + tc][:, :])
                        li = v.tensor_mul(ybf[:, s5],
                                          sq1[:, tc * 512:(tc + 1) * 512],
                                          psum[tc][:, :])
                bb.inc(li, "sV", 1, (tag, "done"))
            on("vector", f_v3)
            set_bank_free(range(8), "sV", (tag, "done"))

        def emit_gateup(tag, call, ln_tag, pre_bar=None):
            parbufs = [(bufA, bufC), (bufB, bufD)]

            if call == 1:
                def f_gp(g):
                    if pre_bar:
                        bb.bar(g, pre_bar)
                    for k in range(IT):
                        gq, uq = parbufs[k % 2]
                        if k >= 2:
                            bb.wait(g, "sPE", (tag, "udone", k - 2))
                            bb.wait(g, ("sYa", "sYb")[k % 2], (tag, "wb", k - 2))
                        sw = ("sWa", "sWb")[k % 2]
                        bb.inc(g.dma_start(gq[:, :], wg8[k, :, :]), sw, 16)
                        bb.inc(g.dma_start(uq[:, :], wu8[k, :, :]), sw, 16,
                               (tag, "cast", k))
                        for hh in range(2):
                            if k >= 1 or hh == 1:
                                pk, ph = (k, 0) if hh == 1 else (k - 1, 1)
                                bb.wait(g, "sV", (tag, "dqg", pk, ph))
                                bb.wait(g, "sV", (tag, "dqu", pk, ph))
                            src_g = bass.AP(gsr.tensor, k * 4096 + hh * 2048,
                                            [[0, 128], [1, 2048]])
                            src_u = bass.AP(usr.tensor, k * 4096 + hh * 2048,
                                            [[0, 128], [1, 2048]])
                            bb.inc(g.dma_start(sb_g[:, :], src_g), "sBg", 16,
                                   (tag, "bcg", k, hh))
                            bb.inc(g.dma_start(sb_u[:, :], src_u), "sBu", 16,
                                   (tag, "bcu", k, hh))
                        for j in range(4):
                            q = 4 * k + j
                            wq = (rs_[0], rs_[2])[q % 2][:, :].bitcast(BF16)
                            if q >= 2:
                                bb.wait(g, ("sYc", "sYd")[q % 2],
                                        (tag, "wdw", q - 2))
                            bb.inc(g.dma_start(wq, wd8[k, :, j * 1024:
                                                       (j + 1) * 1024]),
                                   ("sWc", "sWd")[q % 2], 16, (tag, "wdc", q))
                            if q >= 1:
                                bb.wait(g, "sV", (tag, "wds", q - 1))
                            src_d = bass.AP(dsr.tensor, k * 4096 + j * 1024,
                                            [[0, 128], [1, 1024]])
                            bb.inc(g.dma_start(rs_[1][:, :].bitcast(BF16),
                                               src_d), "sBd", 16,
                                   (tag, "wdb", q))
                on("gpsimd", f_gp)

                def f_vdq_atm(v):
                    def atm(k):
                        att = (at0, at1)[k % 2]
                        bb.wait(v, "sPE", (tag, "udone", k))
                        if k >= 2:
                            bb.wait(v, ("sAa", "sAb")[k % 2], (tag, "ats", k - 2))
                        for tc in range(TC):
                            sil = (sil0, sil1)[tc % 2]
                            bb.wait(v, "sS", (tag, "silt", k, tc))
                            bb.inc(v.tensor_mul(att[:, tc * 512:(tc + 1) * 512],
                                                sil[:, :], psum[4 + tc][:, :]),
                                   "sV", 1, (tag, "atmt", k, tc))
                    for k in range(IT):
                        gq, uq = parbufs[k % 2]
                        bb.wait(v, ("sWa", "sWb")[k % 2], (tag, "cast", k))
                        for hh in range(2):
                            sl = slice(hh * 2048, (hh + 1) * 2048)
                            bb.wait(v, "sBg", (tag, "bcg", k, hh))
                            bb.inc(v.scalar_tensor_tensor(
                                gq[:, sl], gq[:, sl], 8.0, sb_g[:, :],
                                op0=ALU.subtract, op1=ALU.mult),
                                "sV", 1, (tag, "dqg", k, hh))
                            bb.wait(v, "sBu", (tag, "bcu", k, hh))
                            bb.inc(v.scalar_tensor_tensor(
                                uq[:, sl], uq[:, sl], 8.0, sb_u[:, :],
                                op0=ALU.subtract, op1=ALU.mult),
                                "sV", 1, (tag, "dqu", k, hh))
                        bb.mark("sV", (tag, "dq", k))
                        if k >= 1:
                            atm(k - 1)
                        for j in range(4):
                            q = 4 * k + j
                            wq = (rs_[0], rs_[2])[q % 2][:, :].bitcast(BF16)
                            bb.wait(v, ("sWc", "sWd")[q % 2], (tag, "wdc", q))
                            bb.wait(v, "sBd", (tag, "wdb", q))
                            bb.inc(v.scalar_tensor_tensor(
                                wq, wq, 8.0, rs_[1][:, :].bitcast(BF16),
                                op0=ALU.subtract, op1=ALU.mult),
                                "sV", 1, (tag, "wds", q))
                    atm(IT - 1)
                on("vector", f_vdq_atm)

                def f_wb_sil(sc):
                    for k in range(IT):
                        gq, uq = parbufs[k % 2]
                        bb.wait(sc, "sV", (tag, "dq", k))
                        sy = ("sYa", "sYb")[k % 2]
                        bb.inc(sc.dma_start(wgbf[k, :, :], gq[:, :]), sy, 16)
                        bb.inc(sc.dma_start(wubf[k, :, :], uq[:, :]), sy, 16,
                               (tag, "wb", k))
                        for j in range(4):
                            q = 4 * k + j
                            wq = (rs_[0], rs_[2])[q % 2][:, :].bitcast(BF16)
                            bb.wait(sc, "sV", (tag, "wds", q))
                            if q >= 2:
                                bb.wait(sc, ("sYc", "sYd")[q % 2],
                                        (tag, "wdw", q - 2))
                            bb.inc(sc.dma_start(
                                wdbf[k, :, j * 1024:(j + 1) * 1024], wq),
                                ("sYc", "sYd")[q % 2], 16, (tag, "wdw", q))
                        bb.wait(sc, "sPE", (tag, "gdone", k))
                        for tc in range(TC):
                            sil = (sil0, sil1)[tc % 2]
                            if k > 0 or tc >= 2:
                                kk2, tc2 = ((k, tc - 2) if tc >= 2
                                            else (k - 1, tc + 2))
                                bb.wait(sc, "sV", (tag, "atmt", kk2, tc2))
                            bb.inc(sc.activation(sil[:, :], psum[tc][:, :],
                                                 AF.Silu),
                                   "sS", 1, (tag, "silt", k, tc))
                on("scalar", f_wb_sil)
            else:
                def f_ld(s):
                    if pre_bar:
                        bb.bar(s, pre_bar)
                    for k in range(IT):
                        gq, uq = parbufs[k % 2]
                        if k >= 2:
                            bb.wait(s, "sPE", (tag, "udone", k - 2))
                        sw = ("sXa", "sXb")[k % 2]
                        bb.inc(s.dma_start(gq[:, :], wgbf[k, :, :]), sw, 16)
                        bb.inc(s.dma_start(uq[:, :], wubf[k, :, :]), sw, 16,
                               (tag, "ld", k))
                on("sync", f_ld)

                def f_atm2(v):
                    for k in range(IT):
                        att = (at0, at1)[k % 2]
                        bb.wait(v, "sPE", (tag, "udone", k))
                        if k >= 2:
                            bb.wait(v, ("sAa", "sAb")[k % 2], (tag, "ats", k - 2))
                        for tc in range(TC):
                            sil = (sil0, sil1)[tc % 2]
                            bb.wait(v, "sS", (tag, "silt", k, tc))
                            bb.inc(v.tensor_mul(att[:, tc * 512:(tc + 1) * 512],
                                                sil[:, :], psum[4 + tc][:, :]),
                                   "sV", 1, (tag, "atmt", k, tc))
                on("vector", f_atm2)

                def f_sil_ats(sc):
                    for k in range(IT):
                        att = (at0, at1)[k % 2]
                        bb.wait(sc, "sPE", (tag, "gdone", k))
                        for tc in range(TC):
                            sil = (sil0, sil1)[tc % 2]
                            if k > 0 or tc >= 2:
                                kk2, tc2 = ((k, tc - 2) if tc >= 2
                                            else (k - 1, tc + 2))
                                bb.wait(sc, "sV", (tag, "atmt", kk2, tc2))
                            bb.inc(sc.activation(sil[:, :], psum[tc][:, :],
                                                 AF.Silu),
                                   "sS", 1, (tag, "silt", k, tc))
                        if k >= 1:
                            for tc in range(TC):
                                bb.wait(sc, "sV", (tag, "atmt", k - 1, tc))
                                bb.inc(sc.dma_start(
                                    at_s[tc, :, k - 1, :],
                                    (at0, at1)[(k - 1) % 2]
                                    [:, tc * 512:(tc + 1) * 512]),
                                    ("sAa", "sAb")[(k - 1) % 2], 16,
                                    (tag, "ats", k - 1) if tc == TC - 1
                                    else None)
                    for tc in range(TC):
                        bb.wait(sc, "sV", (tag, "atmt", IT - 1, tc))
                        bb.inc(sc.dma_start(
                            at_s[tc, :, IT - 1, :],
                            (at0, at1)[(IT - 1) % 2][:, tc * 512:(tc + 1) * 512]),
                            ("sAa", "sAb")[(IT - 1) % 2], 16,
                            (tag, "ats", IT - 1) if tc == TC - 1 else None)
                on("scalar", f_sil_ats)

            bcap = cap_banks()

            def f_pe(t):
                bb.wait(t, "sV", (ln_tag, "done"))
                for k in range(IT):
                    gq, uq = parbufs[k % 2]
                    if call == 1:
                        bb.wait(t, "sV", (tag, "dq", k))
                    else:
                        bb.wait(t, ("sXa", "sXb")[k % 2], (tag, "ld", k))
                    if k == 0:
                        wait_banks(t, bcap)
                    if k >= 1:
                        bb.wait(t, "sS", (tag, "silt", k - 1, TC - 1))
                    for kk in range(DT):
                        wsl = gq[:, kk * 128:(kk + 1) * 128]
                        for tc in range(TC):
                            li = mm(t, psum[tc][:, :], wsl,
                                    ybf[:, kk * T + tc * 512:
                                        kk * T + (tc + 1) * 512],
                                    kk == 0, kk == DT - 1)
                    bb.inc(li, "sPE", 1, (tag, "gdone", k))
                    if k >= 1:
                        bb.wait(t, "sV", (tag, "atmt", k - 1, TC - 1))
                    for kk in range(DT):
                        wsl = uq[:, kk * 128:(kk + 1) * 128]
                        for tc in range(TC):
                            li = mm(t, psum[4 + tc][:, :], wsl,
                                    ybf[:, kk * T + tc * 512:
                                        kk * T + (tc + 1) * 512],
                                    kk == 0, kk == DT - 1)
                    bb.inc(li, "sPE", 1, (tag, "udone", k))
            on("tensor", f_pe)

            if call == 1:
                def f_ats(e):
                    for k in range(IT):
                        att = (at0, at1)[k % 2]
                        for tc in range(TC):
                            bb.wait(e, "sV", (tag, "atmt", k, tc))
                            bb.inc(e.dma_start(at_s[tc, :, k, :],
                                               att[:, tc * 512:(tc + 1) * 512]),
                                   ("sAa", "sAb")[k % 2], 16,
                                   (tag, "ats", k) if tc == TC - 1 else None)
                on("sync", f_ats)

            set_bank_free(range(4), "sS", (tag, "silt", IT - 1, TC - 1))
            set_bank_free(range(4, 8), "sV", (tag, "atmt", IT - 1, TC - 1))

        def emit_down(tag, call, rsrc, rdst, pre_bar, out_sem):
            parw = [bufA, bufB, bufC, bufD]
            sbb = [sb_g, sb_u]
            kblocks = []
            kk = 0
            while kk < IT:
                kblocks.append((kk, min(2, IT - kk)))
                kk += 2
            NKB = len(kblocks)
            dqmode = False

            def wd_re(ap3):
                return ap3.rearrange("k p j -> p k j")

            KH = 43
            slots = [ybf[:, 0:KH * 512], ybf[:, KH * 512:2 * KH * 512], None]

            def slot_tile(s, t):
                if s < 2:
                    return slots[s][:, t * 512:(t + 1) * 512]
                if t < 42:
                    return ybf[:, (86 + t) * 512:(86 + t + 1) * 512]
                return sq0[:, 0:512]

            def slotA(tc):
                return (2 * tc) % 3

            def slotB(tc):
                return (2 * tc + 1) % 3

            def at_rd(tc, k):
                if k < KH:
                    return slot_tile(slotA(tc), k)
                return slot_tile(slotB(tc), k - KH)

            def lda_emit(e, sem, tc, half, key):
                s = slotA(tc) if half == 0 else slotB(tc)
                k0 = 0 if half == 0 else KH
                if s < 2:
                    bb.inc(e.dma_start(
                        slots[s][:, :],
                        at_s[tc, :, k0:k0 + KH, :].rearrange("p k j -> p (k j)")),
                        sem, 16, key)
                else:
                    bb.inc(e.dma_start(
                        ybf[:, 86 * 512:128 * 512],
                        at_s[tc, :, k0:k0 + 42, :].rearrange("p k j -> p (k j)")),
                        sem, 16)
                    bb.inc(e.dma_start(sq0[:, 0:512], at_s[tc, :, k0 + 42, :]),
                           sem, 16, key)

            def f_sync(s):
                bb.bar(s, pre_bar)
                lda_emit(s, "sLA", 0, 0, (tag, "ldA", 0))
                lda_emit(s, "sLC", 0, 1, None)
                lda_emit(s, "sLC", 1, 0, (tag, "ldC0"))
                prevuse = {}
                for tc in range(TC):
                    for o in range(4):
                        for m in range(8):
                            dt = o * 8 + m
                            rs = rs_[m % 4]
                            pk = prevuse.get(m % 4)
                            if pk is not None:
                                bb.wait(s, "sV", pk[0])
                                bb.wait(s, f"sST{m % 4}", pk[1])
                            bb.inc(s.dma_start(
                                rs[:, :], rsrc[dt, :, tc * 512:(tc + 1) * 512]),
                                f"sR{m % 4}", 16, (tag, "rld", tc, o, m))
                            prevuse[m % 4] = ((tag, "evac", tc, o, m),
                                              (tag, "ost", tc, o, m))
            on("sync", f_sync)

            def f_ldw(g):
                if pre_bar:
                    bb.bar(g, pre_bar)
                bidx = 0
                for tc in range(TC):
                    for o in range(4):
                        for bi, (k0, nk) in enumerate(kblocks):
                            wbuf = parw[bidx % 4]
                            if bidx >= 4:
                                bb.wait(g, "sPE", (tag, "wused", bidx - 4))
                            sw = ("sWa", "sWb", "sWc", "sWd")[bidx % 4]
                            if dqmode and tc == 0:
                                bb.inc(g.dma_start(
                                    wbuf[:, 0:nk * 1024],
                                    wd_re(wd8)[:, k0:k0 + nk,
                                               o * 1024:(o + 1) * 1024]),
                                    sw, 16, (tag, "wld", bidx))
                                if bidx >= 2:
                                    bb.wait(g, "sV", (tag, "wdqp", bidx - 2))
                                src_s = bass.AP(
                                    dsr.tensor, k0 * 4096 + o * 1024,
                                    [[0, 128], [4096, nk], [1, 1024]])
                                bb.inc(g.dma_start(
                                    sbb[bidx % 2][:, 0:nk * 1024], src_s),
                                    ("sBg", "sBu")[bidx % 2], 16,
                                    (tag, "wbc", bidx))
                            else:
                                if dqmode:
                                    bb.wait(g, ("sYa", "sYb", "sYc", "sYd")
                                            [(o * NKB + bi) % 4],
                                            (tag, "wwb", o * NKB + bi))
                                bb.inc(g.dma_start(
                                    wbuf[:, 0:nk * 1024],
                                    wd_re(wdbf)[:, k0:k0 + nk,
                                                o * 1024:(o + 1) * 1024]),
                                    sw, 16, (tag, "wld", bidx))
                            bidx += 1
            on("gpsimd", f_ldw)

            def f_wdq_ev(v):
                for tc in range(TC):
                    for o in range(4):
                        if dqmode and tc == 0:
                            for bi, (k0, nk) in enumerate(kblocks):
                                bidx = o * NKB + bi
                                wbuf = parw[bidx % 4]
                                bb.wait(v, ("sWa", "sWb", "sWc", "sWd")
                                        [bidx % 4], (tag, "wld", bidx))
                                bb.wait(v, ("sBg", "sBu")[bidx % 2],
                                        (tag, "wbc", bidx))
                                bb.inc(v.scalar_tensor_tensor(
                                    wbuf[:, 0:nk * 1024],
                                    wbuf[:, 0:nk * 1024], 8.0,
                                    sbb[bidx % 2][:, 0:nk * 1024],
                                    op0=ALU.subtract, op1=ALU.mult),
                                    "sV", 1, (tag, "wdqp", bidx))
                        bb.wait(v, "sPE", (tag, "odone", tc, o))
                        for m in range(8):
                            rs = rs_[m % 4]
                            bb.wait(v, f"sR{m % 4}", (tag, "rld", tc, o, m))
                            bb.inc(v.scalar_tensor_tensor(
                                rs[:, :], psum[m][:, :], 0.5, rs[:, :],
                                op0=ALU.mult, op1=ALU.add),
                                "sV", 1, (tag, "evac", tc, o, m))
            on("vector", f_wdq_ev)

            bcap = cap_banks()

            def f_pe(t):
                bidx = 0
                for tc in range(TC):
                    awaits = {0: ("sLA", (tag, "ldA", 0)),
                              1: ("sLC", (tag, "ldC0")),
                              2: ("sLB", (tag, "ldpair", 0)),
                              3: ("sLB", (tag, "ldpair", 1))}[tc]
                    bwaits = {0: ("sLC", (tag, "ldC0")),
                              1: ("sLB", (tag, "ldpair", 0)),
                              2: ("sLB", (tag, "ldpair", 1)),
                              3: ("sLB", (tag, "ldpair", 2))}[tc]
                    bb.wait(t, awaits[0], awaits[1])
                    for o in range(4):
                        if (tc, o) == (0, 0):
                            wait_banks(t, bcap)
                        else:
                            po = (tc, o - 1) if o > 0 else (tc - 1, 3)
                            bb.wait(t, "sV", (tag, "evac", po[0], po[1], 7))
                        for bi, (k0, nk) in enumerate(kblocks):
                            wbuf = parw[bidx % 4]
                            if k0 >= KH or (k0 + nk > KH):
                                bb.wait(t, bwaits[0], bwaits[1])
                            if dqmode and tc == 0:
                                bb.wait(t, "sV", (tag, "wdqp", bidx))
                            else:
                                bb.wait(t, ("sWa", "sWb", "sWc", "sWd")
                                        [bidx % 4], (tag, "wld", bidx))
                            for ki in range(nk):
                                k = k0 + ki
                                for m_ in range(8):
                                    li = mm(t, psum[m_][:, :],
                                            wbuf[:, ki * 1024 + m_ * 128:
                                                 ki * 1024 + (m_ + 1) * 128],
                                            at_rd(tc, k),
                                            k == 0, k == IT - 1)
                            bb.inc(li, "sPE", 1, (tag, "wused", bidx))
                            bidx += 1
                        bb.mark("sPE", (tag, "odone", tc, o))
            on("tensor", f_pe)

            def f_wwb_ost(sc):
                for tc in range(TC):
                    for o in range(4):
                        if dqmode and tc == 0:
                            for bi, (k0, nk) in enumerate(kblocks):
                                bidx = o * NKB + bi
                                wbuf = parw[bidx % 4]
                                bb.wait(sc, "sV", (tag, "wdqp", bidx))
                                if bidx >= 4:
                                    bb.wait(sc,
                                            ("sYa", "sYb", "sYc", "sYd")
                                            [bidx % 4], (tag, "wwb", bidx - 4))
                                bb.inc(sc.dma_start(
                                    wd_re(wdbf)[:, k0:k0 + nk,
                                                o * 1024:(o + 1) * 1024],
                                    wbuf[:, 0:nk * 1024]),
                                    ("sYa", "sYb", "sYc", "sYd")[bidx % 4], 16,
                                    (tag, "wwb", bidx))
                        for m in range(8):
                            dt = o * 8 + m
                            rs = rs_[m % 4]
                            bb.wait(sc, "sV", (tag, "evac", tc, o, m))
                            bb.inc(sc.dma_start(
                                rdst[dt, :, tc * 512:(tc + 1) * 512], rs[:, :]),
                                f"sST{m % 4}", 16, (tag, "ost", tc, o, m))
                    if tc + 1 < TC:
                        lda_emit(sc, "sLB", tc + 1, 1, None)
                    if tc + 2 < TC:
                        lda_emit(sc, "sLB", tc + 2, 0, None)
                    if tc + 1 < TC:
                        bb.mark("sLB", (tag, "ldpair", tc))
            on("scalar", f_wwb_ost)
            set_bank_free(range(8), "sV", (tag, "evac", TC - 1, 3, 7))

        def emit_attn_ln(tag, hready_key):
            def f_cast(sc):
                for dt in range(ADT):
                    hb = (hbf, hbf2)[dt % 2]
                    sq = (sq0, sq1)[dt % 2]
                    bb.wait(sc, "sV", hready_key(dt))
                    if dt >= 2:
                        bb.wait(sc, "sPE", (tag, "st", dt - 2))
                    sc.copy(hb[:, :], h_t[dt])
                    bb.inc(sc.square(sq[:, 0:T], h_t[dt]), "sS", 1,
                           (tag, "sq", dt))
            on("scalar", f_cast)

            bcap = cap_banks()

            def f_pe(t):
                for dt in range(ADT):
                    hb = (hbf, hbf2)[dt % 2]
                    sq = (sq0, sq1)[dt % 2]
                    bb.wait(t, "sS", (tag, "sq", dt))
                    if dt == 0:
                        wait_banks(t, bcap)
                    for tc in range(TC):
                        mm(t, psum[tc][0:1, :], ones_k[:, :],
                           hb[:, tc * 512:(tc + 1) * 512], dt == 0, dt == ADT - 1)
                        li = mm(t, psum[4 + tc][0:1, :], ones_k[:, :],
                                sq[:, tc * 512:(tc + 1) * 512],
                                dt == 0, dt == ADT - 1)
                    bb.inc(li, "sPE", 1, (tag, "st", dt))
            on("tensor", f_pe)

            def f_v1(v):
                bb.wait(v, "sPE", (tag, "st", ADT - 1))
                for tc in range(TC):
                    v.tensor_copy(r_mean[0:1, tc * 512:(tc + 1) * 512],
                                  psum[tc][0:1, :])
                    v.tensor_copy(r_sq[0:1, tc * 512:(tc + 1) * 512],
                                  psum[4 + tc][0:1, :])
                v.tensor_scalar_mul(r_mean[0:1, :], r_mean[0:1, :], 1.0 / 512.0)
                bb.inc(v.tensor_scalar_mul(r_sq[0:1, :], r_sq[0:1, :],
                                           1.0 / 512.0),
                       "sV", 1, (tag, "ex2"))
            on("vector", f_v1)

            def f_msq(sc):
                bb.wait(sc, "sV", (tag, "ex2"))
                for tc in range(TC):
                    bb.inc(sc.square(psum[4 + tc][64:65, :],
                                     r_mean[0:1, tc * 512:(tc + 1) * 512]),
                           "sS", 1, (tag, "msq", tc))
            on("scalar", f_msq)

            def f_var(v):
                bb.wait(v, "sS", (tag, "msq", TC - 1))
                for tc in range(TC):
                    v.tensor_sub(r_sq[0:1, tc * 512:(tc + 1) * 512],
                                 r_sq[0:1, tc * 512:(tc + 1) * 512],
                                 psum[4 + tc][64:65, :])
                bb.inc(v.tensor_scalar_add(r_sq[0:1, :], r_sq[0:1, :], EPS),
                       "sV", 1, (tag, "var"))
            on("vector", f_var)

            def f_sc2(sc):
                bb.wait(sc, "sV", (tag, "var"))
                sc.activation(r_tmp[0:1, :], r_sq[0:1, :], AF.Ln)
                bb.inc(sc.activation(r_rstd[0:1, :], r_tmp[0:1, :], AF.Exp,
                                     scale=pcol1("c_neghalf")),
                       "sS", 1, (tag, "rstd"))
            on("scalar", f_sc2)

            def f_pe2(t):
                bb.wait(t, "sS", (tag, "rstd"))
                for tc in range(TC):
                    mm(t, psum[tc][:, :], ones_b[0:1, :],
                       r_rstd[0:1, tc * 512:(tc + 1) * 512], True, True)
                    li = mm(t, psum[4 + tc][:, :], ones_b[0:1, :],
                            r_mean[0:1, tc * 512:(tc + 1) * 512], True, True)
                bb.inc(li, "sPE", 1, (tag, "bcast"))
            on("tensor", f_pe2)

        def emit_attn_norm_out(tag, wname, bname, func, outf):
            def f_vn(v):
                bb.wait(v, "sPE", (tag, "bcast"))
                for dt in range(ADT):
                    for tc in range(TC):
                        i = dt * TC + tc
                        ra, rb = ((rs_[0], rs_[1]) if i % 2 == 0
                                  else (rs_[2], rs_[3]))
                        if i >= 2:
                            bb.wait(v, "sS", (tag, "out", i - 2))
                        s5 = slice(tc * 512, (tc + 1) * 512)
                        v.tensor_sub(ra[:, :], h_t[dt][:, s5], psum[4 + tc][:, :])
                        bb.inc(v.tensor_mul(rb[:, :], ra[:, :], psum[tc][:, :]),
                               "sV", 1, (tag, "nt", i))
            on("vector", f_vn)

            def f_so(sc):
                for dt in range(ADT):
                    for tc in range(TC):
                        i = dt * TC + tc
                        rb = rs_[1] if i % 2 == 0 else rs_[3]
                        bb.wait(sc, "sV", (tag, "nt", i))
                        kw = {}
                        if wname is not None:
                            kw = dict(scale=pcol(f"{wname}{dt}"),
                                      bias=pcol(f"{bname}{dt}"))
                        bb.inc(sc.activation(outf(dt)[:, tc * 512:(tc + 1) * 512],
                                             rb[:, :], func, **kw),
                               "sS", 1, (tag, "out", i))
                bb.mark("sS", (tag, "outdone"))
            on("scalar", f_so)
            set_bank_free(range(8), "sV", (tag, "nt", ADT * TC - 1))

        def emit_scan(l, ct, mid=None):
            tag = ("sc", l, ct)
            pw = pwt[:, ct * 32:(ct + 1) * 32]
            prg_r = prgs_r[:, ct * CH:(ct + 1) * CH]
            prg_i = prgs_i[:, ct * CH:(ct + 1) * CH]

            def f_tables(v):
                bb.wait(v, "sS", ("y", l, ct))
                pr = pcol(f"phr_{l}_{ct}")
                pi = pcol(f"phi_{l}_{ct}")
                v.tensor_mul(s64a[:, 0:1], pr, pr)
                v.tensor_mul(s64b[:, 0:1], pi, pi)
                v.drain()
                bb.inc(v.scalar_tensor_tensor(s64a[:, 1:2], s64a[:, 0:1],
                                              1e-38, s64b[:, 0:1],
                                              op0=ALU.add, op1=ALU.add),
                       "sV", 1, (tag, "r2"))
            on("vector", f_tables)

            def f_pr(sc):
                bb.wait(sc, "sV", (tag, "r2"))
                sc.activation(s64a[:, 2:3], s64a[:, 1:2], AF.Ln)
                sc.activation(s64a[:, 3:4], s64a[:, 2:3], AF.Exp,
                              scale=pcol("c_half"))
                bb.inc(sc.activation(s64a[:, 4:5], s64a[:, 3:4], AF.Tanh),
                       "sS", 1, (tag, "tanh"))
            on("scalar", f_pr)

            def f_pw(v):
                bb.wait(v, "sS", (tag, "tanh"))
                v.tensor_scalar_max(s64a[:, 3:4], s64a[:, 3:4], 1e-30)
                v.drain()
                v.reciprocal(s64b[:, 0:1], s64a[:, 3:4])
                v.drain()
                v.tensor_mul(s64a[:, 5:6], s64a[:, 4:5], s64b[:, 0:1])
                v.drain()
                v.tensor_mul(pw[:, 0:1], pcol(f"phr_{l}_{ct}"), s64a[:, 5:6])
                v.tensor_mul(pw[:, 1:2], pcol(f"phi_{l}_{ct}"), s64a[:, 5:6])
                for m in range(1, 12):
                    a = pw[:, 2 * m - 2:2 * m - 1]
                    b = pw[:, 2 * m - 1:2 * m]
                    v.tensor_mul(s64a[:, 6:7], a, a)
                    v.tensor_mul(s64b[:, 1:2], b, b)
                    v.tensor_mul(s64a[:, 7:8], a, b)
                    v.tensor_sub(pw[:, 2 * m:2 * m + 1], s64a[:, 6:7],
                                 s64b[:, 1:2])
                    v.tensor_add(pw[:, 2 * m + 1:2 * m + 2], s64a[:, 7:8],
                                 s64a[:, 7:8])
                v.tensor_copy(prg_i[:, 0:1], pw[:, 1:2])
                v.tensor_copy(prg_r[:, 0:1], pw[:, 0:1])
                for m in range(7):
                    w = 1 << m
                    ar = pw[:, 2 * m:2 * m + 1]
                    ai = pw[:, 2 * m + 1:2 * m + 2]
                    v.tensor_scalar_mul(s64b[:, 0:w], prg_i[:, 0:w], ar)
                    v.tensor_scalar_mul(s64a[:, 0:w], prg_i[:, 0:w], ai)
                    v.scalar_tensor_tensor(prg_i[:, w:2 * w], prg_r[:, 0:w],
                                           ai, s64b[:, 0:w],
                                           op0=ALU.mult, op1=ALU.add)
                    li = v.scalar_tensor_tensor(prg_r[:, w:2 * w],
                                                prg_r[:, 0:w],
                                                ar, s64a[:, 0:w],
                                                op0=ALU.mult, op1=ALU.subtract)
                bb.inc(li, "sV", 1, (tag, "tables"))
            on("vector", f_pw)

            if mid is not None:
                mid()

            def f_intra(v):
                y3 = y_ct.rearrange("p (c j) -> p c j", c=NCH)
                Ar = LAr.rearrange("p (c j) -> p c j", c=NCH)
                Ai = LAi.rearrange("p (c j) -> p c j", c=NCH)
                Br = LBr.rearrange("p (c j) -> p c j", c=NCH)
                Bi = LBi.rearrange("p (c j) -> p c j", c=NCH)
                v.scalar_tensor_tensor(Ar[:, :, 1:], y3[:, :, :CH - 1],
                                       pw[:, 0:1], y3[:, :, 1:],
                                       op0=ALU.mult, op1=ALU.add)
                v.tensor_scalar_mul(Ai[:, :, 1:], y3[:, :, :CH - 1], pw[:, 1:2])
                v.tensor_copy(Ar[:, :, 0:1], y3[:, :, 0:1])
                v.memset(Ai[:, :, 0:1], 0.0)
                cur = (Ar, Ai)
                oth = (Br, Bi)
                for m in range(1, 7):
                    d = 1 << m
                    sr, si = cur
                    dr, di = oth
                    ar = pw[:, 2 * m:2 * m + 1]
                    ai = pw[:, 2 * m + 1:2 * m + 2]
                    nai = s64a[:, 8:9]
                    v.tensor_scalar_mul(nai, ai, -1.0)
                    v.scalar_tensor_tensor(dr[:, :, d:], sr[:, :, :CH - d], ar,
                                           sr[:, :, d:],
                                           op0=ALU.mult, op1=ALU.add)
                    v.scalar_tensor_tensor(di[:, :, d:], si[:, :, :CH - d], ar,
                                           si[:, :, d:],
                                           op0=ALU.mult, op1=ALU.add)
                    v.scalar_tensor_tensor(dr[:, :, d:], si[:, :, :CH - d], nai,
                                           dr[:, :, d:],
                                           op0=ALU.mult, op1=ALU.add)
                    v.scalar_tensor_tensor(di[:, :, d:], sr[:, :, :CH - d], ai,
                                           di[:, :, d:],
                                           op0=ALU.mult, op1=ALU.add)
                    v.tensor_copy(dr[:, :, 0:d], sr[:, :, 0:d])
                    v.tensor_copy(di[:, :, 0:d], si[:, :, 0:d])
                    cur, oth = oth, cur
                fr, fi = cur
                v.tensor_copy(e0r[:, 1:NCH + 1],
                              fr[:, :, CH - 1:CH].rearrange("p c j -> p (c j)"))
                bb.inc(v.tensor_copy(
                    e0i[:, 1:NCH + 1],
                    fi[:, :, CH - 1:CH].rearrange("p c j -> p (c j)")),
                    "sV", 1, (tag, "intra"))
            on("vector", f_intra)

            def f_escan(v):
                lcr = pcol(f"lcr_{l}_{ct}")
                lci = pcol(f"lci_{l}_{ct}")
                v.tensor_scalar_mul(e0r[:, 0:1], lcr, emask[:, 0:1])
                v.tensor_scalar_mul(e0i[:, 0:1], lci, emask[:, 0:1])
                cur = (e0r, e0i)
                oth = (e1r, e1i)
                N = NCH + 1
                for mi, m in enumerate(range(7, 12)):
                    d = 1 << mi
                    sr, si = cur
                    dr, di = oth
                    ar = pw[:, 2 * m:2 * m + 1]
                    ai = pw[:, 2 * m + 1:2 * m + 2]
                    nai = s64a[:, 9:10]
                    v.tensor_scalar_mul(nai, ai, -1.0)
                    v.scalar_tensor_tensor(dr[:, d:N], sr[:, 0:N - d], ar,
                                           sr[:, d:N], op0=ALU.mult, op1=ALU.add)
                    v.scalar_tensor_tensor(di[:, d:N], si[:, 0:N - d], ar,
                                           si[:, d:N], op0=ALU.mult, op1=ALU.add)
                    v.scalar_tensor_tensor(dr[:, d:N], si[:, 0:N - d], nai,
                                           dr[:, d:N], op0=ALU.mult, op1=ALU.add)
                    v.scalar_tensor_tensor(di[:, d:N], sr[:, 0:N - d], ai,
                                           di[:, d:N], op0=ALU.mult, op1=ALU.add)
                    v.tensor_copy(dr[:, 0:d], sr[:, 0:d])
                    v.tensor_copy(di[:, 0:d], si[:, 0:d])
                    cur, oth = oth, cur
                fr, fi = cur
                v.tensor_copy(sx[:, 2 * ct:2 * ct + 1], fr[:, NCH:NCH + 1])
                bb.inc(v.tensor_copy(sx[:, 2 * ct + 1:2 * ct + 2],
                                     fi[:, NCH:NCH + 1]),
                       "sV", 1, (tag, "escan"))
            on("vector", f_escan)

        def emit_fixup_apply(l, ct):
            tag = ("sc", l, ct)
            pw = pwt[:, ct * 32:(ct + 1) * 32]
            prg_r = prgs_r[:, ct * CH:(ct + 1) * CH]
            prg_i = prgs_i[:, ct * CH:(ct + 1) * CH]
            FINr3 = LAr.rearrange("p (c j) -> p c j", c=NCH)
            N = NCH + 1

            def f_fix(v):
                bb.wait(v, "sEx", ("exr", l, ct))
                Ar = s64a[:, 16:16 + N]
                Ai = s64b[:, 16:16 + N]
                v.tensor_copy(Ai[:, 0:1], rdat[:, 2 * ct + 1:2 * ct + 2])
                v.tensor_copy(Ar[:, 0:1], rdat[:, 2 * ct:2 * ct + 1])
                for mi in range(5):
                    w = 1 << mi
                    n = min(2 * w, N) - w
                    ar = pw[:, 2 * (7 + mi):2 * (7 + mi) + 1]
                    ai = pw[:, 2 * (7 + mi) + 1:2 * (7 + mi) + 2]
                    v.tensor_scalar_mul(s64b[:, 40:40 + n], Ai[:, 0:n], ar)
                    v.tensor_scalar_mul(s64a[:, 40:40 + n], Ai[:, 0:n], ai)
                    v.scalar_tensor_tensor(Ai[:, w:w + n], Ar[:, 0:n], ai,
                                           s64b[:, 40:40 + n],
                                           op0=ALU.mult, op1=ALU.add)
                    v.scalar_tensor_tensor(Ar[:, w:w + n], Ar[:, 0:n], ar,
                                           s64a[:, 40:40 + n],
                                           op0=ALU.mult, op1=ALU.subtract)
                v.tensor_add(e1i[:, :], e1i[:, :], Ai[:, :])
                v.tensor_add(e1r[:, :], e1r[:, :], Ar[:, :])
                v.tensor_scalar_mul(esn[:, :], e1i[:, :], -1.0)
            on("vector", f_fix)

            def f_apply(v):
                for c in range(NCH):
                    v.scalar_tensor_tensor(FINr3[:, c, :], prg_r[:, :],
                                           e1r[:, c:c + 1], FINr3[:, c, :],
                                           op0=ALU.mult, op1=ALU.add)
                for c in range(NCH):
                    v.scalar_tensor_tensor(FINr3[:, c, :], prg_i[:, :],
                                           esn[:, c:c + 1], FINr3[:, c, :],
                                           op0=ALU.mult, op1=ALU.add)
                bb.inc(v.scalar_tensor_tensor(
                    h_t[ct], FINr3.rearrange("p c j -> p (c j)"),
                    pcol(f"scl_{l}_{ct}"), h_t[ct],
                    op0=ALU.mult, op1=ALU.add),
                    "sV", 1, ("hupd", l, ct))
            on("vector", f_apply)

        def emit_attn(pre_bar, stopat=None):
            emit_ln("lnA", resid_s, pre_bar=pre_bar)

            def f_winstream(g):
                bb.bar(g, pre_bar)
                for half in range(2):
                    for k in range(DT):
                        i = half * DT + k
                        if i >= 2:
                            bb.wait(g, "sPE", ("winmm", i - 2))
                        bb.inc(g.dma_start((sil0, sil1)[i % 2],
                                           win_t[k, :, :]),
                               ("sWa", "sWb")[i % 2], 16, ("winld", i))
            on("gpsimd", f_winstream)

            bcap_win = cap_banks()

            def f_win(t):
                bb.wait(t, "sV", ("lnA", "done"))
                for half in range(2):
                    if half == 0:
                        wait_banks(t, bcap_win)
                    else:
                        bb.wait(t, "sS", ("winev", 0, 7))
                    for k in range(DT):
                        i = half * DT + k
                        bb.wait(t, ("sWa", "sWb")[i % 2], ("winld", i))
                        wt = (sil0, sil1)[i % 2]
                        for gi in range(8):
                            gidx = half * 8 + gi
                            dt4, tc = gidx // TC, gidx % TC
                            li = mm(t, psum[gi][:, :],
                                    wt[:, dt4 * 128:(dt4 + 1) * 128],
                                    ybf[:, k * T + tc * 512:
                                        k * T + (tc + 1) * 512],
                                    k == 0, k == DT - 1)
                        bb.inc(li, "sPE", 1, ("winmm", i))
                    bb.mark("sPE", ("win", half))
            on("tensor", f_win)

            def f_winev(sc):
                for half in range(2):
                    bb.wait(sc, "sPE", ("win", half))
                    for gi in range(8):
                        gidx = half * 8 + gi
                        dt4, tc = gidx // TC, gidx % TC
                        bb.inc(sc.activation(
                            h_t[dt4][:, tc * 512:(tc + 1) * 512],
                            psum[gi][:, :], AF.Identity, bias=pcol(f"bin{dt4}")),
                            "sS", 1, ("winev", half, gi))
            on("scalar", f_winev)
            set_bank_free(range(8), "sS", ("winev", 1, 7))

            def f_hrdy(v):
                bb.wait(v, "sS", ("winev", 1, 7))
                bb.inc(v.nop(), "sV", 1, ("hrdy", 0))
            on("vector", f_hrdy)

            def dump_h(gatekey):
                def f_d(g):
                    bb.wait(g, "sV", gatekey)
                    for dt in range(ADT):
                        bb.inc(g.dma_start(dbg[dt, :, :], h_t[dt]), "sDB", 16)
                    bb.mark("sDB", "fin_sO")
                    bb.wait(g, "sDB", "fin_sO")
                on("gpsimd", f_d)

            if stopat in ("y00", "in00", "tb00"):
                pass
            if stopat == "win":
                def f_dw(g):
                    bb.wait(g, "sV", ("hrdy", 0))
                    for dt in range(ADT):
                        bb.inc(g.dma_start(dbg[dt, :, :], h_t[dt]), "sDB", 16)
                    bb.mark("sDB", "fin_sO")
                    bb.wait(g, "sDB", "fin_sO")
                on("gpsimd", f_dw)
                return

            for l in range(DEPTH):
                t1 = ("aln", l, 1)
                emit_attn_ln(t1, lambda dt, l=l: ("hrdy", l))

                for ct in range(ADT):
                    def f_vn(v, l=l, ct=ct, t1=t1):
                        if ct == 0:
                            bb.wait(v, "sPE", (t1, "bcast"))
                        for tc in range(TC):
                            i = ct * TC + tc
                            ra, rb = ((rs_[0], rs_[1]) if i % 2 == 0
                                      else (rs_[2], rs_[3]))
                            if i >= 2:
                                bb.wait(v, "sS", (t1, "yout", i - 2))
                            s5 = slice(tc * 512, (tc + 1) * 512)
                            v.tensor_sub(ra[:, :], h_t[ct][:, s5],
                                         psum[4 + tc][:, :])
                            bb.inc(v.tensor_mul(rb[:, :], ra[:, :],
                                                psum[tc][:, :]),
                                   "sV", 1, (t1, "nt", i))
                    on("vector", f_vn)

                    def f_ys(sc, l=l, ct=ct, t1=t1):
                        if ct >= 1:
                            bb.wait(sc, "sV", (("sc", l, ct - 1), "intra"))
                        elif l >= 1:
                            bb.wait(sc, "sV", (("sc", l - 1, ADT - 1), "intra"))
                        for tc in range(TC):
                            i = ct * TC + tc
                            rb = rs_[1] if i % 2 == 0 else rs_[3]
                            bb.wait(sc, "sV", (t1, "nt", i))
                            bb.inc(sc.activation(
                                y_ct[:, tc * 512:(tc + 1) * 512], rb[:, :],
                                AF.Silu, scale=pcol(f"ln1w_{l}_{ct}"),
                                bias=pcol(f"ln1b_{l}_{ct}")),
                                "sS", 1, (t1, "yout", i))
                        bb.mark("sS", ("y", l, ct))
                    on("scalar", f_ys)

                    mid = None
                    if ct >= 1:
                        mid = (lambda l=l, pct=ct - 1:
                               emit_fixup_apply(l, pct))
                    emit_scan(l, ct, mid=mid)
                    if l == 0 and ct == 0 and stopat in ("y00", "in00", "tb00"):
                        key = {"y00": None, "in00": (("sc", 0, 0), "intra"),
                               "tb00": (("sc", 0, 0), "tables")}[stopat]

                        def f_md(g, key=key, stopat=stopat):
                            if stopat == "y00":
                                bb.wait(g, "sS", ("y", 0, 0))
                                bb.inc(g.dma_start(dbg[0, :, :], y_ct[:, 0:T]),
                                       "sDB", 16)
                            elif stopat == "tb00":
                                bb.wait(g, "sV", key)
                                bb.inc(g.dma_start(
                                    dbg[0, :, 0:CH],
                                    prgs_r[:, 0:CH]), "sDB", 16)
                                bb.inc(g.dma_start(
                                    dbg[0, :, CH:2 * CH],
                                    prgs_i[:, 0:CH]), "sDB", 16)
                                bb.inc(g.dma_start(
                                    dbg[0, :, 2 * CH:2 * CH + 128],
                                    pwt[:, 0:128]), "sDB", 16)
                            else:
                                bb.wait(g, "sV", key)
                                bb.inc(g.dma_start(dbg[0, :, :], LAr[:, 0:T]),
                                       "sDB", 16)
                                bb.inc(g.dma_start(dbg[1, :, :], LAi[:, 0:T]),
                                       "sDB", 16)
                                bb.inc(g.dma_start(dbg[2, :, 0:NCH + 1],
                                                   e0r[:, :]), "sDB", 16)
                                bb.inc(g.dma_start(dbg[2, :, 32:32 + NCH + 1],
                                                   e0i[:, :]), "sDB", 16)
                            bb.mark("sDB", "fin_sO")
                            bb.wait(g, "sDB", "fin_sO")
                        on("gpsimd", f_md)
                        return
                emit_fixup_apply(l, ADT - 1)
                set_bank_free(range(8), "sV", (t1, "nt", ADT * TC - 1))

                sp = f"sP{l}"

                def f_ex(g, l=l, sp=sp):
                    if bb.mode == "plan":
                        for ct in range(ADT):
                            bb.inc(g.nop(), "sEx", 1, ("exr", l, ct))
                        return
                    with (g.register(f"fev{l}") as fev,
                          g.register(f"cnt{l}") as cnt,
                          g.register(f"fl{l}") as fl,
                          g.register(f"nd{l}") as nd):
                        g.reg_load(fev, rflag[0:1, 0:1])
                        g.reg_mov(cnt, 0)
                        for ct in range(ADT):
                            bb.wait(g, "sV", (("sc", l, ct), "escan"))
                            with g.If_ne(fev, 0):
                                g.dma_start(
                                    xch[l, :, 2 * ct:2 * ct + 2],
                                    sx[:, 2 * ct:2 * ct + 2]).then_inc(
                                    bb.h[sp], 16)
                                g.reg_add(cnt, cnt, 16)
                                g.wait_ge(bb.h[sp], cnt)
                                g.dma_start(xfl[l:l + 1, ct:ct + 1],
                                            magict[:, :]).then_inc(bb.h[sp], 16)
                                g.reg_add(cnt, cnt, 16)
                                g.wait_ge(bb.h[sp], cnt)
                            with g.Else():
                                g.reg_mov(nd, 1)
                                with g.While(nd):
                                    g.dma_start(
                                        pollt[:, :],
                                        xfl[l:l + 1, ct:ct + 1]).then_inc(
                                        bb.h[sp], 16)
                                    g.reg_add(cnt, cnt, 16)
                                    g.wait_ge(bb.h[sp], cnt)
                                    g.reg_load(fl, pollt[0:1, 0:1])
                                    g.reg_mov(nd, MAGIC)
                                    g.reg_sub(nd, nd, fl)
                                g.dma_start(
                                    rdat[:, 2 * ct:2 * ct + 2],
                                    xch[l, :, 2 * ct:2 * ct + 2]).then_inc(
                                    bb.h[sp], 16)
                                g.reg_add(cnt, cnt, 16)
                                g.wait_ge(bb.h[sp], cnt)
                                g.dma_start(xfl[l:l + 1, ct:ct + 1],
                                            zerot[:, :]).then_inc(bb.h[sp], 16)
                                g.reg_add(cnt, cnt, 16)
                                g.wait_ge(bb.h[sp], cnt)
                            g.nop().then_inc(bb.h["sEx"], 1)
                on("gpsimd", f_ex)

                t2 = ("aln", l, 2)

                def f_h2rdy(v, l=l):
                    bb.wait(v, "sV", ("hupd", l, ADT - 1))
                    bb.inc(v.nop(), "sV", 1, ("h2rdy", l))
                on("vector", f_h2rdy)
                emit_attn_ln(t2, lambda dt, l=l: ("h2rdy", l))
                emit_attn_norm_out(t2, f"ln2w_{l}_", f"ln2b_{l}_", AF.Identity,
                                   lambda dt: y2bf[dt])

                def f_w12(g, l=l):
                    if l == 0:
                        bb.wait(g, "sPE", ("win", 1))
                    else:
                        bb.wait(g, "sPE", (("ffnin", l - 1), "f2done"))
                    for k in range(4):
                        bb.inc(g.dma_start(w1bf[:, k * 2048:(k + 1) * 2048],
                                           w1t[l, k, :, :]), "sW12", 16)
                    for k in range(16):
                        bb.inc(g.dma_start(w2bf[:, k * 512:(k + 1) * 512],
                                           w2t[l, k, :, :]), "sW12", 16,
                               ("w12", l) if k == 15 else None)
                on("gpsimd", f_w12)

                tf = ("ffnin", l)
                bcap_f = cap_banks()

                def f_ffn_pe(t, l=l, tf=tf, t2=t2, bcap_f=bcap_f):
                    bb.wait(t, "sW12", ("w12", l))
                    bb.wait(t, "sS", (t2, "outdone"))
                    wait_banks(t, bcap_f)
                    for tc in range(TC):
                        if tc >= 1:
                            bb.wait(t, "sV", (tf, "f2ev", tc - 1))
                        for ft in range(FT):
                            fb = psum[4 + ft % 2]
                            if ft >= 2:
                                bb.wait(t, "sS", (tf, "sil", tc, ft - 2))
                            for k in range(4):
                                li = mm(t, fb[:, :],
                                        w1bf[:, k * 2048 + ft * 128:
                                             k * 2048 + (ft + 1) * 128],
                                        y2bf[k][:, tc * 512:(tc + 1) * 512],
                                        k == 0, k == 3)
                            bb.inc(li, "sPE", 1, (tf, "f1", tc, ft))
                            if ft >= 1:
                                fp = ft - 1
                                silb = (sil0, sil1)[fp % 2]
                                bb.wait(t, "sS", (tf, "sil", tc, fp))
                                for dt in range(ADT):
                                    mm(t, psum[dt][:, :],
                                       w2bf[:, fp * 512 + dt * 128:
                                            fp * 512 + (dt + 1) * 128],
                                       silb[:, :], fp == 0, False)
                        fp = FT - 1
                        silb = (sil0, sil1)[fp % 2]
                        bb.wait(t, "sS", (tf, "sil", tc, fp))
                        for dt in range(ADT):
                            li = mm(t, psum[dt][:, :],
                                    w2bf[:, fp * 512 + dt * 128:
                                         fp * 512 + (dt + 1) * 128],
                                    silb[:, :], False, True)
                        bb.inc(li, "sPE", 1, (tf, "f2", tc))
                    bb.mark("sPE", (tf, "f2done"))
                on("tensor", f_ffn_pe)

                def f_ffn_sil(sc, l=l, tf=tf):
                    for tc in range(TC):
                        for ft in range(FT):
                            silb = (sil0, sil1)[ft % 2]
                            bb.wait(sc, "sPE", (tf, "f1", tc, ft))
                            bb.inc(sc.activation(silb[:, :],
                                                 psum[4 + ft % 2][:, :], AF.Silu,
                                                 bias=pcol(f"b1_{l}_{ft}")),
                                   "sS", 1, (tf, "sil", tc, ft))
                on("scalar", f_ffn_sil)

                def f_ffn_ev(v, l=l, tf=tf):
                    for tc in range(TC):
                        bb.wait(v, "sPE", (tf, "f2", tc))
                        for dt in range(ADT):
                            bb.inc(v.scalar_tensor_tensor(
                                h_t[dt][:, tc * 512:(tc + 1) * 512],
                                psum[dt][:, :], pcol(f"b2_{l}_{dt}"),
                                h_t[dt][:, tc * 512:(tc + 1) * 512],
                                op0=ALU.add, op1=ALU.add),
                                "sV", 1,
                                (tf, "f2ev", tc) if dt == ADT - 1 else None)
                on("vector", f_ffn_ev)
                set_bank_free(range(8), "sV", (tf, "f2ev", TC - 1))

                def f_h3rdy(v, l=l):
                    bb.wait(v, "sV", (("ffnin", l), "f2ev", TC - 1))
                    bb.inc(v.nop(), "sV", 1, ("hrdy", l + 1))
                on("vector", f_h3rdy)

                if stopat == f"l{l}":
                    dump_h(("hrdy", l + 1))
                    return
                if stopat == f"s{l}":
                    dump_h(("hupd", l, ADT - 1))
                    return

            tl = ("aln", "lnl", 0)
            emit_attn_ln(tl, lambda dt: ("hrdy", DEPTH))
            emit_attn_norm_out(tl, "lnlw", "lnlb", AF.Identity, lambda dt: y2bf[dt])

            def f_woutld(g):
                bb.wait(g, "sPE", ("win", 1))
                for k in range(4):
                    bb.inc(g.dma_start(woutbf[:, k * 4096:(k + 1) * 4096],
                                       wout_t[k, :, :]), "sW12", 16,
                           ("woutld",) if k == 3 else None)
            on("gpsimd", f_woutld)

            def f_wout_sync(s):
                prevuse = {}
                for vt in range(DT):
                    for tc in range(TC):
                        m4 = (vt * TC + tc) % 4
                        pk = prevuse.get(m4)
                        if pk is not None:
                            bb.wait(s, "sV", pk[0])
                            bb.wait(s, f"sST{m4}", pk[1])
                        else:
                            bb.wait(s, "sS", (tl, "outdone"))
                        bb.inc(s.dma_start(
                            rs_[m4][:, :],
                            resid_s[vt, :, tc * 512:(tc + 1) * 512]),
                            f"sR{m4}", 16, ("wo_rld", vt, tc))
                        prevuse[m4] = (("wo_ev", vt, tc), ("wo_st", vt, tc))
            on("sync", f_wout_sync)

            bcap_wo = cap_banks()

            def f_wout_pe(t, bcap_wo=bcap_wo):
                bb.wait(t, "sW12", ("woutld",))
                bb.wait(t, "sS", (tl, "outdone"))
                gi = 0
                for vt in range(DT):
                    for tc in range(TC):
                        b_ = gi % 8
                        if gi >= 8:
                            bb.wait(t, "sV", ("wo_ev_g", gi - 8))
                        elif gi == 0:
                            wait_banks(t, bcap_wo)
                        for k in range(4):
                            li = mm(t, psum[b_][:, :],
                                    woutbf[:, k * 4096 + vt * 128:
                                           k * 4096 + (vt + 1) * 128],
                                    y2bf[k][:, tc * 512:(tc + 1) * 512],
                                    k == 0, k == 3)
                        bb.inc(li, "sPE", 1, ("wo_pe", gi))
                        gi += 1
            on("tensor", f_wout_pe)

            def f_wout_ev(v):
                gi = 0
                for vt in range(DT):
                    for tc in range(TC):
                        b_ = gi % 8
                        m4 = gi % 4
                        bb.wait(v, "sPE", ("wo_pe", gi))
                        bb.wait(v, f"sR{gi % 4}", ("wo_rld", vt, tc))
                        bb.inc(v.scalar_tensor_tensor(
                            rs_[m4][:, :], psum[b_][:, :], pcol(f"bout{vt}"),
                            rs_[m4][:, :], op0=ALU.add, op1=ALU.add),
                            "sV", 1, ("wo_ev", vt, tc))
                        bb.mark("sV", ("wo_ev_g", gi))
                        gi += 1
            on("vector", f_wout_ev)

            def f_wout_st(sc):
                for vt in range(DT):
                    for tc in range(TC):
                        bb.wait(sc, "sV", ("wo_ev", vt, tc))
                        bb.inc(sc.dma_start(
                            resid_s[vt, :, tc * 512:(tc + 1) * 512],
                            rs_[(vt * TC + tc) % 4][:, :]),
                            f"sST{(vt * TC + tc) % 4}", 16, ("wo_st", vt, tc))
            on("scalar", f_wout_st)
            set_bank_free(range(8), "sV", ("wo_ev", DT - 1, TC - 1))

        emit_ln("ln1", resid_in)
        if stop_after == "ln":
            def f_snap(e):
                bb.snap(("bar", "x"))
            on("sync", f_snap)

            def f_dumpln(g):
                bb.wait(g, "sV", ("ln1", "done"))
                for dt in range(DT):
                    bb.inc(g.dma_start(dbg[dt, :, :], ybf[:, dt * T:(dt + 1) * T]),
                           "sDB", 16)
                bb.mark("sDB", "fin_sO")
                bb.wait(g, "sDB", "fin_sO")
            on("gpsimd", f_dumpln)
        else:
            emit_gateup("gu1", 1, "ln1")

            def f_snapgu1(e):
                bb.snap(("bar", "gu1"))
            on("sync", f_snapgu1)
            if stop_after == "gu":
                def f_dumpgu(g):
                    bb.bar(g, ("bar", "gu1"))
                    for tc in range(TC):
                        bb.inc(g.dma_start(
                            dbg[tc, :, :, :].rearrange("p k j -> p (k j)"),
                            at_s[tc, :, :, :].rearrange("p k j -> p (k j)")),
                            "sDB", 16)
                    bb.mark("sDB", "fin_sO")
                    bb.wait(g, "sDB", "fin_sO")
                on("gpsimd", f_dumpgu)
            else:
                emit_down("dn1", 1, resid_in, resid_s, ("bar", "gu1"), "sY")

                def f_snapffn1(e):
                    bb.snap(("bar", "ffn1"))
                on("sync", f_snapffn1)
                if stop_after == "ffn1":
                    def f_dumpf1(g):
                        bb.bar(g, ("bar", "ffn1"))
                        for dt in range(DT):
                            bb.inc(g.dma_start(dbg[dt, :, :], resid_s[dt, :, :]),
                                   "sDB", 16)
                        bb.mark("sDB", "fin_sO")
                        bb.wait(g, "sDB", "fin_sO")
                    on("gpsimd", f_dumpf1)
                else:
                    if stop_after in ("win", "l0", "l1", "l2", "l3",
                                      "s0", "s1", "s2", "s3",
                                      "y00", "in00", "tb00"):
                        emit_attn(("bar", "ffn1"), stopat=stop_after)
                        bb.run_plan()
                        for nm in BB.SEMS:
                            bb.h[nm] = es.enter_context(nc.semaphore(nm))

                        @blk.sync
                        def _(s):
                            bb.run_emit("sync", s)

                        @blk.scalar
                        def _(sc):
                            bb.run_emit("scalar", sc)

                        @blk.vector
                        def _(v):
                            bb.run_emit("vector", v)

                        @blk.tensor
                        def _(t):
                            bb.run_emit("tensor", t)

                        @blk.gpsimd
                        def _(g):
                            bb.run_emit("gpsimd", g)
                        return nc
                    emit_attn(("bar", "ffn1"))

                    def f_snapattn(e):
                        bb.snap(("bar", "attn"))
                    on("sync", f_snapattn)
                    if stop_after == "attn":
                        def f_dumpat(g):
                            bb.bar(g, ("bar", "attn"))
                            for dt in range(DT):
                                bb.inc(g.dma_start(dbg[dt, :, :],
                                                   resid_s[dt, :, :]), "sDB", 16)
                            bb.mark("sDB", "fin_sO")
                            bb.wait(g, "sDB", "fin_sO")
                        on("gpsimd", f_dumpat)
                    else:
                        emit_ln("lnF", resid_s, pre_bar=("bar", "attn"))
                        emit_gateup("gu2", 2, "lnF", pre_bar=("bar", "attn"))

                        def f_snapgu2(e):
                            bb.snap(("bar", "gu2"))
                        on("sync", f_snapgu2)
                        emit_down("dn2", 2, resid_s, resid_out,
                                  ("bar", "gu2"), "sO")

                        def f_fin(g):
                            bb.snap(("bar", "fin"))
                            bb.bar(g, ("bar", "fin"))
                        on("gpsimd", f_fin)

        bb.run_plan()
        for nm in BB.SEMS:
            bb.h[nm] = es.enter_context(nc.semaphore(nm))

        @blk.sync
        def _(s):
            bb.run_emit("sync", s)

        @blk.scalar
        def _(sc):
            bb.run_emit("scalar", sc)

        @blk.vector
        def _(v):
            bb.run_emit("vector", v)

        @blk.tensor
        def _(t):
            bb.run_emit("tensor", t)

        @blk.gpsimd
        def _(g):
            bb.run_emit("gpsimd", g)

    return nc


def _in_maps(staged):
    maps = []
    for c in range(8):
        m = {k: staged[k] for k in
             ("wg8", "wu8", "wd8", "gsr", "usr", "dsr", "win_t", "w1t", "w2t",
              "wout_t", "params")}
        m["resid_in"] = staged["_resid_shards"][c]
        ev = 1 - (c % 2)
        m["evenflag"] = np.array([[ev]], np.int32)
        em = np.zeros((128, 2), np.float32)
        em[:, 0] = ev
        em[:, 1] = 1 - ev
        m["emaskin"] = em
        maps.append(m)
    return maps


def kernel(**inputs):
    staged = _stage_inputs(inputs)
    nc = build_program()
    res = run_bass_kernel_spmd(nc, _in_maps(staged), core_ids=list(range(8)))
    out = np.empty((4, 4096, 4096), np.float32)
    for c in range(8):
        b, h = c // 2, c % 2
        sh = np.asarray(res.results[c]["resid_out"]).reshape(4096, T)
        out[b, h * T:(h + 1) * T, :] = sh.T
    return out
